# version 01
# baseline (speedup 1.0000x reference)
import numpy as np
from contextlib import ExitStack

import concourse.bass as bass
import concourse.mybir as mybir
from concourse.bass_utils import run_bass_kernel_spmd

F32 = mybir.dt.float32
F32R = mybir.dt.float32r
BF16 = mybir.dt.bfloat16
I32 = mybir.dt.int32
AF = mybir.ActivationFunctionType
ALU = mybir.AluOpType
AX = mybir.AxisListType.X

NCORES = 8
SEQ = 8192
T = 1024
TH = T + 3
D = 2048
DI = 4096
NH = 64
HP = 64
INTER = 5632
CH = 8
KD = 16
KDI = 32
KI = 44
KI_PAD = 48
EPS_LN = 1e-5
EPS_RMS = 1e-6
EPS_FFN = 1e-5
NCHX = 35
SH12 = 6 * KD * 128 * 128
SH3 = 2 * KI * 128 * 128

_cache = {}


def _prep_inputs(inp):
    import ml_dtypes
    x = np.asarray(inp["L0_attn_hidden_states"]).reshape(SEQ, D).astype(np.float32, copy=False)
    Win = np.asarray(inp["L0_attn_in_proj_w"], dtype=np.float32)
    Wout = np.asarray(inp["L0_attn_out_proj_w"], dtype=np.float32)
    Wg = np.asarray(inp["L0_attn_gate_w"], dtype=np.float32)
    Wu = np.asarray(inp["L0_attn_up_w"], dtype=np.float32)
    Wd = np.asarray(inp["L0_attn_down_w"], dtype=np.float32)
    w1 = np.asarray(inp["L0_ffn_w1"], dtype=np.float32)
    w2 = np.asarray(inp["L0_ffn_w2"], dtype=np.float32)
    w3 = np.asarray(inp["L0_ffn_w3"], dtype=np.float32)

    def btile(w, K, N):
        return np.ascontiguousarray(w.reshape(K // 128, 128, N // 128, 128).transpose(2, 0, 1, 3))

    w_xbcdt_cols = np.concatenate([
        Win[:, 8448:8512], np.zeros((D, 64), np.float32),
        Win[:, 8192:8320], Win[:, 8320:8448], Win[:, 4096:8192]], axis=1)
    w_xbcdt = btile(w_xbcdt_cols, D, NCHX * 128)
    w_z = btile(Win[:, 0:4096], D, DI)
    w_out = btile(Wout, DI, D)
    w_gate = btile(Wg, D, INTER)
    w_up = btile(Wu, D, INTER)
    w_down = btile(Wd, INTER, D)
    w1p = np.concatenate([np.ascontiguousarray(w1.T),
                          np.zeros((D, KI_PAD * 128 - INTER), np.float32)], 1)
    w2p = np.concatenate([np.ascontiguousarray(w2.T),
                          np.zeros((D, KI_PAD * 128 - INTER), np.float32)], 1)
    w1t = btile(w1p, D, KI_PAD * 128)
    w2t = btile(w2p, D, KI_PAD * 128)
    w3t = btile(np.ascontiguousarray(w3.T), INTER, D)

    conv_w = np.asarray(inp["L0_attn_conv_w"], dtype=np.float32)
    conv_b = np.asarray(inp["L0_attn_conv_b"], dtype=np.float32)
    cw = np.ascontiguousarray(conv_w.reshape(34, 128, 4).transpose(1, 0, 2).reshape(128, 136))
    cb = np.ascontiguousarray(conv_b.reshape(34, 128).T)

    consts = dict(
        ident_f=np.eye(128, dtype=np.float32),
        ident_b=np.eye(128, dtype=np.float32).astype(ml_dtypes.bfloat16),
        tri01=np.tril(np.ones((128, 128), np.float32)).T.astype(ml_dtypes.bfloat16),
        cw=cw, cb=cb,
        dtb=np.asarray(inp["L0_attn_dt_bias"], np.float32).reshape(64, 1),
        alog=np.asarray(inp["L0_attn_A_log"], np.float32).reshape(64, 1),
        drow=np.asarray(inp["L0_attn_D_param"], np.float32).reshape(1, 64),
        norm1=np.asarray(inp["L0_attn_norm1_w"], np.float32).reshape(1, D),
        norm2=np.asarray(inp["L0_attn_norm2_w"], np.float32).reshape(1, D),
        mixw=np.asarray(inp["L0_attn_mixer_norm_w"], np.float32).reshape(1, DI),
    )

    in_maps = []
    for k in range(NCORES):
        lo = k * T
        xs_sl = np.zeros((TH, D), np.float32)
        if k == 0:
            xs_sl[3:] = x[0:T]
        else:
            xs_sl[:] = x[lo - 3:lo + T]
        sel = np.zeros((8, 8), np.float32)
        low = np.zeros((8, 1), np.float32)
        for j in range(8):
            if j < k:
                low[j, 0] = 1.0
            for i in range(8):
                if j < i < k:
                    sel[i, j] = 1.0
        in_maps.append(dict(
            x_slice=np.ascontiguousarray(xs_sl),
            w_xbcdt=w_xbcdt, w_z=w_z, w_out=w_out,
            w_gate=w_gate, w_up=w_up, w_down=w_down,
            w1s=np.ascontiguousarray(w1t[6 * k:6 * k + 6]),
            w2s=np.ascontiguousarray(w2t[6 * k:6 * k + 6]),
            w3s=np.ascontiguousarray(w3t[2 * k:2 * k + 2]),
            sel=sel, low=low, **consts))
    return in_maps


class Arena:
    def __init__(self, t, elems, name):
        self.t, self.elems, self.cur, self.name = t, elems, 0, name

    def alloc(self, n_elems, dt):
        nb = n_elems * (2 if dt == BF16 else 4) // 2
        nb = (nb + 15) // 16 * 16
        off = self.cur
        assert off + nb <= self.elems, f"{self.name} overflow {off}+{nb}>{self.elems}"
        self.cur = off + nb
        sl = self.t[:, off:off + nb]
        return sl if dt == BF16 else sl.bitcast(dt)

    def reset(self, to=0):
        self.cur = to


def build_program(dbg=(), upto=None):
    key = (tuple(sorted(dbg)), upto)
    if key in _cache:
        return _cache[key]
    nc = bass.Bass("TRN2", target_bir_lowering=False)
    stack = ExitStack()
    sem, cnt = {}, {}
    for n in ["dW", "dG", "T", "V", "S", "G", "CC"]:
        sem[n] = stack.enter_context(nc.semaphore(n))
        cnt[n] = 0
    dbgset = set(dbg)

    def inc(ins, s, n=1):
        cnt[s] += n
        ins.then_inc(sem[s], n)
        return cnt[s]

    def vf(v, ins):
        val = inc(ins, "V")
        v.wait_ge(sem["V"], val)

    def sf(s, ins):
        val = inc(ins, "S")
        s.wait_ge(sem["S"], val)

    dram = {}
    for name, shape, dt in [
            ("x_slice", (TH, D), F32), ("w_xbcdt", (NCHX, KD, 128, 128), F32),
            ("w_z", (KDI, KD, 128, 128), F32), ("w_out", (KD, KDI, 128, 128), F32),
            ("w_gate", (KI, KD, 128, 128), F32), ("w_up", (KI, KD, 128, 128), F32),
            ("w_down", (KD, KI, 128, 128), F32),
            ("w1s", (6, KD, 128, 128), F32), ("w2s", (6, KD, 128, 128), F32),
            ("w3s", (2, KI, 128, 128), F32),
            ("sel", (8, 8), F32), ("low", (8, 1), F32),
            ("ident_f", (128, 128), F32), ("ident_b", (128, 128), BF16),
            ("tri01", (128, 128), BF16), ("cw", (128, 136), F32), ("cb", (128, 34), F32),
            ("dtb", (64, 1), F32), ("alog", (64, 1), F32), ("drow", (1, 64), F32),
            ("norm1", (1, D), F32), ("norm2", (1, D), F32), ("mixw", (1, DI), F32)]:
        dram[name] = nc.dram_tensor(name, list(shape), dt, kind="ExternalInput").ap()
    y_out = nc.dram_tensor("y_out", [T, D], F32, kind="ExternalOutput").ap()

    scr = {}
    for name, shape, dt in [
            ("xn_scr", (KD, 128, T), BF16), ("xq_scr", (KD, 128, T), BF16),
            ("hin_scr", (KD, 128, TH), BF16), ("h1_scr", (KD, 128, T), BF16),
            ("s_scr", (KD, 128, T), BF16), ("acum_scr", (64, T), F32),
            ("eal_scr", (8, 512), F32), ("row_scr", (8, 1024), F32),
            ("cvals_scr", (8, 64), F32),
            ("cc2_in", (1, 8), F32), ("cc2_out", (8, 8), F32),
            ("cc3_in", (1, 2 * SH12 + SH3), BF16), ("cc3_out", (8, 2 * SH12 + SH3), BF16),
            ("cc4_in", (1, 128 * 4096 + 64), F32), ("cc4_out", (8, 128 * 4096 + 64), F32)]:
        scr[name] = nc.dram_tensor(name, list(shape), dt).ap()
    cc3o = scr["cc3_out"]

    def wq_group(which, ncu, kg, KG):
        if which in (1, 2):
            core, loc = divmod(ncu, 6)
            off = (0 if which == 1 else SH12) + (loc * KD + kg * KG) * 16384
        else:
            core, loc = divmod(ncu, 2)
            off = 2 * SH12 + (loc * KI + kg * KG) * 16384
        n_el = KG * 16384
        return cc3o[core, off:off + n_el].rearrange("(k p f) -> p k f", p=128, k=KG)

    sb = {}
    def sbuf(name, shape, dt):
        sb[name] = stack.enter_context(nc.sbuf_tensor("sb_" + name, list(shape), dt))
        return sb[name]

    for name, shape, dt in [
            ("ident_f", (128, 128), F32), ("ident_b", (128, 128), BF16),
            ("tri01", (128, 128), BF16), ("cw", (128, 136), F32), ("cb", (128, 34), F32),
            ("dtb", (64, 1), F32), ("alog", (64, 1), F32), ("negA", (64, 1), F32),
            ("Dbc", (128, 64), F32), ("cols", (128, 16), F32),
            ("iqs_cols", (128, 8), F32),
            ("wsc", (4, 8), F32), ("wsc_col", (128, 4), F32), ("wsci_col", (128, 4), F32),
            ("lndt_tm", (128, CH * 64), F32), ("acum_tm", (128, CH * 64), F32),
            ("ea_tm", (128, CH * 64), F32), ("ea2_tm", (128, CH * 64), F32),
            ("sd_tm", (128, CH * 64), F32), ("ealrep", (128, CH * 64), F32),
            ("prefx", (64, 24), F32), ("ealst", (64, 8), F32),
            ("B_fm", (128, T), BF16), ("C_fm", (128, T), BF16), ("B_tm", (128, T), BF16),
            ("Gt", (128, CH * 128), BF16),
            ("Pbf", (128, 256), BF16), ("xdtd", (128, 2 * 512), BF16),
            ("ptmp", (128, 64), F32),
            ("selw", (8, 8), F32), ("loww", (8, 1), F32),
            ("svals", (8, 64), F32), ("cvals", (8, 64), F32),
            ("onesc", (128, 8), F32),
            ("rowbuf", (1, 1024), F32),
            ("kc", (128, 8), F32),
            ("dump", (128, 512), F32)]:
        sbuf(name, shape, dt)

    ARA = Arena(sbuf("ARA", (128, 32912), BF16), 32912, "A")
    ARB = Arena(sbuf("ARB", (128, 32800), BF16), 32800, "B")
    ARC = Arena(sbuf("ARC", (128, 11400), BF16), 11400, "C")
    ARD = Arena(sbuf("ARD", (128, 8200), BF16), 8200, "D")

    psA = stack.enter_context(nc.psum_tensor("psA", [128, 2048], F32))
    psB = stack.enter_context(nc.psum_tensor("psB", [128, 2048], F32))
    psAb = psA.bitcast(BF16)
    psBb = psB.bitcast(BF16)

    block = stack.enter_context(nc.Block())

    def barrier():
        vals = dict(cnt)

        def w(eng):
            for s in ["T", "V", "S", "G", "CC", "dW", "dG"]:
                if vals[s]:
                    eng.wait_ge(sem[s], vals[s])

        @block.tensor
        def _(te):
            w(te)

        @block.vector
        def _(v):
            w(v)

        @block.scalar
        def _(s):
            w(s)

        @block.gpsimd
        def _(g):
            w(g)

        @block.sync
        def _(sy):
            w(sy)

    def dbg_dump(tag, srcs, shape, dt=F32):
        if tag not in dbgset:
            return
        o = nc.dram_tensor("dbg_" + tag, list(shape), dt, kind="ExternalOutput").ap()

        @block.sync
        def _(sy):
            for s in ["T", "V", "S", "G", "CC"]:
                if cnt[s]:
                    sy.wait_ge(sem[s], cnt[s])
            for osl, src in srcs:
                inc(sy.dma_start(out=o[osl] if osl is not None else o, in_=src), "dW", 16)

    def finish():
        @block.sync
        def _(sy):
            for s in ["T", "V", "S", "G", "CC", "dG", "dW"]:
                if cnt[s]:
                    sy.wait_ge(sem[s], cnt[s])
        stack.close()
        _cache[key] = nc
        return nc

    def bgemm(NC, KTOT, KG, wsrcs, wdt, cast, rhs_fn, ncols, regions, evac_fn,
              bufs, pre_wait_T=None, skip_n=None):
        NGR = KTOT // KG
        npair = len(wsrcs)
        t_grp = {}
        v_ncu = {}
        width = ncols * 512 * npair
        gidx = 0
        for ncu in range(NC):
            if skip_n and ncu in skip_n:
                continue
            reg = regions[ncu % len(regions)]
            ps, off = reg
            for kg in range(NGR):
                par = gidx % 2
                if cast:
                    for pi in range(npair):
                        @block.gpsimd
                        def _(g, ncu=ncu, kg=kg, par=par, pi=pi, gidx=gidx):
                            if gidx >= 2:
                                g.wait_ge(sem["T"], t_grp[gidx - 2])
                            ins = g.dma_start(out=bufs[pi][2 + par][:], in_=wsrcs[pi](ncu, kg))
                            inc(ins, "dG", 16)
                            g.wait_ge(sem["dG"], cnt["dG"])
                else:
                    for pi in range(npair):
                        @block.sync
                        def _(sy, ncu=ncu, kg=kg, par=par, pi=pi, gidx=gidx):
                            if gidx >= 2:
                                sy.wait_ge(sem["T"], t_grp[gidx - 2])
                            if gidx == 0 and pre_wait_T is not None:
                                sy.wait_ge(sem["CC"], cnt["CC"])
                            ins = sy.dma_start(out=bufs[pi][par][:], in_=wsrcs[pi](ncu, kg))
                            inc(ins, "dW", 16)
                            sy.wait_ge(sem["dW"], cnt["dW"])
                ld = cnt["dW"]
                gld = cnt["dG"]

                @block.tensor
                def _(te, ncu=ncu, kg=kg, par=par, ps=ps, off=off, ld=ld, gld=gld,
                      gidx=gidx):
                    if cast:
                        te.wait_ge(sem["dG"], gld)
                    else:
                        te.wait_ge(sem["dW"], ld)
                    if pre_wait_T is not None and ncu == 0 and kg == 0:
                        pre_wait_T(te)
                    if kg == 0 and len(v_ncu) >= len(regions):
                        prev = ncu - len(regions)
                        while prev not in v_ncu and prev >= 0:
                            prev -= 1
                        if prev in v_ncu:
                            te.wait_ge(sem["V"], v_ncu[prev])
                    for ko in range(KG):
                        k = kg * KG + ko
                        st, sp = (k == 0), (k == KTOT - 1)
                        last = (ko == KG - 1)
                        for pi in range(npair):
                            if cast:
                                lt = bufs[pi][2 + par][:, ko * 128:ko * 128 + 128]
                            elif wdt == F32R:
                                lt = bufs[pi][par].bitcast(F32R)[:, ko * 128:ko * 128 + 128]
                            else:
                                lt = bufs[pi][par][:, ko * 128:ko * 128 + 128]
                            for cb in range(ncols):
                                o2 = off + (pi * ncols + cb) * 512
                                ins = te.matmul(ps[:, o2:o2 + 512], lt, rhs_fn(pi, k, cb),
                                                start=st, stop=sp)
                                if sp or last:
                                    inc(ins, "T")
                    t_grp[gidx] = cnt["T"]
                gidx += 1
            v_ncu[ncu] = evac_fn(ncu, reg)

    @block.sync
    def _(sy):
        for n in ["ident_f", "ident_b", "tri01", "cw", "cb", "dtb", "alog"]:
            inc(sy.dma_start(out=sb[n][:], in_=dram[n]), "dW", 16)
        inc(sy.dma_start(out=sb["selw"][:], in_=dram["sel"]), "dW", 16)
        inc(sy.dma_start(out=sb["loww"][:], in_=dram["low"]), "dW", 16)
        inc(sy.dma_start(out=sb["Dbc"][:], in_=dram["drow"].broadcast_to([128, 64])), "dW", 16)
        sy.wait_ge(sem["dW"], cnt["dW"])
    ph0 = cnt["dW"]

    norm1bc = ARC.alloc(D, F32)
    qt_in = ARC.alloc(2 * 512, F32)
    qt_bf = ARC.alloc(2 * 512, BF16)
    acc_abs = ARC.alloc(8, F32)

    @block.sync
    def _(sy):
        inc(sy.dma_start(out=norm1bc[:], in_=dram["norm1"].broadcast_to([128, D])), "dW", 16)
        sy.wait_ge(sem["dW"], cnt["dW"])

    @block.vector
    def _(v):
        v.wait_ge(sem["dW"], ph0)
        v.memset(sb["cols"][:], 0.0)
        v.memset(acc_abs[:, 0:8], 0.0)
        v.memset(sb["onesc"][:], 1.0)
        KC = sb["kc"]
        v.memset(KC[:, 0:1], EPS_LN)
        v.memset(KC[:, 1:2], EPS_RMS)
        v.memset(KC[:, 2:3], 1.0)
        v.memset(KC[:, 3:4], -1.0)
        v.memset(KC[:, 4:5], 1.0 / DI)
        v.memset(KC[:, 5:6], 1.0 / D)
        v.memset(KC[:, 6:7], 1.0 / INTER)
        inc(v.memset(sb["Pbf"][:], 0.0), "V")

    @block.scalar
    def _(s):
        s.wait_ge(sem["dW"], ph0)
        inc(s.activation(out=sb["dump"][0:64, 0:1], in_=sb["alog"][:], func=AF.Exp), "S")

    @block.vector
    def _(v):
        v.wait_ge(sem["S"], cnt["S"])
        inc(v.tensor_scalar_mul(out=sb["negA"][:], in0=sb["dump"][0:64, 0:1],
                                scalar1=-1.0), "V")

    shard_list = [("w1s", 24, 0), ("w2s", 24, 1), ("w3s", 22, 2)]
    vmi = []
    for name, ntile, ti in shard_list:
        for i in range(ntile):
            gi = len(vmi)

            @block.sync
            def _(sy, name=name, i=i, gi=gi):
                if gi >= 2:
                    sy.wait_ge(sem["V"], vmi[gi - 2])
                ins = sy.dma_start(out=qt_in[:, (gi % 2) * 512:(gi % 2) * 512 + 512],
                                   in_=dram[name].rearrange("a b p f -> p (a b) f")
                                   [:, i * 4:i * 4 + 4, :])
                inc(ins, "dW", 16)
                sy.wait_ge(sem["dW"], cnt["dW"])

            @block.vector
            def _(v, gi=gi, ti=ti):
                v.wait_ge(sem["dW"], cnt["dW"])
                sl = qt_in[:, (gi % 2) * 512:(gi % 2) * 512 + 512]
                vf(v, v.reduce_sum(out=sb["dump"][:, 0:1], in_=sl, axis=AX,
                                   apply_absolute_value=True))
                ins = v.tensor_add(out=acc_abs[:, ti:ti + 1], in0=sb["dump"][:, 0:1],
                                   in1=acc_abs[:, ti:ti + 1])
                vmi.append(inc(ins, "V"))

    @block.tensor
    def _(te):
        te.wait_ge(sem["V"], cnt["V"])
        inc(te.matmul(psA[0:4, 0:1], acc_abs[:, 0:4], sb["onesc"][:, 0:1],
                      start=True, stop=True), "T")

    @block.vector
    def _(v):
        v.wait_ge(sem["T"], cnt["T"])
        inc(v.tensor_copy(out=sb["wsc"][0:4, 0:1], in_=psA[0:4, 0:1]), "V")

    @block.gpsimd
    def _(g):
        g.wait_ge(sem["V"], cnt["V"])
        inc(g.dma_start(out=scr["cc2_in"][:, 0:4], in_=sb["wsc"][0:4, 0:1]), "dG", 16)
        g.wait_ge(sem["dG"], cnt["dG"])
        inc(g.collective_compute("AllGather", ALU.bypass,
                                 replica_groups=[list(range(NCORES))],
                                 ins=[scr["cc2_in"]], outs=[scr["cc2_out"]]), "CC")
        g.wait_ge(sem["CC"], cnt["CC"])
        inc(g.dma_start(out=sb["wsc"][0:4, 0:8],
                        in_=scr["cc2_out"].rearrange("a b -> b a")[0:4, :],
                        allow_slow_non_contiguous=True), "dG", 16)

    @block.vector
    def _(v):
        v.wait_ge(sem["dG"], cnt["dG"])
        vf(v, v.reduce_sum(out=sb["dump"][0:4, 0:1], in_=sb["wsc"][0:4, 0:8], axis=AX))
        vf(v, v.tensor_scalar(out=sb["wsc"][0:3, 1:2], in0=sb["dump"][0:3, 0:1],
                              scalar1=1.0 / (INTER * D), scalar2=EPS_FFN,
                              op0=ALU.mult, op1=ALU.max))
        inc(v.reciprocal(out=sb["wsc"][0:3, 2:3], in_=sb["wsc"][0:3, 1:2]), "V")

    @block.gpsimd
    def _(g):
        g.wait_ge(sem["V"], cnt["V"])
        inc(g.dma_start(out=scr["eal_scr"][0, 0:3], in_=sb["wsc"][0:3, 2:3]), "dG", 16)
        inc(g.dma_start(out=scr["eal_scr"][1, 0:3], in_=sb["wsc"][0:3, 1:2]), "dG", 16)
        g.wait_ge(sem["dG"], cnt["dG"])
        for t in range(3):
            inc(g.dma_start(out=sb["wsc_col"][:, t:t + 1],
                            in_=scr["eal_scr"][0:1, t:t + 1].broadcast_to([128, 1])), "dG", 16)
            inc(g.dma_start(out=sb["wsci_col"][:, t:t + 1],
                            in_=scr["eal_scr"][1:2, t:t + 1].broadcast_to([128, 1])), "dG", 16)
    g_scales = cnt["dG"]

    qoff = [0, SH12, 2 * SH12]
    vmi2 = []
    for name, ntile, ti in shard_list:
        for i in range(ntile):
            gi = len(vmi2)

            @block.sync
            def _(sy, name=name, i=i, gi=gi):
                if gi >= 2:
                    sy.wait_ge(sem["V"], vmi2[gi - 2])
                ins = sy.dma_start(out=qt_in[:, (gi % 2) * 512:(gi % 2) * 512 + 512],
                                   in_=dram[name].rearrange("a b p f -> p (a b) f")
                                   [:, i * 4:i * 4 + 4, :])
                inc(ins, "dW", 16)
                sy.wait_ge(sem["dW"], cnt["dW"])

            @block.vector
            def _(v, gi=gi, ti=ti):
                v.wait_ge(sem["dW"], cnt["dW"])
                if gi == 0:
                    v.wait_ge(sem["dG"], g_scales)
                sl = qt_in[:, (gi % 2) * 512:(gi % 2) * 512 + 512]
                vf(v, v.tensor_scalar_mul(out=sb["dump"][:], in0=sl,
                                          scalar1=sb["wsc_col"][:, ti:ti + 1]))
                vf(v, v.tensor_copy(out=sl.bitcast(I32), in_=sb["dump"][:]))
                vf(v, v.tensor_copy(out=sb["dump"][:], in_=sl.bitcast(I32)))
                ins = v.tensor_scalar(out=qt_bf[:, (gi % 2) * 512:(gi % 2) * 512 + 512],
                                      in0=sb["dump"][:], scalar1=-1.0, scalar2=1.0,
                                      op0=ALU.max, op1=ALU.min)
                vmi2.append(inc(ins, "V"))

            @block.gpsimd
            def _(g, i=i, gi=gi, ti=ti):
                g.wait_ge(sem["V"], vmi2[gi])
                off = qoff[ti] + i * 65536
                ins = g.dma_start(
                    out=scr["cc3_in"][0, off:off + 65536]
                    .rearrange("(g p f) -> p g f", g=4, p=128),
                    in_=qt_bf[:, (gi % 2) * 512:(gi % 2) * 512 + 512])
                inc(ins, "dG", 16)

    @block.gpsimd
    def _(g):
        g.wait_ge(sem["dG"], cnt["dG"])
        inc(g.collective_compute("AllGather", ALU.bypass,
                                 replica_groups=[list(range(NCORES))],
                                 ins=[scr["cc3_in"]], outs=[scr["cc3_out"]]), "CC")
    cc3_done = cnt["CC"]
    dbg_dump("tern", [((t,), cc3o[0, t * 16384:(t + 1) * 16384]
                       .rearrange("(p f) -> p f", p=128)) for t in range(16)],
             (16, 128, 128), BF16)
    dbg_dump("wsc", [((0,), sb["wsc_col"][:, 0:4]), ((1,), sb["wsci_col"][:, 0:4])],
             (2, 128, 4))

    xa = ARD.alloc(D, F32)
    xb = ARD.alloc(D, F32)
    xqb = ARC.alloc(D, BF16)
    xnb = ARC.alloc(D, BF16)

    hin = ARA.alloc(KD * TH + 8, BF16)
    hinr = hin
    CO = sb["cols"]

    def p0_tile(j, rows, row0):
        colb = 3 + 128 * j if j < 8 else 0
        r = slice(0, rows)

        @block.sync
        def _(sy):
            sy.wait_ge(sem["V"], cnt["V"])
            sy.wait_ge(sem["T"], cnt["T"])
            inc(sy.dma_start(out=xa[r, :], in_=dram["x_slice"][row0:row0 + rows, :]), "dW", 16)

        @block.vector
        def _(v):
            v.wait_ge(sem["dW"], cnt["dW"])
            vf(v, v.reduce_sum(out=CO[r, 0:1], in_=xa[r, :], axis=AX))
            vf(v, v.tensor_scalar_mul(out=CO[r, 1:2], in0=CO[r, 0:1], scalar1=1.0 / D))
            v.tensor_scalar(out=xb[r, :], in0=xa[r, :], scalar1=CO[r, 1:2],
                            scalar2=None, op0=ALU.subtract)
            v.tensor_mul(out=xa[r, 0:1024], in0=xb[r, 0:1024], in1=xb[r, 0:1024])
            vf(v, v.reduce_sum(out=CO[r, 0:1], in_=xa[r, 0:1024], axis=AX))
            v.tensor_mul(out=xa[r, 0:1024], in0=xb[r, 1024:2048], in1=xb[r, 1024:2048])
            vf(v, v.reduce_sum(out=CO[r, 2:3], in_=xa[r, 0:1024], axis=AX))
            vf(v, v.tensor_add(out=CO[r, 6:7], in0=CO[r, 2:3], in1=CO[r, 0:1]))
            inc(v.tensor_scalar(out=CO[r, 3:4], in0=CO[r, 6:7], scalar1=1.0 / D,
                                scalar2=None, op0=ALU.mult), "V")

        @block.scalar
        def _(s):
            s.wait_ge(sem["V"], cnt["V"])
            inc(s.activation(out=CO[r, 8:9], in_=CO[r, 3:4], func=AF.Sqrt,
                             bias=sb["kc"][r, 0:1]), "S")

        @block.vector
        def _(v):
            v.wait_ge(sem["S"], cnt["S"])
            vf(v, v.reciprocal(out=CO[r, 2:3], in_=CO[r, 8:9]))
            vf(v, v.tensor_mul(out=CO[r, 9:10], in0=CO[r, 2:3], in1=CO[r, 2:3]))
            inc(v.tensor_mul(out=CO[r, 10:11], in0=CO[r, 9:10], in1=CO[r, 3:4]), "V")

        @block.scalar
        def _(s):
            s.wait_ge(sem["V"], cnt["V"])
            inc(s.activation(out=CO[r, 11:12], in_=CO[r, 10:11], func=AF.Sqrt,
                             bias=sb["kc"][r, 1:2]), "S")

        @block.vector
        def _(v):
            v.wait_ge(sem["S"], cnt["S"])
            v.wait_ge(sem["dW"], cnt["dW"])
            vf(v, v.reciprocal(out=CO[r, 4:5], in_=CO[r, 11:12]))
            vf(v, v.tensor_mul(out=CO[r, 5:6], in0=CO[r, 2:3], in1=CO[r, 4:5]))
            if j < 8:
                v.tensor_scalar_mul(out=xnb[r, :], in0=xb[r, :], scalar1=CO[r, 2:3])
                vf(v, v.reduce_max(out=CO[r, 6:7], in_=xb[r, :], axis=AX,
                                   apply_absolute_value=True))
                vf(v, v.tensor_mul(out=CO[r, 12:13], in0=CO[r, 6:7], in1=CO[r, 5:6]))
                vf(v, v.tensor_scalar_max(out=CO[r, 13:14], in0=CO[r, 12:13], scalar1=EPS_FFN))
                vf(v, v.reciprocal(out=CO[r, 14:15], in_=CO[r, 13:14]))
                v.tensor_scalar_mul(out=sb["iqs_cols"][r, j:j + 1], in0=CO[r, 13:14],
                                    scalar1=1.0 / 127.0)
                vf(v, v.tensor_scalar_mul(out=CO[r, 15:16], in0=CO[r, 14:15], scalar1=127.0))
                vf(v, v.tensor_mul(out=CO[r, 7:8], in0=CO[r, 5:6], in1=CO[r, 15:16]))
                for hh in range(4):
                    hsl = slice(hh * 512, hh * 512 + 512)
                    vf(v, v.tensor_scalar_mul(out=xa[r, 0:512], in0=xb[r, hsl],
                                              scalar1=CO[r, 7:8]))
                    vf(v, v.tensor_copy(out=xa[r, 512:1024].bitcast(I32), in_=xa[r, 0:512]))
                    vf(v, v.tensor_copy(out=xa[r, 0:512], in_=xa[r, 512:1024].bitcast(I32)))
                    vf(v, v.tensor_scalar(out=xqb[r, hsl], in0=xa[r, 0:512], scalar1=-128.0,
                                          scalar2=127.0, op0=ALU.max, op1=ALU.min))
            v.tensor_scalar_mul(out=xa[r, :], in0=xb[r, :], scalar1=CO[r, 5:6])
            inc(v.tensor_mul(out=xb[r, :], in0=xa[r, :], in1=norm1bc[r, :]), "V")
        v_done = cnt["V"]

        @block.tensor
        def _(te):
            te.wait_ge(sem["V"], v_done)
            for fc in range(KD):
                inc(te.transpose(psA[:, fc * 128:fc * 128 + rows],
                                 xb[r, fc * 128:fc * 128 + 128],
                                 sb["ident_f"][0:rows, 0:rows]), "T")

        @block.vector
        def _(v):
            v.wait_ge(sem["T"], cnt["T"])
            ins = None
            for fc in range(KD):
                ins = v.tensor_copy(out=hinr[:, fc * TH + colb:fc * TH + colb + rows],
                                    in_=psA[:, fc * 128:fc * 128 + rows])
            inc(ins, "V")

        if j < 8:
            @block.tensor
            def _(te):
                te.wait_ge(sem["V"], cnt["V"])
                for fc in range(KD):
                    inc(te.transpose(psBb[:, fc * 128:fc * 128 + 128],
                                     xnb[:, fc * 128:fc * 128 + 128],
                                     sb["ident_b"][:]), "T")

            @block.vector
            def _(v):
                v.wait_ge(sem["T"], cnt["T"])
                inc(v.tensor_copy(out=xnb[:], in_=psBb[:, 0:2048]), "V")
            v_xn = cnt["V"]

            @block.sync
            def _(sy):
                sy.wait_ge(sem["V"], v_xn)
                for fc in range(KD):
                    inc(sy.dma_start(out=scr["xn_scr"][fc, :, j * 128:j * 128 + 128],
                                     in_=xnb[:, fc * 128:fc * 128 + 128]), "dW", 16)

            @block.tensor
            def _(te):
                te.wait_ge(sem["V"], cnt["V"])
                te.wait_ge(sem["dW"], cnt["dW"])
                for fc in range(KD):
                    inc(te.transpose(psBb[:, fc * 128:fc * 128 + 128],
                                     xqb[:, fc * 128:fc * 128 + 128],
                                     sb["ident_b"][:]), "T")

            @block.vector
            def _(v):
                v.wait_ge(sem["T"], cnt["T"])
                inc(v.tensor_copy(out=xqb[:], in_=psBb[:, 0:2048]), "V")
            v_xq = cnt["V"]

            @block.sync
            def _(sy):
                sy.wait_ge(sem["V"], v_xq)
                for fc in range(KD):
                    inc(sy.dma_start(out=scr["xq_scr"][fc, :, j * 128:j * 128 + 128],
                                     in_=xqb[:, fc * 128:fc * 128 + 128]), "dW", 16)

    for j in range(8):
        p0_tile(j, 128, 3 + 128 * j)
    p0_tile(8, 3, 0)
    p0_hin_V = cnt["V"]

    dbg_dump("hin", [((fc,), hin[:, fc * TH:(fc + 1) * TH]) for fc in range(KD)],
             (KD, 128, TH), BF16)
    dbg_dump("xn", [((fc,), scr["xn_scr"][fc]) for fc in range(KD)], (KD, 128, T), BF16)
    dbg_dump("co", [(None, CO[:])], (128, 16))
    dbg_dump("xq", [((fc,), scr["xq_scr"][fc]) for fc in range(KD)], (KD, 128, T), BF16)
    dbg_dump("iqs", [(None, sb["iqs_cols"][:])], (128, 8))
    if upto == "p0":
        return finish()

    barrier()
    ARD.reset(0)
    dt_fm = ARD.alloc(T, F32)
    acum_fm = ARD.alloc(T, F32)
    tmp_fm = ARD.alloc(T, F32)
    tmp2_fm = ARD.alloc(T, F32)

    ARC.reset(0)
    wbuf = [ARC.alloc(8 * 128, BF16), ARC.alloc(8 * 128, BF16)]
    convacc = ARC.alloc(T, F32)
    convac2 = ARC.alloc(T, F32)
    convo = ARC.alloc(T, BF16)

    xs_tm = ARB.alloc(CH * DI, BF16)

    t_ms = {}
    v_free = {}

    def emit_tmcols(src_fm, dst, wait_sem, wait_val):
        @block.tensor
        def _(te):
            te.wait_ge(sem[wait_sem], wait_val)
            te.wait_ge(sem["V"], cnt["V"])
            for c in range(CH):
                inc(te.transpose(psB[:, 1536 + c * 64:1536 + c * 64 + 64],
                                 src_fm[0:64, c * 128:c * 128 + 128],
                                 sb["ident_f"][0:64, 0:64]), "T")

        @block.vector
        def _(v):
            v.wait_ge(sem["T"], cnt["T"])
            inc(v.tensor_copy(out=dst[:], in_=psB[:, 1536:2048]), "V")

    tg_ms = {}
    for j in range(NCHX):
        par = j % 2
        ps = psA if par == 0 else psB

        for kh in range(2):
            gi = 2 * j + kh

            @block.gpsimd
            def _(g, j=j, kh=kh, gi=gi):
                if gi >= 2:
                    g.wait_ge(sem["T"], tg_ms[gi - 2])
                ins = g.dma_start(
                    out=wbuf[gi % 2][:],
                    in_=dram["w_xbcdt"][j].rearrange("k p n -> p k n")
                    [:, kh * 8:kh * 8 + 8, :])
                inc(ins, "dG", 16)
                g.wait_ge(sem["dG"], cnt["dG"])
            w_ld = cnt["dG"]

            @block.tensor
            def _(te, j=j, kh=kh, gi=gi, ps=ps, w_ld=w_ld):
                te.wait_ge(sem["dG"], w_ld)
                if gi == 0:
                    te.wait_ge(sem["V"], p0_hin_V)
                if kh == 0 and j >= 2:
                    te.wait_ge(sem["V"], v_free[j - 2])
                for ko in range(8):
                    k = kh * 8 + ko
                    lt = wbuf[gi % 2][:, ko * 128:ko * 128 + 128]
                    st, sp = (k == 0), (k == KD - 1)
                    i1 = te.matmul(ps[:, 0:512], lt, hinr[:, k * TH:k * TH + 512],
                                   start=st, stop=sp)
                    i2 = te.matmul(ps[:, 512:1024], lt, hinr[:, k * TH + 512:k * TH + 1024],
                                   start=st, stop=sp)
                    i3 = te.matmul(ps[:, 1024:1028], lt,
                                   hinr[:, k * TH + 1024:k * TH + 1028], start=st, stop=sp)
                    if sp or ko == 7:
                        inc(i1, "T")
                        inc(i2, "T")
                        inc(i3, "T")
                tg_ms[gi] = cnt["T"]
                if kh == 1:
                    t_ms[j] = tg_ms[gi]

        if j == 0:
            @block.scalar
            def _(s):
                s.wait_ge(sem["T"], t_ms[0])
                sf(s, s.activation(out=tmp_fm[0:64, :], in_=psA[0:64, 3:1027], func=AF.Abs,
                                   bias=sb["dtb"][:], scale=1.0))
                inc(s.activation(out=acum_fm[0:64, :], in_=tmp_fm[0:64, :],
                                 func=AF.Exp, scale=sb["kc"][0:64, 3:4]), "S")

            @block.vector
            def _(v):
                v.wait_ge(sem["S"], cnt["S"])
                ins = v.tensor_scalar(out=dt_fm[0:64, :], in0=psA[0:64, 3:1027],
                                      scalar1=sb["dtb"][:], scalar2=0.0,
                                      op0=ALU.add, op1=ALU.max)
                v_free[0] = inc(ins, "V")

            @block.scalar
            def _(s):
                s.wait_ge(sem["V"], cnt["V"])
                inc(s.activation(out=tmp_fm[0:64, :], in_=acum_fm[0:64, :],
                                 func=AF.Ln, bias=sb["kc"][0:64, 2:3]), "S")

            @block.vector
            def _(v):
                v.wait_ge(sem["S"], cnt["S"])
                v.tensor_add(out=dt_fm[0:64, :], in0=tmp_fm[0:64, :], in1=dt_fm[0:64, :])
                v.tensor_scalar_mul(out=tmp2_fm[0:64, :], in0=dt_fm[0:64, :],
                                    scalar1=sb["negA"][:])
                ins = None
                for c in range(CH):
                    ins = v.tensor_tensor_scan(out=acum_fm[0:64, c * 128:c * 128 + 128],
                                               data0=tmp2_fm[0:64, c * 128:c * 128 + 128],
                                               data1=tmp2_fm[0:64, c * 128:c * 128 + 128],
                                               initial=0.0, op0=ALU.add, op1=ALU.bypass)
                inc(ins, "V")
            v_acum = cnt["V"]

            @block.sync
            def _(sy):
                sy.wait_ge(sem["V"], v_acum)
                inc(sy.dma_start(out=scr["acum_scr"], in_=acum_fm[0:64, :]), "dW", 16)

            @block.vector
            def _(v):
                v.wait_ge(sem["V"], v_acum)
                ins = None
                for c in range(CH):
                    ins = v.tensor_copy(out=sb["prefx"][:, c:c + 1],
                                        in_=acum_fm[0:64, c * 128 + 127:c * 128 + 128])
                vf(v, ins)
                vf(v, v.tensor_tensor_scan(out=sb["prefx"][:, 8:16], data0=sb["prefx"][:, 0:8],
                                           data1=sb["prefx"][:, 0:8], initial=0.0,
                                           op0=ALU.add, op1=ALU.bypass))
                v.memset(sb["prefx"][:, 16:17], 0.0)
                inc(v.tensor_copy(out=sb["prefx"][:, 17:24], in_=sb["prefx"][:, 8:15]), "V")

            @block.scalar
            def _(s):
                s.wait_ge(sem["V"], cnt["V"])
                inc(s.activation(out=sb["ealst"][:], in_=sb["prefx"][:, 0:8],
                                 func=AF.Exp), "S")

            @block.sync
            def _(sy):
                sy.wait_ge(sem["S"], cnt["S"])
                inc(sy.dma_start(out=scr["eal_scr"][2:3, :]
                                 .rearrange("a (c h) -> (a h) c", h=64),
                                 in_=sb["ealst"][:],
                                 allow_slow_non_contiguous=True), "dW", 16)
            eal_w = cnt["dW"]

            @block.gpsimd
            def _(g):
                g.wait_ge(sem["dW"], eal_w)
                inc(g.dma_start(out=sb["ealrep"][:],
                                in_=scr["eal_scr"][2:3, :].broadcast_to([128, 512])), "dG", 16)

            @block.scalar
            def _(s):
                s.wait_ge(sem["V"], v_acum)
                inc(s.activation(out=tmp_fm[0:64, :], in_=dt_fm[0:64, :], func=AF.Ln), "S")
            emit_tmcols(tmp_fm, sb["lndt_tm"], "S", cnt["S"])
            emit_tmcols(acum_fm, sb["acum_tm"], "V", v_acum)

            @block.scalar
            def _(s):
                s.wait_ge(sem["V"], cnt["V"])
                inc(s.activation(out=tmp_fm[0:64, :], in_=acum_fm[0:64, :], func=AF.Exp), "S")
            emit_tmcols(tmp_fm, sb["ea_tm"], "S", cnt["S"])

            @block.vector
            def _(v):
                v.wait_ge(sem["V"], cnt["V"])
                ins = None
                for c in range(CH):
                    ins = v.tensor_scalar(out=tmp_fm[0:64, c * 128:c * 128 + 128],
                                          in0=acum_fm[0:64, c * 128:c * 128 + 128],
                                          scalar1=acum_fm[0:64, c * 128 + 127:c * 128 + 128],
                                          scalar2=None, op0=ALU.subtract)
                inc(ins, "V")

            @block.scalar
            def _(s):
                s.wait_ge(sem["V"], cnt["V"])
                inc(s.activation(out=tmp2_fm[0:64, :], in_=tmp_fm[0:64, :],
                                 func=AF.Exp, scale=sb["kc"][0:64, 3:4]), "S")

            @block.vector
            def _(v):
                v.wait_ge(sem["S"], cnt["S"])
                inc(v.tensor_mul(out=tmp_fm[0:64, :], in0=tmp2_fm[0:64, :],
                                 in1=dt_fm[0:64, :]), "V")
            emit_tmcols(tmp_fm, sb["sd_tm"], "V", cnt["V"])

            @block.vector
            def _(v):
                v.wait_ge(sem["V"], cnt["V"])
                ins = None
                for c in range(CH):
                    ins = v.tensor_scalar(out=tmp_fm[0:64, c * 128:c * 128 + 128],
                                          in0=acum_fm[0:64, c * 128:c * 128 + 128],
                                          scalar1=sb["prefx"][:, 16 + c:17 + c],
                                          scalar2=None, op0=ALU.add)
                inc(ins, "V")

            @block.scalar
            def _(s):
                s.wait_ge(sem["V"], cnt["V"])
                inc(s.activation(out=tmp2_fm[0:64, :], in_=tmp_fm[0:64, :], func=AF.Exp), "S")
            emit_tmcols(tmp2_fm, sb["ea2_tm"], "S", cnt["S"])

        elif j in (1, 2):
            tgt = sb["B_fm"] if j == 1 else sb["C_fm"]
            cch = 32 if j == 1 else 33

            @block.vector
            def _(v, ps=ps, cch=cch, j=j):
                v.wait_ge(sem["T"], t_ms[j])
                accs = [convacc[:, 0:T], convac2[:, 0:T]]
                v.tensor_scalar_mul(out=accs[0], in0=ps[:, 0:1024],
                                    scalar1=sb["cw"][:, cch * 4:cch * 4 + 1])
                ins = None
                for tap in range(1, 4):
                    ins = v.scalar_tensor_tensor(out=accs[tap % 2], in0=ps[:, tap:tap + 1024],
                                                 scalar=sb["cw"][:, cch * 4 + tap:cch * 4 + tap + 1],
                                                 in1=accs[(tap - 1) % 2], op0=ALU.mult, op1=ALU.add)
                v_free[j] = inc(ins, "V")

            @block.scalar
            def _(s, tgt=tgt, cch=cch):
                s.wait_ge(sem["V"], cnt["V"])
                inc(s.activation(out=tgt[:, 0:T], in_=convac2[:, 0:T], func=AF.Silu,
                                 bias=sb["cb"][:, cch:cch + 1], scale=1.0), "S")
            s_cv = cnt["S"]

            if j == 1:
                for c in range(CH):
                    @block.tensor
                    def _(te, c=c, s_cv=s_cv):
                        te.wait_ge(sem["S"], s_cv)
                        if c > 0:
                            te.wait_ge(sem["V"], cnt["V"])
                        inc(te.transpose(psAb[:, 3840:3968],
                                         sb["B_fm"][:, c * 128:c * 128 + 128],
                                         sb["ident_b"][:]), "T")

                    @block.vector
                    def _(v, c=c):
                        v.wait_ge(sem["T"], cnt["T"])
                        inc(v.tensor_copy(out=sb["B_tm"][:, c * 128:c * 128 + 128],
                                          in_=psAb[:, 3840:3968]), "V")
        else:
            ci = j - 3

            @block.vector
            def _(v, ps=ps, ci=ci, j=j):
                v.wait_ge(sem["T"], t_ms[j])
                accs = [convacc[:, 0:T], convac2[:, 0:T]]
                v.tensor_scalar_mul(out=accs[0], in0=ps[:, 0:1024],
                                    scalar1=sb["cw"][:, ci * 4:ci * 4 + 1])
                ins = None
                for tap in range(1, 4):
                    ins = v.scalar_tensor_tensor(out=accs[tap % 2], in0=ps[:, tap:tap + 1024],
                                                 scalar=sb["cw"][:, ci * 4 + tap:ci * 4 + tap + 1],
                                                 in1=accs[(tap - 1) % 2], op0=ALU.mult, op1=ALU.add)
                inc(ins, "V")

            @block.scalar
            def _(s, ci=ci):
                s.wait_ge(sem["V"], cnt["V"])
                inc(s.activation(out=convo[:, 0:T], in_=convac2[:, 0:T], func=AF.Silu,
                                 bias=sb["cb"][:, ci:ci + 1], scale=1.0), "S")

            @block.tensor
            def _(te, ps=ps):
                te.wait_ge(sem["S"], cnt["S"])
                pb = ps.bitcast(BF16)
                for c in range(CH):
                    inc(te.transpose(pb[:, c * 128:c * 128 + 128],
                                     convo[:, c * 128:c * 128 + 128],
                                     sb["ident_b"][:]), "T")

            @block.vector
            def _(v, ps=ps, ci=ci, j=j):
                v.wait_ge(sem["T"], cnt["T"])
                pb = ps.bitcast(BF16)
                ins = None
                for c in range(CH):
                    ins = v.tensor_copy(out=xs_tm[:, c * DI + ci * 128:c * DI + ci * 128 + 128],
                                        in_=pb[:, c * 128:c * 128 + 128])
                v_free[j] = inc(ins, "V")

    dbg_dump("dt", [(None, dt_fm[0:64, :])], (64, T))
    dbg_dump("acum", [(None, acum_fm[0:64, :])], (64, T))
    dbg_dump("bfm", [(None, sb["B_fm"][:, 0:T])], (128, T), BF16)
    dbg_dump("cfm", [(None, sb["C_fm"][:, 0:T])], (128, T), BF16)
    dbg_dump("xstm", [((c,), xs_tm[:, c * DI:(c + 1) * DI]) for c in range(CH)],
             (CH, 128, DI), BF16)
    if upto == "p1":
        return finish()

    @block.sync
    def _(sy):
        sy.wait_ge(sem["T"], cnt["T"])
        sy.wait_ge(sem["V"], cnt["V"])
        for fc in range(KD):
            inc(sy.dma_start(out=scr["hin_scr"][fc], in_=hin[:, fc * TH:(fc + 1) * TH]),
                "dW", 16)
    hin_spill = cnt["dW"]

    barrier()
    ARA.reset(0)
    Y_tm = ARA.alloc(CH * DI, BF16)
    ARD.reset(0)
    P = ARD.alloc(NH * HP, F32)
    ARC.reset(0)
    bc = [ARC.alloc(T, F32), ARC.alloc(T, F32)]
    Db = [ARC.alloc(128, F32), ARC.alloc(128, F32)]
    Eb = [ARC.alloc(128, F32), ARC.alloc(128, F32)]
    T1 = [ARC.alloc(128, BF16), ARC.alloc(128, BF16)]
    dgD = [ARC.alloc(128, BF16), ARC.alloc(128, BF16)]

    @block.vector
    def _(v):
        v.wait_ge(sem["dW"], hin_spill)
        inc(v.memset(P[:], 0.0), "V")
    p_init = cnt["V"]

    for c in range(CH):
        @block.tensor
        def _(te, c=c):
            te.wait_ge(sem["V"], p_init if c == 0 else cnt["V"])
            inc(te.matmul(psA[:, 0:128], sb["B_fm"][:, c * 128:c * 128 + 128],
                          sb["C_fm"][:, c * 128:c * 128 + 128], start=True, stop=True), "T")

        @block.vector
        def _(v, c=c):
            v.wait_ge(sem["T"], cnt["T"])
            inc(v.tensor_tensor(out=sb["Gt"][:, c * 128:c * 128 + 128],
                                in0=psA[:, 0:128], in1=sb["tri01"][:], op=ALU.mult), "V")
    gt_V = cnt["V"]

    hv = {}
    ht = {}
    t_mm = {}

    for h in range(NH):
        par = h % 2
        ps = psA if par == 0 else psB
        pbf = sb["Pbf"][:, par * 128:par * 128 + 64]

        @block.gpsimd
        def _(g, h=h, par=par):
            if h >= 2:
                g.wait_ge(sem["V"], hv[h - 2])
            if h == 0:
                g.wait_ge(sem["V"], gt_V)
            inc(g.dma_start(out=bc[par][:, 0:T],
                            in_=scr["acum_scr"][h:h + 1, :].broadcast_to([128, T])), "dG", 16)
            g.wait_ge(sem["dG"], cnt["dG"])
        bc_ld = cnt["dG"]

        @block.vector
        def _(v, h=h, par=par):
            if h >= 2:
                v.wait_ge(sem["V"], hv[h - 2])
                v.wait_ge(sem["T"], ht[h - 2])
            inc(v.tensor_scalar_mul(out=dgD[par][:, 0:128], in0=sb["ident_b"][:],
                                    scalar1=sb["Dbc"][:, h:h + 1]), "V")

        @block.scalar
        def _(s, h=h, par=par):
            s.wait_ge(sem["V"], cnt["V"])
            if h >= 2:
                s.wait_ge(sem["T"], ht[h - 2])
            for c in range(CH):
                inc(s.mul(out=sb["xdtd"][:, par * 512 + c * 64:par * 512 + c * 64 + 64],
                          in_=xs_tm[:, c * DI + h * 64:c * DI + h * 64 + 64],
                          mul=sb["sd_tm"][:, c * 64 + h:c * 64 + h + 1]), "S")
        xdtd_done = cnt["S"]

        for c in range(CH):
            i = h * CH + c
            p2 = c % 2

            @block.vector
            def _(v, h=h, c=c, par=par, p2=p2, bc_ld=bc_ld, i=i):
                v.wait_ge(sem["dG"], bc_ld)
                if i >= 2 and (i - 2) in t_mm:
                    v.wait_ge(sem["T"], t_mm[i - 2])
                ins = v.tensor_scalar(out=Db[p2][:, 0:128],
                                      in0=bc[par][:, c * 128:c * 128 + 128],
                                      scalar1=sb["acum_tm"][:, c * 64 + h:c * 64 + h + 1],
                                      scalar2=0.0, op0=ALU.subtract, op1=ALU.min)
                inc(ins, "V")

            @block.scalar
            def _(s, h=h, c=c, p2=p2):
                s.wait_ge(sem["V"], cnt["V"])
                inc(s.activation(out=Eb[p2][:, 0:128], in_=Db[p2][:, 0:128], func=AF.Exp,
                                 bias=sb["lndt_tm"][:, c * 64 + h:c * 64 + h + 1], scale=1.0),
                    "S")

            @block.gpsimd
            def _(g, h=h, c=c, p2=p2):
                g.wait_ge(sem["S"], cnt["S"])
                inc(g.tensor_tensor(out=T1[p2][:, 0:128], in0=Eb[p2][:, 0:128],
                                    in1=sb["Gt"][:, c * 128:c * 128 + 128], op=ALU.mult), "G")
            t1_done = cnt["G"]

            @block.tensor
            def _(te, h=h, c=c, par=par, p2=p2, ps=ps, pbf=pbf, t1_done=t1_done,
                  xdtd_done=xdtd_done, i=i):
                te.wait_ge(sem["G"], t1_done)
                if c == 0 and h >= 2:
                    te.wait_ge(sem["V"], hv[h - 2])
                xsl = xs_tm[:, c * DI + h * 64:c * DI + h * 64 + 64]
                te.matmul(ps[:, c * 64:c * 64 + 64], T1[p2][:, 0:128], xsl,
                          start=True, stop=False)
                inc(te.matmul(ps[:, c * 64:c * 64 + 64], dgD[par][:, 0:128], xsl,
                              start=False, stop=True), "T")
                if c > 0:
                    te.wait_ge(sem["S"], cnt["S"])
                    inc(te.matmul(ps[:, 512 + c * 64:512 + c * 64 + 64],
                                  sb["C_fm"][:, c * 128:c * 128 + 128], pbf,
                                  start=True, stop=True), "T")
                te.wait_ge(sem["S"], xdtd_done)
                ins = te.matmul(ps[:, 1024 + c * 64:1024 + c * 64 + 64],
                                sb["B_tm"][:, c * 128:c * 128 + 128],
                                sb["xdtd"][:, par * 512 + c * 64:par * 512 + c * 64 + 64],
                                start=True, stop=True)
                t_mm[i] = inc(ins, "T")

            @block.scalar
            def _(s, h=h, c=c, i=i):
                s.wait_ge(sem["T"], t_mm[i])
                inc(s.mul(out=sb["ptmp"][:], in_=P[:, h * 64:h * 64 + 64],
                          mul=sb["ealrep"][:, c * 64 + h:c * 64 + h + 1]), "S")

            @block.vector
            def _(v, h=h, c=c, ps=ps, i=i):
                v.wait_ge(sem["S"], cnt["S"])
                ins = v.tensor_add(out=P[:, h * 64:h * 64 + 64],
                                   in0=ps[:, 1024 + c * 64:1024 + c * 64 + 64],
                                   in1=sb["ptmp"][:])
                inc(ins, "V")

            if c < CH - 1:
                @block.scalar
                def _(s, h=h, pbf=pbf):
                    s.wait_ge(sem["V"], cnt["V"])
                    inc(s.copy(out=pbf, in_=P[:, h * 64:h * 64 + 64]), "S")

        ht[h] = cnt["T"]

        @block.vector
        def _(v, h=h, ps=ps):
            v.wait_ge(sem["T"], ht[h])
            yv = Y_tm.rearrange("p (c f) -> p c f", c=CH)[:, :, h * 64:h * 64 + 64]
            v.tensor_copy(out=yv, in_=ps[:, 0:512].rearrange("p (c f) -> p c f", c=CH))
            ins = None
            for c in range(1, CH):
                ins = v.scalar_tensor_tensor(out=Y_tm[:, c * DI + h * 64:c * DI + h * 64 + 64],
                                             in0=ps[:, 512 + c * 64:512 + c * 64 + 64],
                                             scalar=sb["ea_tm"][:, c * 64 + h:c * 64 + h + 1],
                                             in1=Y_tm[:, c * DI + h * 64:c * DI + h * 64 + 64],
                                             op0=ALU.mult, op1=ALU.add)
            hv[h] = inc(ins, "V")

    ssd_V = cnt["V"]

    @block.gpsimd
    def _(g):
        g.wait_ge(sem["V"], ssd_V)
        inc(g.dma_start(out=scr["cc4_in"][0, 0:128 * 4096].rearrange("(p f) -> p f", p=128),
                        in_=P[:]), "dG", 16)
        inc(g.dma_start(out=scr["cc4_in"][0, 128 * 4096:128 * 4096 + 64],
                        in_=sb["prefx"][:, 15:16]), "dG", 16)
        g.wait_ge(sem["dG"], cnt["dG"])
        inc(g.collective_compute("AllGather", ALU.bypass,
                                 replica_groups=[list(range(NCORES))],
                                 ins=[scr["cc4_in"]], outs=[scr["cc4_out"]]), "CC")
    cc4_done = cnt["CC"]

    dbg_dump("ytm_loc", [((c,), Y_tm[:, c * DI:(c + 1) * DI]) for c in range(CH)],
             (CH, 128, DI), BF16)
    dbg_dump("pstate", [(None, P[:])], (128, NH * HP))
    if upto == "ssd":
        return finish()

    barrier()
    ARC.reset(0)
    bbuf = ARC.alloc(4096, F32)
    cvb = ARC.alloc(64, F32)
    ARB.reset(0)
    coefb = ARB.alloc(4096, BF16)

    @block.gpsimd
    def _(g):
        g.wait_ge(sem["CC"], cc4_done)
        inc(g.dma_start(out=sb["svals"][:],
                        in_=scr["cc4_out"][:, 128 * 4096:128 * 4096 + 64]), "dG", 16)
        g.wait_ge(sem["dG"], cnt["dG"])

    @block.tensor
    def _(te):
        te.wait_ge(sem["dG"], cnt["dG"])
        inc(te.matmul(psA[0:8, 0:64], sb["selw"][:], sb["svals"][:], start=True, stop=True), "T")

    @block.scalar
    def _(s):
        s.wait_ge(sem["T"], cnt["T"])
        inc(s.activation(out=sb["cvals"][:], in_=psA[0:8, 0:64], func=AF.Exp), "S")

    @block.vector
    def _(v):
        v.wait_ge(sem["S"], cnt["S"])
        inc(v.tensor_scalar_mul(out=sb["svals"][:], in0=sb["cvals"][:],
                                scalar1=sb["loww"][:]), "V")

    @block.sync
    def _(sy):
        sy.wait_ge(sem["V"], cnt["V"])
        inc(sy.dma_start(out=scr["cvals_scr"], in_=sb["svals"][:]), "dW", 16)
    cv_w = cnt["dW"]

    @block.vector
    def _(v):
        v.wait_ge(sem["dW"], cv_w)
        inc(v.memset(P[:], 0.0), "V")

    for jj in range(NCORES):
        @block.sync
        def _(sy, jj=jj):
            sy.wait_ge(sem["V"], cnt["V"])
            inc(sy.dma_start(out=bbuf[:],
                             in_=scr["cc4_out"][jj, 0:128 * 4096]
                             .rearrange("(p f) -> p f", p=128)), "dW", 16)

        @block.gpsimd
        def _(g, jj=jj):
            g.wait_ge(sem["V"], cnt["V"])
            ins = g.dma_start(out=cvb[:, 0:64],
                              in_=scr["cvals_scr"][jj:jj + 1, :].broadcast_to([128, 64]))
            inc(ins, "dG", 16)
            g.wait_ge(sem["dG"], cnt["dG"])

        @block.vector
        def _(v, jj=jj):
            v.wait_ge(sem["dW"], cnt["dW"])
            v.wait_ge(sem["dG"], cnt["dG"])
            ins = None
            for h in range(NH):
                ins = v.scalar_tensor_tensor(out=P[:, h * 64:h * 64 + 64],
                                             in0=bbuf[:, h * 64:h * 64 + 64],
                                             scalar=cvb[:, h:h + 1],
                                             in1=P[:, h * 64:h * 64 + 64],
                                             op0=ALU.mult, op1=ALU.add)
            inc(ins, "V")

    @block.vector
    def _(v):
        v.wait_ge(sem["V"], cnt["V"])
        inc(v.tensor_copy(out=coefb[:], in_=P[:]), "V")
    pin_bf = cnt["V"]
    dbg_dump("pin", [(None, P[:])], (128, NH * HP))
    dbg_dump("cv", [(None, sb["svals"][:])], (8, 64))

    for c in range(CH):
        @block.tensor
        def _(te, c=c):
            te.wait_ge(sem["V"], cnt["V"])
            for nb in range(8):
                ps2 = psA if nb < 4 else psB
                inc(te.matmul(ps2[:, (nb % 4) * 512:(nb % 4) * 512 + 512],
                              sb["C_fm"][:, c * 128:c * 128 + 128],
                              coefb[:, nb * 512:nb * 512 + 512],
                              start=True, stop=True), "T")

        @block.vector
        def _(v, c=c):
            v.wait_ge(sem["T"], cnt["T"])
            ins = None
            for h in range(NH):
                ps2 = psA if h < 32 else psB
                ho = h % 32
                ins = v.scalar_tensor_tensor(out=Y_tm[:, c * DI + h * 64:c * DI + h * 64 + 64],
                                             in0=ps2[:, ho * 64:ho * 64 + 64],
                                             scalar=sb["ea2_tm"][:, c * 64 + h:c * 64 + h + 1],
                                             in1=Y_tm[:, c * DI + h * 64:c * DI + h * 64 + 64],
                                             op0=ALU.mult, op1=ALU.add)
            inc(ins, "V")

    dbg_dump("ytm", [((c,), Y_tm[:, c * DI:(c + 1) * DI]) for c in range(CH)],
             (CH, 128, DI), BF16)
    if upto == "dy":
        return finish()

    barrier()
    ARB.reset(0)
    Y_fm = ARB.alloc(KDI * T, BF16)

    for c in range(CH):
        for grp in range(2):
            ps = psAb if grp == 0 else psBb

            @block.tensor
            def _(te, c=c, grp=grp, ps=ps):
                te.wait_ge(sem["V"], cnt["V"])
                for q2 in range(16):
                    q = grp * 16 + q2
                    inc(te.transpose(ps[:, q2 * 128:q2 * 128 + 128],
                                     Y_tm[:, c * DI + q * 128:c * DI + q * 128 + 128],
                                     sb["ident_b"][:]), "T")

            @block.vector
            def _(v, c=c, grp=grp, ps=ps):
                v.wait_ge(sem["T"], cnt["T"])
                ins = None
                for q2 in range(16):
                    q = grp * 16 + q2
                    ins = v.tensor_copy(out=Y_fm[:, q * T + c * 128:q * T + c * 128 + 128],
                                        in_=ps[:, q2 * 128:q2 * 128 + 128])
                inc(ins, "V")
    ytrans_V = cnt["V"]

    barrier()
    ARA.reset(0)
    hin2 = ARA.alloc(KD * T, BF16)

    @block.sync
    def _(sy):
        sy.wait_ge(sem["V"], ytrans_V)
        for fc in range(KD):
            inc(sy.dma_start(out=hin2[:, fc * T:(fc + 1) * T],
                             in_=scr["hin_scr"][fc][:, 3:TH]), "dW", 16)
    hin_rel = cnt["dW"]

    ARC.reset(0)
    zwb = [[None, None, ARC.alloc(KD * 128, BF16), ARC.alloc(KD * 128, BF16)]]
    siluz = ARC.alloc(T, F32)
    sqb = ARC.alloc(T, F32)

    def z_evac(ncu, reg):
        ps, off = reg

        @block.scalar
        def _(s, ncu=ncu, ps=ps, off=off):
            s.wait_ge(sem["T"], cnt["T"])
            inc(s.activation(out=siluz[:, 0:T], in_=ps[:, off:off + 1024], func=AF.Silu), "S")

        @block.vector
        def _(v, ncu=ncu):
            v.wait_ge(sem["S"], cnt["S"])
            ins = v.tensor_tensor(out=Y_fm[:, ncu * T:(ncu + 1) * T],
                                  in0=siluz[:, 0:T],
                                  in1=Y_fm[:, ncu * T:(ncu + 1) * T], op=ALU.mult)
            m = inc(ins, "V")

        @block.scalar
        def _(s, ncu=ncu):
            s.wait_ge(sem["V"], cnt["V"])
            inc(s.activation(out=sqb[:, 0:T], in_=Y_fm[:, ncu * T:(ncu + 1) * T],
                             func=AF.Square), "S")

        @block.tensor
        def _(te, ncu=ncu):
            te.wait_ge(sem["S"], cnt["S"])
            st = (ncu == 0)
            inc(te.matmul(psB[0:1, 1536:2048], sb["onesc"][:, 0:1], sqb[:, 0:512],
                          start=st, stop=(ncu == KDI - 1)), "T")
            inc(te.matmul(psB[0:1, 1024:1536], sb["onesc"][:, 0:1], sqb[:, 512:1024],
                          start=st, stop=(ncu == KDI - 1)), "T")
        return cnt["V"]

    bgemm(NC=KDI, KTOT=KD, KG=KD, wsrcs=[lambda n, kg: dram["w_z"][n]
                                         .rearrange("k p n -> p k n")],
          wdt=BF16, cast=True,
          rhs_fn=lambda pi, k, cb: hin2[:, k * T + cb * 512:k * T + cb * 512 + 512],
          ncols=2, regions=[(psA, 0), (psA, 1024)], evac_fn=z_evac, bufs=[zwb[0]],
          pre_wait_T=lambda te: te.wait_ge(sem["dW"], hin_rel))
    zstat_T = cnt["T"]

    @block.vector
    def _(v):
        v.wait_ge(sem["T"], zstat_T)
        inc(v.tensor_copy(out=sb["rowbuf"][0:1, 0:512], in_=psB[0:1, 1536:2048]), "V")
        inc(v.tensor_copy(out=sb["rowbuf"][0:1, 512:1024], in_=psB[0:1, 1024:1536]), "V")

    @block.scalar
    def _(s):
        s.wait_ge(sem["V"], cnt["V"])
        inc(s.activation(out=sb["rowbuf"][0:1, 0:T], in_=sb["rowbuf"][0:1, 0:T],
                         func=AF.Sqrt, bias=sb["kc"][0:1, 1:2], scale=sb["kc"][0:1, 4:5]), "S")

    @block.vector
    def _(v):
        v.wait_ge(sem["S"], cnt["S"])
        inc(v.reciprocal(out=sb["rowbuf"][0:1, 0:T], in_=sb["rowbuf"][0:1, 0:T]), "V")

    @block.sync
    def _(sy):
        sy.wait_ge(sem["V"], cnt["V"])
        inc(sy.dma_start(out=scr["row_scr"][0], in_=sb["rowbuf"][0:1, 0:T]), "dW", 16)
    r3_w = cnt["dW"]

    ARC.reset(0)
    rho3bc = ARC.alloc(T, F32)
    mixcol = ARC.alloc(32, F32)

    @block.sync
    def _(sy):
        sy.wait_ge(sem["dW"], r3_w)
        inc(sy.dma_start(out=rho3bc[:], in_=scr["row_scr"][0:1, :].broadcast_to([128, T])),
            "dW", 16)
        inc(sy.dma_start(out=mixcol[:, 0:32],
                         in_=dram["mixw"].rearrange("a (c p) -> (a p) c", p=128),
                         allow_slow_non_contiguous=True), "dW", 16)
        sy.wait_ge(sem["dW"], cnt["dW"])
    r3_b = cnt["dW"]

    @block.vector
    def _(v):
        v.wait_ge(sem["dW"], r3_b)
        ins = None
        for zc in range(KDI):
            ins = v.scalar_tensor_tensor(out=Y_fm[:, zc * T:(zc + 1) * T],
                                         in0=rho3bc[:], scalar=mixcol[:, zc:zc + 1],
                                         in1=Y_fm[:, zc * T:(zc + 1) * T],
                                         op0=ALU.mult, op1=ALU.mult)
        inc(ins, "V")
    ymix_V = cnt["V"]

    dbg_dump("ymix", [((q,), Y_fm[:, q * T:(q + 1) * T]) for q in range(KDI)],
             (KDI, 128, T), BF16)
    if upto == "ymix":
        return finish()

    barrier()
    ARC.reset(0)
    owb = [[None, None, ARC.alloc(KD * 128, BF16), ARC.alloc(KD * 128, BF16)]]
    xnb2 = ARC.alloc(T, BF16)
    h1b = ARC.alloc(T, BF16)
    sqb2 = ARC.alloc(T, F32)

    def op_evac(ncu, reg):
        ps, off = reg

        @block.sync
        def _(sy, ncu=ncu):
            sy.wait_ge(sem["V"], cnt["V"])
            inc(sy.dma_start(out=xnb2[:], in_=scr["xn_scr"][ncu]), "dW", 16)

        @block.vector
        def _(v, ncu=ncu, ps=ps, off=off):
            v.wait_ge(sem["T"], cnt["T"])
            v.wait_ge(sem["dW"], cnt["dW"])
            inc(v.tensor_tensor(out=h1b[:, 0:T], in0=ps[:, off:off + 1024], in1=xnb2[:],
                                op=ALU.add), "V")
        vm = cnt["V"]

        @block.sync
        def _(sy, ncu=ncu, vm=vm):
            sy.wait_ge(sem["V"], vm)
            inc(sy.dma_start(out=scr["h1_scr"][ncu], in_=h1b[:, 0:T]), "dW", 16)

        @block.scalar
        def _(s, ncu=ncu, vm=vm):
            s.wait_ge(sem["V"], vm)
            inc(s.activation(out=sqb2[:, 0:T], in_=h1b[:, 0:T], func=AF.Square), "S")

        @block.tensor
        def _(te, ncu=ncu):
            te.wait_ge(sem["S"], cnt["S"])
            st = (ncu == 0)
            inc(te.matmul(psB[0:1, 1536:2048], sb["onesc"][:, 0:1], sqb2[:, 0:512],
                          start=st, stop=(ncu == KD - 1)), "T")
            inc(te.matmul(psB[0:1, 1024:1536], sb["onesc"][:, 0:1], sqb2[:, 512:1024],
                          start=st, stop=(ncu == KD - 1)), "T")
        return vm

    bgemm(NC=KD, KTOT=KDI, KG=KD, wsrcs=[lambda n, kg: dram["w_out"][n]
                                         .rearrange("k p n -> p k n")
                                         [:, kg * 16:kg * 16 + 16, :]],
          wdt=BF16, cast=True,
          rhs_fn=lambda pi, k, cb: Y_fm[:, k * T + cb * 512:k * T + cb * 512 + 512],
          ncols=2, regions=[(psA, 0), (psA, 1024)], evac_fn=op_evac, bufs=[owb[0]],
          pre_wait_T=lambda te: te.wait_ge(sem["V"], ymix_V))
    h1stat_T = cnt["T"]

    @block.vector
    def _(v):
        v.wait_ge(sem["T"], h1stat_T)
        inc(v.tensor_copy(out=sb["rowbuf"][0:1, 0:512], in_=psB[0:1, 1536:2048]), "V")
        inc(v.tensor_copy(out=sb["rowbuf"][0:1, 512:1024], in_=psB[0:1, 1024:1536]), "V")

    @block.scalar
    def _(s):
        s.wait_ge(sem["V"], cnt["V"])
        inc(s.activation(out=sb["rowbuf"][0:1, 0:T], in_=sb["rowbuf"][0:1, 0:T],
                         func=AF.Sqrt, bias=sb["kc"][0:1, 1:2], scale=sb["kc"][0:1, 5:6]), "S")

    @block.vector
    def _(v):
        v.wait_ge(sem["S"], cnt["S"])
        inc(v.reciprocal(out=sb["rowbuf"][0:1, 0:T], in_=sb["rowbuf"][0:1, 0:T]), "V")

    @block.sync
    def _(sy):
        sy.wait_ge(sem["V"], cnt["V"])
        inc(sy.dma_start(out=scr["row_scr"][1], in_=sb["rowbuf"][0:1, 0:T]), "dW", 16)
    r4_w = cnt["dW"]

    barrier()
    ARA.reset(0)
    h2_fm = ARA.alloc(KD * T, BF16)
    rho4bc = ARC.alloc(T, F32)
    n2col = ARC.alloc(16, F32)

    @block.sync
    def _(sy):
        sy.wait_ge(sem["dW"], r4_w)
        inc(sy.dma_start(out=rho4bc[:], in_=scr["row_scr"][1:2, :].broadcast_to([128, T])),
            "dW", 16)
        inc(sy.dma_start(out=n2col[:, 0:16],
                         in_=dram["norm2"].rearrange("a (c p) -> (a p) c", p=128),
                         allow_slow_non_contiguous=True), "dW", 16)
        sy.wait_ge(sem["dW"], cnt["dW"])
    r4_b = cnt["dW"]

    for fc in range(KD):
        @block.sync
        def _(sy, fc=fc):
            sy.wait_ge(sem["V"], cnt["V"])
            inc(sy.dma_start(out=h1b[:, 0:T], in_=scr["h1_scr"][fc]), "dW", 16)

        @block.vector
        def _(v, fc=fc):
            v.wait_ge(sem["dW"], max(cnt["dW"], r4_b))
            inc(v.scalar_tensor_tensor(out=h2_fm[:, fc * T:(fc + 1) * T],
                                       in0=rho4bc[:], scalar=n2col[:, fc:fc + 1],
                                       in1=h1b[:, 0:T], op0=ALU.mult, op1=ALU.mult), "V")
    h2_V = cnt["V"]

    dbg_dump("h2", [((fc,), h2_fm[:, fc * T:(fc + 1) * T]) for fc in range(KD)],
             (KD, 128, T), BF16)
    if upto == "h2":
        return finish()

    s_half_done = []
    for half in range(2):
        barrier()
        ARC.reset(0)
        gwb = [[None, None, ARC.alloc(8 * 128, BF16), ARC.alloc(8 * 128, BF16)],
               [None, None, ARC.alloc(8 * 128, BF16), ARC.alloc(8 * 128, BF16)]]
        silug = ARC.alloc(512, F32)
        ARB.reset(0)
        mid = ARB.alloc(KI * 512, BF16)

        def gu_evac(ncu, reg, half=half, mid=mid, silug=silug):
            ps, off = reg

            @block.scalar
            def _(s, ncu=ncu, ps=ps, off=off):
                s.wait_ge(sem["T"], cnt["T"])
                inc(s.activation(out=silug[:, 0:512], in_=ps[:, off:off + 512],
                                 func=AF.Silu), "S")

            @block.vector
            def _(v, ncu=ncu, ps=ps, off=off):
                v.wait_ge(sem["S"], cnt["S"])
                inc(v.tensor_tensor(out=mid[:, ncu * 512:ncu * 512 + 512],
                                    in0=silug[:, 0:512], in1=ps[:, off + 512:off + 1024],
                                    op=ALU.mult), "V")
            return cnt["V"]

        bgemm(NC=KI, KTOT=KD, KG=8,
              wsrcs=[lambda n, kg: dram["w_gate"][n].rearrange("k p n -> p k n")
                     [:, kg * 8:kg * 8 + 8, :],
                     lambda n, kg: dram["w_up"][n].rearrange("k p n -> p k n")
                     [:, kg * 8:kg * 8 + 8, :]],
              wdt=BF16, cast=True,
              rhs_fn=lambda pi, k, cb, half=half: h2_fm[:, k * T + half * 512:
                                                        k * T + half * 512 + 512],
              ncols=1, regions=[(psA, 0), (psA, 1024), (psB, 0), (psB, 1024)],
              evac_fn=gu_evac, bufs=gwb,
              pre_wait_T=(lambda te: te.wait_ge(sem["V"], h2_V)) if half == 0 else None)

        barrier()
        ARC.reset(0)
        dwb = [[None, None, ARC.alloc(11 * 128, BF16), ARC.alloc(11 * 128, BF16)]]
        h1b2 = ARC.alloc(T, BF16)
        sout = ARC.alloc(512, BF16)

        def dn_evac(ncu, reg, half=half, h1b2=h1b2, sout=sout):
            ps, off = reg

            @block.sync
            def _(sy, ncu=ncu):
                sy.wait_ge(sem["V"], cnt["V"])
                inc(sy.dma_start(out=h1b2[:, 0:T], in_=scr["h1_scr"][ncu]), "dW", 16)

            @block.vector
            def _(v, ncu=ncu, ps=ps, off=off, half=half):
                v.wait_ge(sem["T"], cnt["T"])
                v.wait_ge(sem["dW"], cnt["dW"])
                inc(v.tensor_tensor(out=sout[:, 0:512], in0=ps[:, off:off + 512],
                                    in1=h1b2[:, half * 512:half * 512 + 512], op=ALU.add), "V")
            vm = cnt["V"]

            @block.sync
            def _(sy, ncu=ncu, half=half, vm=vm):
                sy.wait_ge(sem["V"], vm)
                inc(sy.dma_start(out=scr["s_scr"][ncu][:, half * 512:half * 512 + 512],
                                 in_=sout[:, 0:512]), "dW", 16)
            return vm

        bgemm(NC=KD, KTOT=KI, KG=11,
              wsrcs=[lambda n, kg: dram["w_down"][n].rearrange("k p n -> p k n")
                     [:, kg * 11:kg * 11 + 11, :]],
              wdt=BF16, cast=True,
              rhs_fn=lambda pi, k, cb, mid=mid: mid[:, k * 512:k * 512 + 512],
              ncols=1, regions=[(psA, 0), (psA, 512), (psA, 1024), (psA, 1536)],
              evac_fn=dn_evac, bufs=dwb)
        s_half_done.append(cnt["dW"])

    dbg_dump("s_scr", [((fc,), scr["s_scr"][fc]) for fc in range(KD)],
             (KD, 128, T), BF16)
    if upto == "s":
        return finish()

    @block.sync
    def _(sy):
        sy.wait_ge(sem["V"], cnt["V"])
        inc(sy.dma_start(out=scr["row_scr"][2].rearrange("(j p) -> p j", p=128),
                         in_=sb["iqs_cols"][:],
                         allow_slow_non_contiguous=True), "dW", 16)
    iqs_w = cnt["dW"]

    dbgcap = None
    if "m2cap" in dbgset:
        dbgcap = nc.dram_tensor("dbg_m2cap", [KI, 128, 512], BF16,
                                kind="ExternalOutput").ap()

    for half in range(2):
        barrier()
        ARC.reset(0)
        iqsbc = ARC.alloc(512, F32)
        q2bc = ARC.alloc(512, F32)
        w3cbc = ARC.alloc(512, F32)
        combo1 = ARC.alloc(512, F32)
        combo2 = ARC.alloc(512, F32)
        sldb = ARC.alloc(512, BF16)
        ARC_M = ARC.cur
        fwb = [[ARC.alloc(8 * 128, BF16), ARC.alloc(8 * 128, BF16)],
               [ARC.alloc(8 * 128, BF16), ARC.alloc(8 * 128, BF16)]]

        ARA.reset(0)
        mid2 = ARA.alloc(KI * 512, BF16)
        q2c = mid2
        macc = ARA.alloc(512, F32)
        mtmp = ARA.alloc(512, F32)
        gsc = ARA.alloc(512, F32)
        usc = ARA.alloc(512, F32)
        sqb3 = ARA.alloc(512, F32)

        ARD.reset(0)
        xqh = ARD.alloc(KD * 512, BF16)

        @block.sync
        def _(sy, half=half):
            sy.wait_ge(sem["dW"], iqs_w)
            sy.wait_ge(sem["V"], cnt["V"])
            inc(sy.dma_start(out=iqsbc[:],
                             in_=scr["row_scr"][2:3, half * 512:half * 512 + 512]
                             .broadcast_to([128, 512])), "dW", 16)
            for fc in range(KD):
                inc(sy.dma_start(out=xqh[:, fc * 512:fc * 512 + 512],
                                 in_=scr["xq_scr"][fc][:, half * 512:half * 512 + 512]),
                    "dW", 16)
            sy.wait_ge(sem["dW"], cnt["dW"])
        xq_ld = cnt["dW"]

        @block.vector
        def _(v):
            v.wait_ge(sem["dW"], xq_ld)
            v.memset(macc[:, 0:512], 0.0)
            v.tensor_scalar_mul(out=combo1[:, 0:512], in0=iqsbc[:, 0:512],
                                scalar1=sb["wsci_col"][:, 0:1])
            inc(v.tensor_scalar_mul(out=combo2[:, 0:512], in0=iqsbc[:, 0:512],
                                    scalar1=sb["wsci_col"][:, 1:2]), "V")
        macc_V = cnt["V"]

        def f12_evac(ncu, reg, half=half, mid2=mid2, macc=macc, iqsbc=iqsbc,
                     gsc=gsc, usc=usc, sqb3=sqb3):
            ps, off = reg

            @block.vector
            def _(v, ncu=ncu, ps=ps, off=off):
                v.wait_ge(sem["T"], cnt["T"])
                inc(v.tensor_tensor(out=gsc[:, 0:512], in0=ps[:, off:off + 512],
                                    in1=combo1[:], op=ALU.mult), "V")
                inc(v.tensor_tensor(out=usc[:, 0:512], in0=ps[:, off + 512:off + 1024],
                                    in1=combo2[:], op=ALU.mult), "V")

            @block.scalar
            def _(s, ncu=ncu):
                s.wait_ge(sem["V"], cnt["V"])
                inc(s.activation(out=sqb3[:, 0:512], in_=gsc[:, 0:512], func=AF.Silu), "S")

            @block.vector
            def _(v, ncu=ncu):
                v.wait_ge(sem["S"], cnt["S"])
                vf(v, v.tensor_tensor(out=mtmp[:, 0:512], in0=sqb3[:, 0:512],
                                      in1=usc[:, 0:512], op=ALU.mult))
                inc(v.tensor_copy(out=mid2[:, ncu * 512:ncu * 512 + 512],
                                  in_=mtmp[:, 0:512]), "V")

            @block.scalar
            def _(s, ncu=ncu):
                s.wait_ge(sem["V"], cnt["V"])
                inc(s.activation(out=sqb3[:, 0:512], in_=mtmp[:, 0:512],
                                 func=AF.Square), "S")

            @block.vector
            def _(v, ncu=ncu):
                v.wait_ge(sem["S"], cnt["S"])
                inc(v.tensor_tensor(out=macc[:, 0:512], in0=sqb3[:, 0:512],
                                    in1=macc[:, 0:512], op=ALU.max), "V")

            @block.tensor
            def _(te, ncu=ncu):
                te.wait_ge(sem["S"], cnt["S"])
                inc(te.matmul(psB[0:1, 1536:2048], sb["onesc"][:, 0:1], sqb3[:, 0:512],
                              start=(ncu == 0), stop=(ncu == KI - 1)), "T")

            if half == 0 and "m2cap" in dbgset:
                @block.gpsimd
                def _(g, ncu=ncu):
                    g.wait_ge(sem["V"], cnt["V"])
                    inc(g.dma_start(out=dbgcap[ncu],
                                    in_=mid2[:, ncu * 512:ncu * 512 + 512]), "dG", 16)
            return cnt["V"]

        bgemm(NC=KI, KTOT=KD, KG=8,
              wsrcs=[lambda n, kg: wq_group(1, n, kg, 8),
                     lambda n, kg: wq_group(2, n, kg, 8)],
              wdt=BF16, cast=False,
              rhs_fn=lambda pi, k, cb, xqh=xqh: xqh[:, k * 512:k * 512 + 512],
              ncols=1, regions=[(psA, 0), (psA, 1024)],
              evac_fn=f12_evac, bufs=fwb,
              pre_wait_T=(lambda te: (te.wait_ge(sem["CC"], cc3_done),
                                      te.wait_ge(sem["V"], macc_V))[0]))
        f12_T = cnt["T"]
        if upto == "f12":
            dbg_dump("mid2", [((q,), mid2[:, q * 512:(q + 1) * 512]) for q in range(KI)],
                     (KI, 128, 512), BF16)
            return finish()

        @block.vector
        def _(v):
            v.wait_ge(sem["T"], f12_T)
            v.wait_ge(sem["V"], cnt["V"])
            inc(v.tensor_copy(out=sb["rowbuf"][0:1, 0:512], in_=psB[0:1, 1536:2048]), "V")

        @block.sync
        def _(sy):
            sy.wait_ge(sem["V"], cnt["V"])
            inc(sy.dma_start(out=scr["row_scr"][5, 0:512], in_=sb["rowbuf"][0:1, 0:512]),
                "dW", 16)
        ss_w = cnt["dW"]

        @block.sync
        def _(sy):
            sy.wait_ge(sem["dW"], ss_w)
            inc(sy.dma_start(out=sb["dump"][:, 8:12],
                             in_=scr["row_scr"][5, 0:512].rearrange("(q p) -> p q", p=128),
                             allow_slow_non_contiguous=True), "dW", 16)
        ss_r = cnt["dW"]

        @block.tensor
        def _(te):
            te.wait_ge(sem["V"], cnt["V"])
            for q in range(4):
                inc(te.transpose(psA[:, q * 128:q * 128 + 128],
                                 macc[:, q * 128:q * 128 + 128], sb["ident_f"][:]), "T")

        @block.vector
        def _(v):
            v.wait_ge(sem["T"], cnt["T"])
            v.wait_ge(sem["dW"], ss_r)
            D0 = sb["dump"]
            for q in range(4):
                vf(v, v.reduce_max(out=D0[:, 4 + q:5 + q], in_=psA[:, q * 128:q * 128 + 128],
                                   axis=AX))
            inc(v.tensor_scalar(out=D0[:, 0:4], in0=D0[:, 8:12], scalar1=1.0 / INTER,
                                scalar2=None, op0=ALU.mult), "V")

        @block.scalar
        def _(s):
            s.wait_ge(sem["V"], cnt["V"])
            sf(s, s.activation(out=sb["dump"][:, 12:16], in_=sb["dump"][:, 4:8],
                               func=AF.Sqrt))
            inc(s.activation(out=sb["dump"][:, 4:8], in_=sb["dump"][:, 0:4], func=AF.Sqrt,
                             bias=sb["kc"][:, 1:2]), "S")

        @block.vector
        def _(v):
            v.wait_ge(sem["S"], cnt["S"])
            D0 = sb["dump"]
            for q in range(4):
                vf(v, v.reciprocal(out=D0[:, 0 + q:1 + q], in_=D0[:, 4 + q:5 + q]))
            vf(v, v.tensor_mul(out=D0[:, 4:8], in0=D0[:, 0:4], in1=D0[:, 12:16]))
            vf(v, v.tensor_scalar_max(out=D0[:, 8:12], in0=D0[:, 4:8], scalar1=EPS_FFN))
            for q in range(4):
                vf(v, v.reciprocal(out=D0[:, 4 + q:5 + q], in_=D0[:, 8 + q:9 + q]))
            vf(v, v.tensor_scalar_mul(out=D0[:, 12:16], in0=D0[:, 4:8], scalar1=127.0))
            vf(v, v.tensor_mul(out=D0[:, 4:8], in0=D0[:, 0:4], in1=D0[:, 12:16]))
            vf(v, v.tensor_scalar_mul(out=D0[:, 0:4], in0=D0[:, 8:12], scalar1=1.0 / 127.0))
            inc(v.tensor_scalar_mul(out=D0[:, 8:12], in0=D0[:, 0:4],
                                    scalar1=sb["wsci_col"][:, 2:3]), "V")

        @block.sync
        def _(sy):
            sy.wait_ge(sem["V"], cnt["V"])
            inc(sy.dma_start(out=scr["row_scr"][3, 0:512].rearrange("(q p) -> p q", p=128),
                             in_=sb["dump"][:, 4:8], allow_slow_non_contiguous=True),
                "dW", 16)
            inc(sy.dma_start(out=scr["row_scr"][4, 0:512].rearrange("(q p) -> p q", p=128),
                             in_=sb["dump"][:, 8:12], allow_slow_non_contiguous=True),
                "dW", 16)
        q2r_w = cnt["dW"]

        @block.sync
        def _(sy):
            sy.wait_ge(sem["dW"], q2r_w)
            inc(sy.dma_start(out=q2bc[:], in_=scr["row_scr"][3:4, 0:512]
                             .broadcast_to([128, 512])), "dW", 16)
            inc(sy.dma_start(out=w3cbc[:], in_=scr["row_scr"][4:5, 0:512]
                             .broadcast_to([128, 512])), "dW", 16)
        q2b_b = cnt["dW"]

        for ncu in range(KI):
            @block.vector
            def _(v, ncu=ncu):
                v.wait_ge(sem["dW"], q2b_b)
                sl = mid2[:, ncu * 512:ncu * 512 + 512]
                vf(v, v.tensor_tensor(out=gsc[:, 0:512], in0=sl, in1=q2bc[:], op=ALU.mult))
                vf(v, v.tensor_copy(out=usc[:, 0:512].bitcast(I32), in_=gsc[:, 0:512]))
                vf(v, v.tensor_copy(out=gsc[:, 0:512], in_=usc[:, 0:512].bitcast(I32)))
                inc(v.tensor_scalar(out=q2c[:, ncu * 512:ncu * 512 + 512], in0=gsc[:, 0:512],
                                    scalar1=-128.0, scalar2=127.0, op0=ALU.max, op1=ALU.min),
                    "V")
        q2_V = cnt["V"]
        if upto == "q2":
            dbg_dump("d0", [(None, sb["dump"][:, 0:16])], (128, 16))
            dbg_dump("rs5", [(None, scr["row_scr"][5:6, 0:512])], (1, 512))
            dbg_dump("rs3", [(None, scr["row_scr"][3:4, 0:512])], (1, 512))
            dbg_dump("q2bcd", [(None, q2bc[:, 0:512])], (128, 512))
            dbg_dump("gscd", [(None, gsc[:, 0:512])], (128, 512))
            dbg_dump("uscd", [(None, usc[:, 0:512])], (128, 512))
            dbg_dump("mid2last", [(None, mid2[:, 43 * 512:44 * 512])], (128, 512), BF16)
            dbg_dump("q2c", [((q,), q2c[:, q * 512:(q + 1) * 512]) for q in range(KI)],
                     (KI, 128, 512), BF16)
            return finish()

        barrier()
        ARB.reset(0)
        gpark = ARB.alloc(KD * 512, F32)
        sgb = ARB.alloc(KD * 512, F32)
        ARC.reset(ARC_M)
        w3b = [[ARC.alloc(11 * 128, BF16), ARC.alloc(11 * 128, BF16)]]
        w3tmp = ARC.alloc(512, F32)

        def w3_evac(ncu, reg, half=half, gpark=gpark):
            ps, off = reg

            @block.vector
            def _(v, ncu=ncu, ps=ps, off=off):
                v.wait_ge(sem["T"], cnt["T"])
                v.wait_ge(sem["S"], cnt["S"])
                inc(v.tensor_tensor(out=w3tmp[:, 0:512],
                                    in0=ps[:, off:off + 512], in1=w3cbc[:], op=ALU.mult), "V")

            @block.scalar
            def _(s, ncu=ncu):
                s.wait_ge(sem["V"], cnt["V"])
                inc(s.activation(out=gpark[:, ncu * 512:ncu * 512 + 512],
                                 in_=w3tmp[:, 0:512], func=AF.Silu), "S")
            return cnt["V"]

        bgemm(NC=KD, KTOT=KI, KG=11,
              wsrcs=[lambda n, kg: wq_group(3, n, kg, 11)],
              wdt=BF16, cast=False,
              rhs_fn=lambda pi, k, cb, q2c=q2c: q2c[:, k * 512:k * 512 + 512],
              ncols=1, regions=[(psA, 0), (psA, 512), (psA, 1024), (psA, 1536)],
              evac_fn=w3_evac, bufs=w3b,
              pre_wait_T=lambda te, q2_V=q2_V: te.wait_ge(sem["V"], q2_V))
        w3_S = cnt["S"]

        if upto == "w3":
            dbg_dump("gpark", [((q,), gpark[:, q * 512:(q + 1) * 512]) for q in range(KD)],
                     (KD, 128, 512))
            return finish()

        for fc in range(KD):
            @block.sync
            def _(sy, fc=fc, half=half):
                sy.wait_ge(sem["V"], cnt["V"])
                inc(sy.dma_start(out=sldb[:, 0:512],
                                 in_=scr["s_scr"][fc][:, half * 512:half * 512 + 512]),
                    "dW", 16)

            @block.vector
            def _(v, fc=fc, w3_S=w3_S):
                v.wait_ge(sem["dW"], cnt["dW"])
                v.wait_ge(sem["S"], w3_S)
                inc(v.tensor_tensor(out=sgb[:, fc * 512:fc * 512 + 512],
                                    in0=gpark[:, fc * 512:fc * 512 + 512],
                                    in1=sldb[:, 0:512], op=ALU.mult), "V")
        sg_V = cnt["V"]

        ARC.reset(ARC_M)
        xld = ARC.alloc(2048, F32)
        for tb in range(4):
            trow = half * 512 + tb * 128

            @block.tensor
            def _(te, tb=tb, sg_V=sg_V):
                te.wait_ge(sem["V"], cnt["V"])
                te.wait_ge(sem["V"], sg_V)
                for fc in range(KD):
                    inc(te.transpose(psA[:, fc * 128:fc * 128 + 128],
                                     sgb[:, fc * 512 + tb * 128:fc * 512 + tb * 128 + 128],
                                     sb["ident_f"][:]), "T")
            t_tp = cnt["T"]

            @block.sync
            def _(sy, trow=trow):
                sy.wait_ge(sem["V"], cnt["V"])
                sy.wait_ge(sem["dW"], cnt["dW"])
                inc(sy.dma_start(out=xld[:, 0:2048],
                                 in_=dram["x_slice"][3 + trow:3 + trow + 128, :]), "dW", 16)

            @block.vector
            def _(v, t_tp=t_tp):
                v.wait_ge(sem["T"], t_tp)
                v.wait_ge(sem["dW"], cnt["dW"])
                inc(v.tensor_tensor(out=xld[:, 0:2048], in0=psA[:, 0:2048],
                                    in1=xld[:, 0:2048], op=ALU.add), "V")
            v_y = cnt["V"]

            @block.sync
            def _(sy, trow=trow, v_y=v_y):
                sy.wait_ge(sem["V"], v_y)
                inc(sy.dma_start(out=y_out[trow:trow + 128, :], in_=xld[:, 0:2048]), "dW", 16)

        if half == 0 and "cap" in dbgset:
            dbg_dump("cap_d0", [(None, sb["dump"][:, 0:16])], (128, 16))
            dbg_dump("cap_q2bc", [(None, q2bc[:, 0:512])], (128, 512))
            dbg_dump("cap_w3cbc", [(None, w3cbc[:, 0:512])], (128, 512))
            dbg_dump("cap_iqsbc", [(None, iqsbc[:, 0:512])], (128, 512))
            dbg_dump("cap_gpark", [(None, gpark[:, 0:512])], (128, 512))
            dbg_dump("cap_sgb", [(None, sgb[:, 0:512])], (128, 512))
            dbg_dump("cap_macc", [(None, macc[:, 0:512])], (128, 512))
            dbg_dump("cap_rowbuf", [(None, sb["rowbuf"][0:1, 0:512])], (1, 512))
            dbg_dump("cap_q2", [(None, mid2[:, 0:512])], (128, 512), BF16)
            dbg_dump("cap_xqh", [(None, xqh[:, 0:512])], (128, 512), BF16)

    return finish()


def kernel(**inputs):
    in_maps = _prep_inputs(inputs)
    nc = build_program()
    res = run_bass_kernel_spmd(nc, in_maps, core_ids=list(range(NCORES)))
    out = np.concatenate([np.asarray(r["y_out"]) for r in res.results], axis=0)
    return out.reshape(1, SEQ, D).astype(np.float32)


# version 02
# speedup vs baseline: 1.2160x; 1.2160x over previous
import numpy as np
from contextlib import ExitStack

import concourse.bass as bass
import concourse.mybir as mybir
from concourse.bass_utils import run_bass_kernel_spmd

F32 = mybir.dt.float32
F32R = mybir.dt.float32r
BF16 = mybir.dt.bfloat16
I32 = mybir.dt.int32
AF = mybir.ActivationFunctionType
ALU = mybir.AluOpType
AX = mybir.AxisListType.X

NCORES = 8
SEQ = 8192
T = 1024
TH = T + 3
D = 2048
DI = 4096
NH = 64
HP = 64
INTER = 5632
CH = 8
KD = 16
KDI = 32
KI = 44
KI_PAD = 48
EPS_LN = 1e-5
EPS_RMS = 1e-6
EPS_FFN = 1e-5
NCHX = 35
SH12 = 6 * KD * 128 * 128
SH3 = 2 * KI * 128 * 128

_cache = {}


def _prep_inputs(inp):
    import ml_dtypes
    x = np.asarray(inp["L0_attn_hidden_states"]).reshape(SEQ, D).astype(np.float32, copy=False)
    Win = np.asarray(inp["L0_attn_in_proj_w"], dtype=np.float32)
    Wout = np.asarray(inp["L0_attn_out_proj_w"], dtype=np.float32)
    Wg = np.asarray(inp["L0_attn_gate_w"], dtype=np.float32)
    Wu = np.asarray(inp["L0_attn_up_w"], dtype=np.float32)
    Wd = np.asarray(inp["L0_attn_down_w"], dtype=np.float32)
    w1 = np.asarray(inp["L0_ffn_w1"], dtype=np.float32)
    w2 = np.asarray(inp["L0_ffn_w2"], dtype=np.float32)
    w3 = np.asarray(inp["L0_ffn_w3"], dtype=np.float32)

    def btile(w, K, N):
        return np.ascontiguousarray(w.reshape(K // 128, 128, N // 128, 128).transpose(2, 0, 1, 3))

    w_xbcdt_cols = np.concatenate([
        Win[:, 8448:8512], np.zeros((D, 64), np.float32),
        Win[:, 8192:8320], Win[:, 8320:8448], Win[:, 4096:8192]], axis=1)
    w_xbcdt = btile(w_xbcdt_cols, D, NCHX * 128)
    w_z = btile(Win[:, 0:4096], D, DI)
    w_out = btile(Wout, DI, D)
    w_gate = btile(Wg, D, INTER)
    w_up = btile(Wu, D, INTER)
    w_down = btile(Wd, INTER, D)
    w1p = np.concatenate([np.ascontiguousarray(w1.T),
                          np.zeros((D, KI_PAD * 128 - INTER), np.float32)], 1)
    w2p = np.concatenate([np.ascontiguousarray(w2.T),
                          np.zeros((D, KI_PAD * 128 - INTER), np.float32)], 1)
    w1t = btile(w1p, D, KI_PAD * 128)
    w2t = btile(w2p, D, KI_PAD * 128)
    w3t = btile(np.ascontiguousarray(w3.T), INTER, D)

    conv_w = np.asarray(inp["L0_attn_conv_w"], dtype=np.float32)
    conv_b = np.asarray(inp["L0_attn_conv_b"], dtype=np.float32)
    cw = np.ascontiguousarray(conv_w.reshape(34, 128, 4).transpose(1, 0, 2).reshape(128, 136))
    cb = np.ascontiguousarray(conv_b.reshape(34, 128).T)

    consts = dict(
        ident_f=np.eye(128, dtype=np.float32),
        ident_b=np.eye(128, dtype=np.float32).astype(ml_dtypes.bfloat16),
        tri01=np.tril(np.ones((128, 128), np.float32)).T.astype(ml_dtypes.bfloat16),
        cw=cw, cb=cb,
        dtb=np.asarray(inp["L0_attn_dt_bias"], np.float32).reshape(64, 1),
        alog=np.asarray(inp["L0_attn_A_log"], np.float32).reshape(64, 1),
        drow=np.asarray(inp["L0_attn_D_param"], np.float32).reshape(1, 64),
        norm1=np.asarray(inp["L0_attn_norm1_w"], np.float32).reshape(1, D),
        norm2=np.asarray(inp["L0_attn_norm2_w"], np.float32).reshape(1, D),
        mixw=np.asarray(inp["L0_attn_mixer_norm_w"], np.float32).reshape(1, DI),
    )

    in_maps = []
    for k in range(NCORES):
        lo = k * T
        xs_sl = np.zeros((TH, D), np.float32)
        if k == 0:
            xs_sl[3:] = x[0:T]
        else:
            xs_sl[:] = x[lo - 3:lo + T]
        sel = np.zeros((8, 8), np.float32)
        low = np.zeros((8, 1), np.float32)
        for j in range(8):
            if j < k:
                low[j, 0] = 1.0
            for i in range(8):
                if j < i < k:
                    sel[i, j] = 1.0
        in_maps.append(dict(
            x_slice=np.ascontiguousarray(xs_sl),
            w_xbcdt=w_xbcdt, w_z=w_z, w_out=w_out,
            w_gate=w_gate, w_up=w_up, w_down=w_down,
            w1s=np.ascontiguousarray(w1t[6 * k:6 * k + 6]),
            w2s=np.ascontiguousarray(w2t[6 * k:6 * k + 6]),
            w3s=np.ascontiguousarray(w3t[2 * k:2 * k + 2]),
            sel=sel, low=low, **consts))
    return in_maps


class Arena:
    def __init__(self, t, elems, name):
        self.t, self.elems, self.cur, self.name = t, elems, 0, name

    def alloc(self, n_elems, dt):
        nb = n_elems * (2 if dt == BF16 else 4) // 2
        nb = (nb + 15) // 16 * 16
        off = self.cur
        assert off + nb <= self.elems, f"{self.name} overflow {off}+{nb}>{self.elems}"
        self.cur = off + nb
        sl = self.t[:, off:off + nb]
        return sl if dt == BF16 else sl.bitcast(dt)

    def reset(self, to=0):
        self.cur = to


def build_program(dbg=(), upto=None):
    key = (tuple(sorted(dbg)), upto)
    if key in _cache:
        return _cache[key]
    nc = bass.Bass("TRN2", target_bir_lowering=False)
    stack = ExitStack()
    sem, cnt = {}, {}
    for n in ["dW", "dG", "T", "V", "S", "G", "CC"]:
        sem[n] = stack.enter_context(nc.semaphore(n))
        cnt[n] = 0
    dbgset = set(dbg)

    def inc(ins, s, n=1):
        cnt[s] += n
        ins.then_inc(sem[s], n)
        return cnt[s]

    def vf(v, ins):
        val = inc(ins, "V")
        v.wait_ge(sem["V"], val)

    def sf(s, ins):
        val = inc(ins, "S")
        s.wait_ge(sem["S"], val)

    dram = {}
    for name, shape, dt in [
            ("x_slice", (TH, D), F32), ("w_xbcdt", (NCHX, KD, 128, 128), F32),
            ("w_z", (KDI, KD, 128, 128), F32), ("w_out", (KD, KDI, 128, 128), F32),
            ("w_gate", (KI, KD, 128, 128), F32), ("w_up", (KI, KD, 128, 128), F32),
            ("w_down", (KD, KI, 128, 128), F32),
            ("w1s", (6, KD, 128, 128), F32), ("w2s", (6, KD, 128, 128), F32),
            ("w3s", (2, KI, 128, 128), F32),
            ("sel", (8, 8), F32), ("low", (8, 1), F32),
            ("ident_f", (128, 128), F32), ("ident_b", (128, 128), BF16),
            ("tri01", (128, 128), BF16), ("cw", (128, 136), F32), ("cb", (128, 34), F32),
            ("dtb", (64, 1), F32), ("alog", (64, 1), F32), ("drow", (1, 64), F32),
            ("norm1", (1, D), F32), ("norm2", (1, D), F32), ("mixw", (1, DI), F32)]:
        dram[name] = nc.dram_tensor(name, list(shape), dt, kind="ExternalInput").ap()
    y_out = nc.dram_tensor("y_out", [T, D], F32, kind="ExternalOutput").ap()

    scr = {}
    for name, shape, dt in [
            ("xn_scr", (KD, 128, T), BF16), ("xq_scr", (KD, 128, T), BF16),
            ("hin_scr", (KD, 128, TH), BF16), ("h1_scr", (KD, 128, T), BF16),
            ("s_scr", (KD, 128, T), BF16), ("acum_scr", (64, T), F32),
            ("eal_scr", (8, 512), F32), ("row_scr", (8, 1024), F32),
            ("cvals_scr", (8, 64), F32),
            ("cc2_in", (1, 8), F32), ("cc2_out", (8, 8), F32),
            ("cc3_in", (1, 2 * SH12 + SH3), BF16), ("cc3_out", (8, 2 * SH12 + SH3), BF16),
            ("cc4_in", (1, 128 * 4096 + 64), F32), ("cc4_out", (8, 128 * 4096 + 64), F32)]:
        scr[name] = nc.dram_tensor(name, list(shape), dt).ap()
    cc3o = scr["cc3_out"]

    def wq_group(which, ncu, kg, KG):
        if which in (1, 2):
            core, loc = divmod(ncu, 6)
            off = (0 if which == 1 else SH12) + (loc * KD + kg * KG) * 16384
        else:
            core, loc = divmod(ncu, 2)
            off = 2 * SH12 + (loc * KI + kg * KG) * 16384
        n_el = KG * 16384
        return cc3o[core, off:off + n_el].rearrange("(k p f) -> p k f", p=128, k=KG)

    sb = {}
    def sbuf(name, shape, dt):
        sb[name] = stack.enter_context(nc.sbuf_tensor("sb_" + name, list(shape), dt))
        return sb[name]

    for name, shape, dt in [
            ("ident_f", (128, 128), F32), ("ident_b", (128, 128), BF16),
            ("tri01", (128, 128), BF16), ("cw", (128, 136), F32), ("cb", (128, 34), F32),
            ("dtb", (64, 1), F32), ("alog", (64, 1), F32), ("negA", (64, 1), F32),
            ("Dbc", (128, 64), F32), ("cols", (128, 16), F32),
            ("iqs_cols", (128, 8), F32),
            ("wsc", (4, 8), F32), ("wsc_col", (128, 4), F32), ("wsci_col", (128, 4), F32),
            ("lndt_tm", (128, CH * 64), F32), ("acum_tm", (128, CH * 64), F32),
            ("ea_tm", (128, CH * 64), F32), ("ea2_tm", (128, CH * 64), F32),
            ("sd_tm", (128, CH * 64), F32), ("ealrep", (128, CH * 64), F32),
            ("prefx", (64, 24), F32), ("ealst", (64, 8), F32),
            ("B_fm", (128, T), BF16), ("C_fm", (128, T), BF16), ("B_tm", (128, T), BF16),
            ("Gt", (128, CH * 128), BF16),
            ("Pbf", (128, 256), BF16), ("xdtd", (128, 2 * 512), BF16),
            ("ptmp", (128, 64), F32),
            ("selw", (8, 8), F32), ("loww", (8, 1), F32),
            ("svals", (8, 64), F32), ("cvals", (8, 64), F32),
            ("onesc", (128, 8), F32),
            ("rowbuf", (1, 1024), F32),
            ("kc", (128, 8), F32),
            ("dump", (128, 512), F32)]:
        sbuf(name, shape, dt)

    ARA = Arena(sbuf("ARA", (128, 32912), BF16), 32912, "A")
    ARB = Arena(sbuf("ARB", (128, 32800), BF16), 32800, "B")
    ARC = Arena(sbuf("ARC", (128, 11400), BF16), 11400, "C")
    ARD = Arena(sbuf("ARD", (128, 8200), BF16), 8200, "D")

    psA = stack.enter_context(nc.psum_tensor("psA", [128, 2048], F32))
    psB = stack.enter_context(nc.psum_tensor("psB", [128, 2048], F32))
    psAb = psA.bitcast(BF16)
    psBb = psB.bitcast(BF16)

    block = stack.enter_context(nc.Block())

    def barrier():
        vals = dict(cnt)

        def w(eng):
            for s in ["T", "V", "S", "G", "CC", "dW", "dG"]:
                if vals[s]:
                    eng.wait_ge(sem[s], vals[s])

        @block.tensor
        def _(te):
            w(te)

        @block.vector
        def _(v):
            w(v)

        @block.scalar
        def _(s):
            w(s)

        @block.gpsimd
        def _(g):
            w(g)

        @block.sync
        def _(sy):
            w(sy)

    def dbg_dump(tag, srcs, shape, dt=F32):
        if tag not in dbgset:
            return
        o = nc.dram_tensor("dbg_" + tag, list(shape), dt, kind="ExternalOutput").ap()

        @block.sync
        def _(sy):
            for s in ["T", "V", "S", "G", "CC"]:
                if cnt[s]:
                    sy.wait_ge(sem[s], cnt[s])
            for osl, src in srcs:
                inc(sy.dma_start(out=o[osl] if osl is not None else o, in_=src), "dW", 16)

    def finish():
        @block.sync
        def _(sy):
            for s in ["T", "V", "S", "G", "CC", "dG", "dW"]:
                if cnt[s]:
                    sy.wait_ge(sem[s], cnt[s])
        stack.close()
        _cache[key] = nc
        return nc

    def bgemm(NC, KTOT, KG, wsrcs, wdt, cast, rhs_fn, ncols, regions, evac_fn,
              bufs, pre_wait_T=None, skip_n=None):
        NGR = KTOT // KG
        npair = len(wsrcs)
        t_grp = {}
        v_ncu = {}
        width = ncols * 512 * npair
        gidx = 0
        for ncu in range(NC):
            if skip_n and ncu in skip_n:
                continue
            reg = regions[ncu % len(regions)]
            ps, off = reg
            for kg in range(NGR):
                par = gidx % 2
                if cast:
                    for pi in range(npair):
                        @block.gpsimd
                        def _(g, ncu=ncu, kg=kg, par=par, pi=pi, gidx=gidx):
                            if gidx >= 2:
                                g.wait_ge(sem["T"], t_grp[gidx - 2])
                            ins = g.dma_start(out=bufs[pi][2 + par][:], in_=wsrcs[pi](ncu, kg))
                            inc(ins, "dG", 16)
                            g.wait_ge(sem["dG"], cnt["dG"])
                else:
                    for pi in range(npair):
                        @block.sync
                        def _(sy, ncu=ncu, kg=kg, par=par, pi=pi, gidx=gidx):
                            if gidx >= 2:
                                sy.wait_ge(sem["T"], t_grp[gidx - 2])
                            if gidx == 0 and pre_wait_T is not None:
                                sy.wait_ge(sem["CC"], cnt["CC"])
                            ins = sy.dma_start(out=bufs[pi][par][:], in_=wsrcs[pi](ncu, kg))
                            inc(ins, "dW", 16)
                            sy.wait_ge(sem["dW"], cnt["dW"])
                ld = cnt["dW"]
                gld = cnt["dG"]

                @block.tensor
                def _(te, ncu=ncu, kg=kg, par=par, ps=ps, off=off, ld=ld, gld=gld,
                      gidx=gidx):
                    if cast:
                        te.wait_ge(sem["dG"], gld)
                    else:
                        te.wait_ge(sem["dW"], ld)
                    if pre_wait_T is not None and ncu == 0 and kg == 0:
                        pre_wait_T(te)
                    if kg == 0 and len(v_ncu) >= len(regions):
                        prev = ncu - len(regions)
                        while prev not in v_ncu and prev >= 0:
                            prev -= 1
                        if prev in v_ncu:
                            te.wait_ge(sem["V"], v_ncu[prev])
                    for ko in range(KG):
                        k = kg * KG + ko
                        st, sp = (k == 0), (k == KTOT - 1)
                        last = (ko == KG - 1)
                        for pi in range(npair):
                            if cast:
                                lt = bufs[pi][2 + par][:, ko * 128:ko * 128 + 128]
                            elif wdt == F32R:
                                lt = bufs[pi][par].bitcast(F32R)[:, ko * 128:ko * 128 + 128]
                            else:
                                lt = bufs[pi][par][:, ko * 128:ko * 128 + 128]
                            for cb in range(ncols):
                                o2 = off + (pi * ncols + cb) * 512
                                ins = te.matmul(ps[:, o2:o2 + 512], lt, rhs_fn(pi, k, cb),
                                                start=st, stop=sp)
                                if sp or last:
                                    inc(ins, "T")
                    t_grp[gidx] = cnt["T"]
                gidx += 1
            v_ncu[ncu] = evac_fn(ncu, reg)

    @block.sync
    def _(sy):
        for n in ["ident_f", "ident_b", "tri01", "cw", "cb", "dtb", "alog"]:
            inc(sy.dma_start(out=sb[n][:], in_=dram[n]), "dW", 16)
        inc(sy.dma_start(out=sb["selw"][:], in_=dram["sel"]), "dW", 16)
        inc(sy.dma_start(out=sb["loww"][:], in_=dram["low"]), "dW", 16)
        inc(sy.dma_start(out=sb["Dbc"][:], in_=dram["drow"].broadcast_to([128, 64])), "dW", 16)
        sy.wait_ge(sem["dW"], cnt["dW"])
    ph0 = cnt["dW"]

    norm1bc = ARC.alloc(D, F32)
    qt_in = ARC.alloc(2 * 512, F32)
    qt_bf = ARC.alloc(2 * 512, BF16)
    acc_abs = ARC.alloc(8, F32)

    @block.sync
    def _(sy):
        inc(sy.dma_start(out=norm1bc[:], in_=dram["norm1"].broadcast_to([128, D])), "dW", 16)
        sy.wait_ge(sem["dW"], cnt["dW"])

    @block.vector
    def _(v):
        v.wait_ge(sem["dW"], ph0)
        v.memset(sb["cols"][:], 0.0)
        v.memset(acc_abs[:, 0:8], 0.0)
        v.memset(sb["onesc"][:], 1.0)
        KC = sb["kc"]
        v.memset(KC[:, 0:1], EPS_LN)
        v.memset(KC[:, 1:2], EPS_RMS)
        v.memset(KC[:, 2:3], 1.0)
        v.memset(KC[:, 3:4], -1.0)
        v.memset(KC[:, 4:5], 1.0 / DI)
        v.memset(KC[:, 5:6], 1.0 / D)
        v.memset(KC[:, 6:7], 1.0 / INTER)
        inc(v.memset(sb["Pbf"][:], 0.0), "V")

    @block.scalar
    def _(s):
        s.wait_ge(sem["dW"], ph0)
        inc(s.activation(out=sb["dump"][0:64, 0:1], in_=sb["alog"][:], func=AF.Exp), "S")

    @block.vector
    def _(v):
        v.wait_ge(sem["S"], cnt["S"])
        inc(v.tensor_scalar_mul(out=sb["negA"][:], in0=sb["dump"][0:64, 0:1],
                                scalar1=-1.0), "V")

    shard_list = [("w1s", 24, 0), ("w2s", 24, 1), ("w3s", 22, 2)]
    vmi = []
    for name, ntile, ti in shard_list:
        for i in range(ntile):
            gi = len(vmi)

            @block.sync
            def _(sy, name=name, i=i, gi=gi):
                if gi >= 2:
                    sy.wait_ge(sem["V"], vmi[gi - 2])
                ins = sy.dma_start(out=qt_in[:, (gi % 2) * 512:(gi % 2) * 512 + 512],
                                   in_=dram[name].rearrange("a b p f -> p (a b) f")
                                   [:, i * 4:i * 4 + 4, :])
                inc(ins, "dW", 16)
                sy.wait_ge(sem["dW"], cnt["dW"])

            @block.vector
            def _(v, gi=gi, ti=ti):
                v.wait_ge(sem["dW"], cnt["dW"])
                sl = qt_in[:, (gi % 2) * 512:(gi % 2) * 512 + 512]
                vf(v, v.reduce_sum(out=sb["dump"][:, 0:1], in_=sl, axis=AX,
                                   apply_absolute_value=True))
                ins = v.tensor_add(out=acc_abs[:, ti:ti + 1], in0=sb["dump"][:, 0:1],
                                   in1=acc_abs[:, ti:ti + 1])
                vmi.append(inc(ins, "V"))

    @block.tensor
    def _(te):
        te.wait_ge(sem["V"], cnt["V"])
        inc(te.matmul(psA[0:4, 0:1], acc_abs[:, 0:4], sb["onesc"][:, 0:1],
                      start=True, stop=True), "T")

    @block.vector
    def _(v):
        v.wait_ge(sem["T"], cnt["T"])
        inc(v.tensor_copy(out=sb["wsc"][0:4, 0:1], in_=psA[0:4, 0:1]), "V")

    @block.gpsimd
    def _(g):
        g.wait_ge(sem["V"], cnt["V"])
        inc(g.dma_start(out=scr["cc2_in"][:, 0:4], in_=sb["wsc"][0:4, 0:1]), "dG", 16)
        g.wait_ge(sem["dG"], cnt["dG"])
        inc(g.collective_compute("AllGather", ALU.bypass,
                                 replica_groups=[list(range(NCORES))],
                                 ins=[scr["cc2_in"]], outs=[scr["cc2_out"]]), "CC")
        g.wait_ge(sem["CC"], cnt["CC"])
        inc(g.dma_start(out=sb["wsc"][0:4, 0:8],
                        in_=scr["cc2_out"].rearrange("a b -> b a")[0:4, :],
                        allow_slow_non_contiguous=True), "dG", 16)

    @block.vector
    def _(v):
        v.wait_ge(sem["dG"], cnt["dG"])
        vf(v, v.reduce_sum(out=sb["dump"][0:4, 0:1], in_=sb["wsc"][0:4, 0:8], axis=AX))
        vf(v, v.tensor_scalar(out=sb["wsc"][0:3, 1:2], in0=sb["dump"][0:3, 0:1],
                              scalar1=1.0 / (INTER * D), scalar2=EPS_FFN,
                              op0=ALU.mult, op1=ALU.max))
        inc(v.reciprocal(out=sb["wsc"][0:3, 2:3], in_=sb["wsc"][0:3, 1:2]), "V")

    @block.gpsimd
    def _(g):
        g.wait_ge(sem["V"], cnt["V"])
        inc(g.dma_start(out=scr["eal_scr"][0, 0:3], in_=sb["wsc"][0:3, 2:3]), "dG", 16)
        inc(g.dma_start(out=scr["eal_scr"][1, 0:3], in_=sb["wsc"][0:3, 1:2]), "dG", 16)
        g.wait_ge(sem["dG"], cnt["dG"])
        for t in range(3):
            inc(g.dma_start(out=sb["wsc_col"][:, t:t + 1],
                            in_=scr["eal_scr"][0:1, t:t + 1].broadcast_to([128, 1])), "dG", 16)
            inc(g.dma_start(out=sb["wsci_col"][:, t:t + 1],
                            in_=scr["eal_scr"][1:2, t:t + 1].broadcast_to([128, 1])), "dG", 16)
    g_scales = cnt["dG"]

    qoff = [0, SH12, 2 * SH12]
    vmi2 = []
    for name, ntile, ti in shard_list:
        for i in range(ntile):
            gi = len(vmi2)

            @block.sync
            def _(sy, name=name, i=i, gi=gi):
                if gi >= 2:
                    sy.wait_ge(sem["V"], vmi2[gi - 2])
                ins = sy.dma_start(out=qt_in[:, (gi % 2) * 512:(gi % 2) * 512 + 512],
                                   in_=dram[name].rearrange("a b p f -> p (a b) f")
                                   [:, i * 4:i * 4 + 4, :])
                inc(ins, "dW", 16)
                sy.wait_ge(sem["dW"], cnt["dW"])

            @block.vector
            def _(v, gi=gi, ti=ti):
                v.wait_ge(sem["dW"], cnt["dW"])
                if gi == 0:
                    v.wait_ge(sem["dG"], g_scales)
                sl = qt_in[:, (gi % 2) * 512:(gi % 2) * 512 + 512]
                vf(v, v.tensor_scalar_mul(out=sb["dump"][:], in0=sl,
                                          scalar1=sb["wsc_col"][:, ti:ti + 1]))
                vf(v, v.tensor_copy(out=sl.bitcast(I32), in_=sb["dump"][:]))
                vf(v, v.tensor_copy(out=sb["dump"][:], in_=sl.bitcast(I32)))
                ins = v.tensor_scalar(out=qt_bf[:, (gi % 2) * 512:(gi % 2) * 512 + 512],
                                      in0=sb["dump"][:], scalar1=-1.0, scalar2=1.0,
                                      op0=ALU.max, op1=ALU.min)
                vmi2.append(inc(ins, "V"))

            @block.gpsimd
            def _(g, i=i, gi=gi, ti=ti):
                g.wait_ge(sem["V"], vmi2[gi])
                off = qoff[ti] + i * 65536
                ins = g.dma_start(
                    out=scr["cc3_in"][0, off:off + 65536]
                    .rearrange("(g p f) -> p g f", g=4, p=128),
                    in_=qt_bf[:, (gi % 2) * 512:(gi % 2) * 512 + 512])
                inc(ins, "dG", 16)

    @block.gpsimd
    def _(g):
        g.wait_ge(sem["dG"], cnt["dG"])
        inc(g.collective_compute("AllGather", ALU.bypass,
                                 replica_groups=[list(range(NCORES))],
                                 ins=[scr["cc3_in"]], outs=[scr["cc3_out"]]), "CC")
    cc3_done = cnt["CC"]
    dbg_dump("tern", [((t,), cc3o[0, t * 16384:(t + 1) * 16384]
                       .rearrange("(p f) -> p f", p=128)) for t in range(16)],
             (16, 128, 128), BF16)
    dbg_dump("wsc", [((0,), sb["wsc_col"][:, 0:4]), ((1,), sb["wsci_col"][:, 0:4])],
             (2, 128, 4))

    xa = ARD.alloc(D, F32)
    xb = ARD.alloc(D, F32)
    xqb = ARC.alloc(D, BF16)
    xnb = ARC.alloc(D, BF16)

    hin = ARA.alloc(KD * TH + 8, BF16)
    hinr = hin
    CO = sb["cols"]

    def p0_tile(j, rows, row0):
        colb = 3 + 128 * j if j < 8 else 0
        r = slice(0, rows)

        @block.sync
        def _(sy):
            sy.wait_ge(sem["V"], cnt["V"])
            sy.wait_ge(sem["T"], cnt["T"])
            inc(sy.dma_start(out=xa[r, :], in_=dram["x_slice"][row0:row0 + rows, :]), "dW", 16)

        @block.vector
        def _(v):
            v.wait_ge(sem["dW"], cnt["dW"])
            vf(v, v.reduce_sum(out=CO[r, 0:1], in_=xa[r, :], axis=AX))
            vf(v, v.tensor_scalar_mul(out=CO[r, 1:2], in0=CO[r, 0:1], scalar1=1.0 / D))
            v.tensor_scalar(out=xb[r, :], in0=xa[r, :], scalar1=CO[r, 1:2],
                            scalar2=None, op0=ALU.subtract)
            v.tensor_mul(out=xa[r, 0:1024], in0=xb[r, 0:1024], in1=xb[r, 0:1024])
            vf(v, v.reduce_sum(out=CO[r, 0:1], in_=xa[r, 0:1024], axis=AX))
            v.tensor_mul(out=xa[r, 0:1024], in0=xb[r, 1024:2048], in1=xb[r, 1024:2048])
            vf(v, v.reduce_sum(out=CO[r, 2:3], in_=xa[r, 0:1024], axis=AX))
            vf(v, v.tensor_add(out=CO[r, 6:7], in0=CO[r, 2:3], in1=CO[r, 0:1]))
            inc(v.tensor_scalar(out=CO[r, 3:4], in0=CO[r, 6:7], scalar1=1.0 / D,
                                scalar2=None, op0=ALU.mult), "V")

        @block.scalar
        def _(s):
            s.wait_ge(sem["V"], cnt["V"])
            inc(s.activation(out=CO[r, 8:9], in_=CO[r, 3:4], func=AF.Sqrt,
                             bias=sb["kc"][r, 0:1]), "S")

        @block.vector
        def _(v):
            v.wait_ge(sem["S"], cnt["S"])
            vf(v, v.reciprocal(out=CO[r, 2:3], in_=CO[r, 8:9]))
            vf(v, v.tensor_mul(out=CO[r, 9:10], in0=CO[r, 2:3], in1=CO[r, 2:3]))
            inc(v.tensor_mul(out=CO[r, 10:11], in0=CO[r, 9:10], in1=CO[r, 3:4]), "V")

        @block.scalar
        def _(s):
            s.wait_ge(sem["V"], cnt["V"])
            inc(s.activation(out=CO[r, 11:12], in_=CO[r, 10:11], func=AF.Sqrt,
                             bias=sb["kc"][r, 1:2]), "S")

        @block.vector
        def _(v):
            v.wait_ge(sem["S"], cnt["S"])
            v.wait_ge(sem["dW"], cnt["dW"])
            vf(v, v.reciprocal(out=CO[r, 4:5], in_=CO[r, 11:12]))
            vf(v, v.tensor_mul(out=CO[r, 5:6], in0=CO[r, 2:3], in1=CO[r, 4:5]))
            if j < 8:
                v.tensor_scalar_mul(out=xnb[r, :], in0=xb[r, :], scalar1=CO[r, 2:3])
                vf(v, v.reduce_max(out=CO[r, 6:7], in_=xb[r, :], axis=AX,
                                   apply_absolute_value=True))
                vf(v, v.tensor_mul(out=CO[r, 12:13], in0=CO[r, 6:7], in1=CO[r, 5:6]))
                vf(v, v.tensor_scalar_max(out=CO[r, 13:14], in0=CO[r, 12:13], scalar1=EPS_FFN))
                vf(v, v.reciprocal(out=CO[r, 14:15], in_=CO[r, 13:14]))
                v.tensor_scalar_mul(out=sb["iqs_cols"][r, j:j + 1], in0=CO[r, 13:14],
                                    scalar1=1.0 / 127.0)
                vf(v, v.tensor_scalar_mul(out=CO[r, 15:16], in0=CO[r, 14:15], scalar1=127.0))
                vf(v, v.tensor_mul(out=CO[r, 7:8], in0=CO[r, 5:6], in1=CO[r, 15:16]))
                for hh in range(4):
                    hsl = slice(hh * 512, hh * 512 + 512)
                    vf(v, v.tensor_scalar_mul(out=xa[r, 0:512], in0=xb[r, hsl],
                                              scalar1=CO[r, 7:8]))
                    vf(v, v.tensor_copy(out=xa[r, 512:1024].bitcast(I32), in_=xa[r, 0:512]))
                    vf(v, v.tensor_copy(out=xa[r, 0:512], in_=xa[r, 512:1024].bitcast(I32)))
                    vf(v, v.tensor_scalar(out=xqb[r, hsl], in0=xa[r, 0:512], scalar1=-128.0,
                                          scalar2=127.0, op0=ALU.max, op1=ALU.min))
            v.tensor_scalar_mul(out=xa[r, :], in0=xb[r, :], scalar1=CO[r, 5:6])
            inc(v.tensor_mul(out=xb[r, :], in0=xa[r, :], in1=norm1bc[r, :]), "V")
        v_done = cnt["V"]

        @block.tensor
        def _(te):
            te.wait_ge(sem["V"], v_done)
            for fc in range(KD):
                inc(te.transpose(psA[:, fc * 128:fc * 128 + rows],
                                 xb[r, fc * 128:fc * 128 + 128],
                                 sb["ident_f"][0:rows, 0:rows]), "T")

        @block.vector
        def _(v):
            v.wait_ge(sem["T"], cnt["T"])
            ins = None
            for fc in range(KD):
                ins = v.tensor_copy(out=hinr[:, fc * TH + colb:fc * TH + colb + rows],
                                    in_=psA[:, fc * 128:fc * 128 + rows])
            inc(ins, "V")

        if j < 8:
            @block.tensor
            def _(te):
                te.wait_ge(sem["V"], cnt["V"])
                for fc in range(KD):
                    inc(te.transpose(psBb[:, fc * 128:fc * 128 + 128],
                                     xnb[:, fc * 128:fc * 128 + 128],
                                     sb["ident_b"][:]), "T")

            @block.vector
            def _(v):
                v.wait_ge(sem["T"], cnt["T"])
                inc(v.tensor_copy(out=xnb[:], in_=psBb[:, 0:2048]), "V")
            v_xn = cnt["V"]

            @block.sync
            def _(sy):
                sy.wait_ge(sem["V"], v_xn)
                for fc in range(KD):
                    inc(sy.dma_start(out=scr["xn_scr"][fc, :, j * 128:j * 128 + 128],
                                     in_=xnb[:, fc * 128:fc * 128 + 128]), "dW", 16)

            @block.tensor
            def _(te):
                te.wait_ge(sem["V"], cnt["V"])
                te.wait_ge(sem["dW"], cnt["dW"])
                for fc in range(KD):
                    inc(te.transpose(psBb[:, fc * 128:fc * 128 + 128],
                                     xqb[:, fc * 128:fc * 128 + 128],
                                     sb["ident_b"][:]), "T")

            @block.vector
            def _(v):
                v.wait_ge(sem["T"], cnt["T"])
                inc(v.tensor_copy(out=xqb[:], in_=psBb[:, 0:2048]), "V")
            v_xq = cnt["V"]

            @block.sync
            def _(sy):
                sy.wait_ge(sem["V"], v_xq)
                for fc in range(KD):
                    inc(sy.dma_start(out=scr["xq_scr"][fc, :, j * 128:j * 128 + 128],
                                     in_=xqb[:, fc * 128:fc * 128 + 128]), "dW", 16)

    for j in range(8):
        p0_tile(j, 128, 3 + 128 * j)
    p0_tile(8, 3, 0)
    p0_hin_V = cnt["V"]

    dbg_dump("hin", [((fc,), hin[:, fc * TH:(fc + 1) * TH]) for fc in range(KD)],
             (KD, 128, TH), BF16)
    dbg_dump("xn", [((fc,), scr["xn_scr"][fc]) for fc in range(KD)], (KD, 128, T), BF16)
    dbg_dump("co", [(None, CO[:])], (128, 16))
    dbg_dump("xq", [((fc,), scr["xq_scr"][fc]) for fc in range(KD)], (KD, 128, T), BF16)
    dbg_dump("iqs", [(None, sb["iqs_cols"][:])], (128, 8))
    if upto == "p0":
        return finish()

    barrier()
    ARD.reset(0)
    dt_fm = ARD.alloc(T, F32)
    acum_fm = ARD.alloc(T, F32)
    tmp_fm = ARD.alloc(T, F32)
    tmp2_fm = ARD.alloc(T, F32)

    ARC.reset(0)
    wbuf = [ARC.alloc(8 * 128, BF16), ARC.alloc(8 * 128, BF16)]
    convacc = ARC.alloc(T, F32)
    convac2 = ARC.alloc(T, F32)
    convo = ARC.alloc(T, BF16)

    xs_tm = ARB.alloc(CH * DI, BF16)

    t_ms = {}
    v_free = {}

    def emit_tmcols(src_fm, dst, wait_sem, wait_val):
        @block.tensor
        def _(te):
            te.wait_ge(sem[wait_sem], wait_val)
            te.wait_ge(sem["V"], cnt["V"])
            for c in range(CH):
                inc(te.transpose(psB[:, 1536 + c * 64:1536 + c * 64 + 64],
                                 src_fm[0:64, c * 128:c * 128 + 128],
                                 sb["ident_f"][0:64, 0:64]), "T")

        @block.vector
        def _(v):
            v.wait_ge(sem["T"], cnt["T"])
            inc(v.tensor_copy(out=dst[:], in_=psB[:, 1536:2048]), "V")

    tg_ms = {}
    for j in range(NCHX):
        par = j % 2
        ps = psA if par == 0 else psB

        for kh in range(2):
            gi = 2 * j + kh

            @block.gpsimd
            def _(g, j=j, kh=kh, gi=gi):
                if gi >= 2:
                    g.wait_ge(sem["T"], tg_ms[gi - 2])
                ins = g.dma_start(
                    out=wbuf[gi % 2][:],
                    in_=dram["w_xbcdt"][j].rearrange("k p n -> p k n")
                    [:, kh * 8:kh * 8 + 8, :])
                inc(ins, "dG", 16)
                g.wait_ge(sem["dG"], cnt["dG"])
            w_ld = cnt["dG"]

            @block.tensor
            def _(te, j=j, kh=kh, gi=gi, ps=ps, w_ld=w_ld):
                te.wait_ge(sem["dG"], w_ld)
                if gi == 0:
                    te.wait_ge(sem["V"], p0_hin_V)
                if kh == 0 and j >= 2:
                    te.wait_ge(sem["V"], v_free[j - 2])
                for ko in range(8):
                    k = kh * 8 + ko
                    lt = wbuf[gi % 2][:, ko * 128:ko * 128 + 128]
                    st, sp = (k == 0), (k == KD - 1)
                    i1 = te.matmul(ps[:, 0:512], lt, hinr[:, k * TH:k * TH + 512],
                                   start=st, stop=sp)
                    i2 = te.matmul(ps[:, 512:1024], lt, hinr[:, k * TH + 512:k * TH + 1024],
                                   start=st, stop=sp)
                    i3 = te.matmul(ps[:, 1024:1028], lt,
                                   hinr[:, k * TH + 1024:k * TH + 1028], start=st, stop=sp)
                    if sp or ko == 7:
                        inc(i1, "T")
                        inc(i2, "T")
                        inc(i3, "T")
                tg_ms[gi] = cnt["T"]
                if kh == 1:
                    t_ms[j] = tg_ms[gi]

        if j == 0:
            @block.scalar
            def _(s):
                s.wait_ge(sem["T"], t_ms[0])
                sf(s, s.activation(out=tmp_fm[0:64, :], in_=psA[0:64, 3:1027], func=AF.Abs,
                                   bias=sb["dtb"][:], scale=1.0))
                inc(s.activation(out=acum_fm[0:64, :], in_=tmp_fm[0:64, :],
                                 func=AF.Exp, scale=sb["kc"][0:64, 3:4]), "S")

            @block.vector
            def _(v):
                v.wait_ge(sem["S"], cnt["S"])
                ins = v.tensor_scalar(out=dt_fm[0:64, :], in0=psA[0:64, 3:1027],
                                      scalar1=sb["dtb"][:], scalar2=0.0,
                                      op0=ALU.add, op1=ALU.max)
                v_free[0] = inc(ins, "V")

            @block.scalar
            def _(s):
                s.wait_ge(sem["V"], cnt["V"])
                inc(s.activation(out=tmp_fm[0:64, :], in_=acum_fm[0:64, :],
                                 func=AF.Ln, bias=sb["kc"][0:64, 2:3]), "S")

            @block.vector
            def _(v):
                v.wait_ge(sem["S"], cnt["S"])
                v.tensor_add(out=dt_fm[0:64, :], in0=tmp_fm[0:64, :], in1=dt_fm[0:64, :])
                v.tensor_scalar_mul(out=tmp2_fm[0:64, :], in0=dt_fm[0:64, :],
                                    scalar1=sb["negA"][:])
                ins = None
                for c in range(CH):
                    ins = v.tensor_tensor_scan(out=acum_fm[0:64, c * 128:c * 128 + 128],
                                               data0=tmp2_fm[0:64, c * 128:c * 128 + 128],
                                               data1=tmp2_fm[0:64, c * 128:c * 128 + 128],
                                               initial=0.0, op0=ALU.add, op1=ALU.bypass)
                inc(ins, "V")
            v_acum = cnt["V"]

            @block.sync
            def _(sy):
                sy.wait_ge(sem["V"], v_acum)
                inc(sy.dma_start(out=scr["acum_scr"], in_=acum_fm[0:64, :]), "dW", 16)

            @block.vector
            def _(v):
                v.wait_ge(sem["V"], v_acum)
                ins = None
                for c in range(CH):
                    ins = v.tensor_copy(out=sb["prefx"][:, c:c + 1],
                                        in_=acum_fm[0:64, c * 128 + 127:c * 128 + 128])
                vf(v, ins)
                vf(v, v.tensor_tensor_scan(out=sb["prefx"][:, 8:16], data0=sb["prefx"][:, 0:8],
                                           data1=sb["prefx"][:, 0:8], initial=0.0,
                                           op0=ALU.add, op1=ALU.bypass))
                v.memset(sb["prefx"][:, 16:17], 0.0)
                inc(v.tensor_copy(out=sb["prefx"][:, 17:24], in_=sb["prefx"][:, 8:15]), "V")

            @block.scalar
            def _(s):
                s.wait_ge(sem["V"], cnt["V"])
                inc(s.activation(out=sb["ealst"][:], in_=sb["prefx"][:, 0:8],
                                 func=AF.Exp), "S")

            @block.sync
            def _(sy):
                sy.wait_ge(sem["S"], cnt["S"])
                inc(sy.dma_start(out=scr["eal_scr"][2:3, :]
                                 .rearrange("a (c h) -> (a h) c", h=64),
                                 in_=sb["ealst"][:],
                                 allow_slow_non_contiguous=True), "dW", 16)
            eal_w = cnt["dW"]

            @block.gpsimd
            def _(g):
                g.wait_ge(sem["dW"], eal_w)
                inc(g.dma_start(out=sb["ealrep"][:],
                                in_=scr["eal_scr"][2:3, :].broadcast_to([128, 512])), "dG", 16)

            @block.scalar
            def _(s):
                s.wait_ge(sem["V"], v_acum)
                inc(s.activation(out=tmp_fm[0:64, :], in_=dt_fm[0:64, :], func=AF.Ln), "S")
            emit_tmcols(tmp_fm, sb["lndt_tm"], "S", cnt["S"])
            emit_tmcols(acum_fm, sb["acum_tm"], "V", v_acum)

            @block.scalar
            def _(s):
                s.wait_ge(sem["V"], cnt["V"])
                inc(s.activation(out=tmp_fm[0:64, :], in_=acum_fm[0:64, :], func=AF.Exp), "S")
            emit_tmcols(tmp_fm, sb["ea_tm"], "S", cnt["S"])

            @block.vector
            def _(v):
                v.wait_ge(sem["V"], cnt["V"])
                ins = None
                for c in range(CH):
                    ins = v.tensor_scalar(out=tmp_fm[0:64, c * 128:c * 128 + 128],
                                          in0=acum_fm[0:64, c * 128:c * 128 + 128],
                                          scalar1=acum_fm[0:64, c * 128 + 127:c * 128 + 128],
                                          scalar2=None, op0=ALU.subtract)
                inc(ins, "V")

            @block.scalar
            def _(s):
                s.wait_ge(sem["V"], cnt["V"])
                inc(s.activation(out=tmp2_fm[0:64, :], in_=tmp_fm[0:64, :],
                                 func=AF.Exp, scale=sb["kc"][0:64, 3:4]), "S")

            @block.vector
            def _(v):
                v.wait_ge(sem["S"], cnt["S"])
                inc(v.tensor_mul(out=tmp_fm[0:64, :], in0=tmp2_fm[0:64, :],
                                 in1=dt_fm[0:64, :]), "V")
            emit_tmcols(tmp_fm, sb["sd_tm"], "V", cnt["V"])

            @block.vector
            def _(v):
                v.wait_ge(sem["V"], cnt["V"])
                ins = None
                for c in range(CH):
                    ins = v.tensor_scalar(out=tmp_fm[0:64, c * 128:c * 128 + 128],
                                          in0=acum_fm[0:64, c * 128:c * 128 + 128],
                                          scalar1=sb["prefx"][:, 16 + c:17 + c],
                                          scalar2=None, op0=ALU.add)
                inc(ins, "V")

            @block.scalar
            def _(s):
                s.wait_ge(sem["V"], cnt["V"])
                inc(s.activation(out=tmp2_fm[0:64, :], in_=tmp_fm[0:64, :], func=AF.Exp), "S")
            emit_tmcols(tmp2_fm, sb["ea2_tm"], "S", cnt["S"])

        elif j in (1, 2):
            tgt = sb["B_fm"] if j == 1 else sb["C_fm"]
            cch = 32 if j == 1 else 33

            @block.vector
            def _(v, ps=ps, cch=cch, j=j):
                v.wait_ge(sem["T"], t_ms[j])
                accs = [convacc[:, 0:T], convac2[:, 0:T]]
                v.tensor_scalar_mul(out=accs[0], in0=ps[:, 0:1024],
                                    scalar1=sb["cw"][:, cch * 4:cch * 4 + 1])
                ins = None
                for tap in range(1, 4):
                    ins = v.scalar_tensor_tensor(out=accs[tap % 2], in0=ps[:, tap:tap + 1024],
                                                 scalar=sb["cw"][:, cch * 4 + tap:cch * 4 + tap + 1],
                                                 in1=accs[(tap - 1) % 2], op0=ALU.mult, op1=ALU.add)
                v_free[j] = inc(ins, "V")

            @block.scalar
            def _(s, tgt=tgt, cch=cch):
                s.wait_ge(sem["V"], cnt["V"])
                inc(s.activation(out=tgt[:, 0:T], in_=convac2[:, 0:T], func=AF.Silu,
                                 bias=sb["cb"][:, cch:cch + 1], scale=1.0), "S")
            s_cv = cnt["S"]

            if j == 1:
                for c in range(CH):
                    @block.tensor
                    def _(te, c=c, s_cv=s_cv):
                        te.wait_ge(sem["S"], s_cv)
                        if c > 0:
                            te.wait_ge(sem["V"], cnt["V"])
                        inc(te.transpose(psAb[:, 3840:3968],
                                         sb["B_fm"][:, c * 128:c * 128 + 128],
                                         sb["ident_b"][:]), "T")

                    @block.vector
                    def _(v, c=c):
                        v.wait_ge(sem["T"], cnt["T"])
                        inc(v.tensor_copy(out=sb["B_tm"][:, c * 128:c * 128 + 128],
                                          in_=psAb[:, 3840:3968]), "V")
        else:
            ci = j - 3

            @block.vector
            def _(v, ps=ps, ci=ci, j=j):
                v.wait_ge(sem["T"], t_ms[j])
                accs = [convacc[:, 0:T], convac2[:, 0:T]]
                v.tensor_scalar_mul(out=accs[0], in0=ps[:, 0:1024],
                                    scalar1=sb["cw"][:, ci * 4:ci * 4 + 1])
                ins = None
                for tap in range(1, 4):
                    ins = v.scalar_tensor_tensor(out=accs[tap % 2], in0=ps[:, tap:tap + 1024],
                                                 scalar=sb["cw"][:, ci * 4 + tap:ci * 4 + tap + 1],
                                                 in1=accs[(tap - 1) % 2], op0=ALU.mult, op1=ALU.add)
                inc(ins, "V")

            @block.scalar
            def _(s, ci=ci):
                s.wait_ge(sem["V"], cnt["V"])
                inc(s.activation(out=convo[:, 0:T], in_=convac2[:, 0:T], func=AF.Silu,
                                 bias=sb["cb"][:, ci:ci + 1], scale=1.0), "S")

            @block.tensor
            def _(te, ps=ps):
                te.wait_ge(sem["S"], cnt["S"])
                pb = ps.bitcast(BF16)
                for c in range(CH):
                    inc(te.transpose(pb[:, c * 128:c * 128 + 128],
                                     convo[:, c * 128:c * 128 + 128],
                                     sb["ident_b"][:]), "T")

            @block.vector
            def _(v, ps=ps, ci=ci, j=j):
                v.wait_ge(sem["T"], cnt["T"])
                pb = ps.bitcast(BF16)
                ins = None
                for c in range(CH):
                    ins = v.tensor_copy(out=xs_tm[:, c * DI + ci * 128:c * DI + ci * 128 + 128],
                                        in_=pb[:, c * 128:c * 128 + 128])
                v_free[j] = inc(ins, "V")

    dbg_dump("dt", [(None, dt_fm[0:64, :])], (64, T))
    dbg_dump("acum", [(None, acum_fm[0:64, :])], (64, T))
    dbg_dump("bfm", [(None, sb["B_fm"][:, 0:T])], (128, T), BF16)
    dbg_dump("cfm", [(None, sb["C_fm"][:, 0:T])], (128, T), BF16)
    dbg_dump("xstm", [((c,), xs_tm[:, c * DI:(c + 1) * DI]) for c in range(CH)],
             (CH, 128, DI), BF16)
    if upto == "p1":
        return finish()

    @block.sync
    def _(sy):
        sy.wait_ge(sem["T"], cnt["T"])
        sy.wait_ge(sem["V"], cnt["V"])
        for fc in range(KD):
            inc(sy.dma_start(out=scr["hin_scr"][fc], in_=hin[:, fc * TH:(fc + 1) * TH]),
                "dW", 16)
    hin_spill = cnt["dW"]

    barrier()
    ARA.reset(0)
    Y_tm = ARA.alloc(CH * DI, BF16)
    ARD.reset(0)
    P = ARD.alloc(NH * HP, F32)
    ARC.reset(0)
    bc = [ARC.alloc(T, F32), ARC.alloc(T, F32)]
    Db = [ARC.alloc(128, F32), ARC.alloc(128, F32)]
    Eb = [ARC.alloc(128, F32), ARC.alloc(128, F32)]
    T1 = [ARC.alloc(128, BF16), ARC.alloc(128, BF16)]
    dgD = [ARC.alloc(128, BF16), ARC.alloc(128, BF16)]

    @block.vector
    def _(v):
        v.wait_ge(sem["dW"], hin_spill)
        inc(v.memset(P[:], 0.0), "V")
    p_init = cnt["V"]

    for c in range(CH):
        @block.tensor
        def _(te, c=c):
            te.wait_ge(sem["V"], p_init if c == 0 else cnt["V"])
            inc(te.matmul(psA[:, 0:128], sb["B_fm"][:, c * 128:c * 128 + 128],
                          sb["C_fm"][:, c * 128:c * 128 + 128], start=True, stop=True), "T")

        @block.vector
        def _(v, c=c):
            v.wait_ge(sem["T"], cnt["T"])
            inc(v.tensor_tensor(out=sb["Gt"][:, c * 128:c * 128 + 128],
                                in0=psA[:, 0:128], in1=sb["tri01"][:], op=ALU.mult), "V")
    gt_V = cnt["V"]

    hv = {}
    ht = {}
    t_mm = {}

    for h in range(NH):
        par = h % 2
        ps = psA if par == 0 else psB
        pbf = sb["Pbf"][:, par * 128:par * 128 + 64]

        @block.gpsimd
        def _(g, h=h, par=par):
            if h >= 2:
                g.wait_ge(sem["V"], hv[h - 2])
            if h == 0:
                g.wait_ge(sem["V"], gt_V)
            inc(g.dma_start(out=bc[par][:, 0:T],
                            in_=scr["acum_scr"][h:h + 1, :].broadcast_to([128, T])), "dG", 16)
            g.wait_ge(sem["dG"], cnt["dG"])
        bc_ld = cnt["dG"]

        @block.vector
        def _(v, h=h, par=par):
            if h >= 2:
                v.wait_ge(sem["V"], hv[h - 2])
                v.wait_ge(sem["T"], ht[h - 2])
            inc(v.tensor_scalar_mul(out=dgD[par][:, 0:128], in0=sb["ident_b"][:],
                                    scalar1=sb["Dbc"][:, h:h + 1]), "V")

        @block.scalar
        def _(s, h=h, par=par):
            s.wait_ge(sem["V"], cnt["V"])
            if h >= 2:
                s.wait_ge(sem["T"], ht[h - 2])
            for c in range(CH):
                inc(s.mul(out=sb["xdtd"][:, par * 512 + c * 64:par * 512 + c * 64 + 64],
                          in_=xs_tm[:, c * DI + h * 64:c * DI + h * 64 + 64],
                          mul=sb["sd_tm"][:, c * 64 + h:c * 64 + h + 1]), "S")
        xdtd_done = cnt["S"]

        pbf_prev = 0
        for c in range(CH):
            i = h * CH + c
            p2 = c % 2

            @block.vector
            def _(v, h=h, c=c, par=par, p2=p2, bc_ld=bc_ld, i=i):
                v.wait_ge(sem["dG"], bc_ld)
                if i >= 2 and (i - 2) in t_mm:
                    v.wait_ge(sem["T"], t_mm[i - 2])
                ins = v.tensor_scalar(out=Db[p2][:, 0:128],
                                      in0=bc[par][:, c * 128:c * 128 + 128],
                                      scalar1=sb["acum_tm"][:, c * 64 + h:c * 64 + h + 1],
                                      scalar2=0.0, op0=ALU.subtract, op1=ALU.min)
                inc(ins, "V")

            @block.scalar
            def _(s, h=h, c=c, p2=p2):
                s.wait_ge(sem["V"], cnt["V"])
                inc(s.activation(out=Eb[p2][:, 0:128], in_=Db[p2][:, 0:128], func=AF.Exp,
                                 bias=sb["lndt_tm"][:, c * 64 + h:c * 64 + h + 1], scale=1.0),
                    "S")

            @block.gpsimd
            def _(g, h=h, c=c, p2=p2):
                g.wait_ge(sem["S"], cnt["S"])
                inc(g.tensor_tensor(out=T1[p2][:, 0:128], in0=Eb[p2][:, 0:128],
                                    in1=sb["Gt"][:, c * 128:c * 128 + 128], op=ALU.mult), "G")
            t1_done = cnt["G"]

            @block.tensor
            def _(te, h=h, c=c, par=par, p2=p2, ps=ps, pbf=pbf, t1_done=t1_done,
                  xdtd_done=xdtd_done, i=i, pbf_prev=pbf_prev):
                te.wait_ge(sem["G"], t1_done)
                if c == 0 and h >= 2:
                    te.wait_ge(sem["V"], hv[h - 2])
                xsl = xs_tm[:, c * DI + h * 64:c * DI + h * 64 + 64]
                te.matmul(ps[:, c * 64:c * 64 + 64], T1[p2][:, 0:128], xsl,
                          start=True, stop=False)
                inc(te.matmul(ps[:, c * 64:c * 64 + 64], dgD[par][:, 0:128], xsl,
                              start=False, stop=True), "T")
                if c > 0:
                    te.wait_ge(sem["V"], pbf_prev)
                    inc(te.matmul(ps[:, 512 + c * 64:512 + c * 64 + 64],
                                  sb["C_fm"][:, c * 128:c * 128 + 128], pbf,
                                  start=True, stop=True), "T")
                te.wait_ge(sem["S"], xdtd_done)
                ins = te.matmul(ps[:, 1024 + c * 64:1024 + c * 64 + 64],
                                sb["B_tm"][:, c * 128:c * 128 + 128],
                                sb["xdtd"][:, par * 512 + c * 64:par * 512 + c * 64 + 64],
                                start=True, stop=True)
                t_mm[i] = inc(ins, "T")

            @block.vector
            def _(v, h=h, c=c, ps=ps, i=i, pbf=pbf):
                v.wait_ge(sem["T"], t_mm[i])
                vf(v, v.tensor_scalar_mul(out=sb["ptmp"][:], in0=P[:, h * 64:h * 64 + 64],
                                          scalar1=sb["ealrep"][:, c * 64 + h:c * 64 + h + 1]))
                vf(v, v.tensor_add(out=P[:, h * 64:h * 64 + 64],
                                   in0=ps[:, 1024 + c * 64:1024 + c * 64 + 64],
                                   in1=sb["ptmp"][:]))
                if c < CH - 1:
                    inc(v.tensor_copy(out=pbf, in_=P[:, h * 64:h * 64 + 64]), "V")
            pbf_prev = cnt["V"]

        ht[h] = cnt["T"]

        @block.vector
        def _(v, h=h, ps=ps):
            v.wait_ge(sem["T"], ht[h])
            yv = Y_tm.rearrange("p (c f) -> p c f", c=CH)[:, :, h * 64:h * 64 + 64]
            v.tensor_copy(out=yv, in_=ps[:, 0:512].rearrange("p (c f) -> p c f", c=CH))
            ins = None
            for c in range(1, CH):
                ins = v.scalar_tensor_tensor(out=Y_tm[:, c * DI + h * 64:c * DI + h * 64 + 64],
                                             in0=ps[:, 512 + c * 64:512 + c * 64 + 64],
                                             scalar=sb["ea_tm"][:, c * 64 + h:c * 64 + h + 1],
                                             in1=Y_tm[:, c * DI + h * 64:c * DI + h * 64 + 64],
                                             op0=ALU.mult, op1=ALU.add)
            hv[h] = inc(ins, "V")

    ssd_V = cnt["V"]

    @block.gpsimd
    def _(g):
        g.wait_ge(sem["V"], ssd_V)
        inc(g.dma_start(out=scr["cc4_in"][0, 0:128 * 4096].rearrange("(p f) -> p f", p=128),
                        in_=P[:]), "dG", 16)
        inc(g.dma_start(out=scr["cc4_in"][0, 128 * 4096:128 * 4096 + 64],
                        in_=sb["prefx"][:, 15:16]), "dG", 16)
        g.wait_ge(sem["dG"], cnt["dG"])
        inc(g.collective_compute("AllGather", ALU.bypass,
                                 replica_groups=[list(range(NCORES))],
                                 ins=[scr["cc4_in"]], outs=[scr["cc4_out"]]), "CC")
    cc4_done = cnt["CC"]

    dbg_dump("ytm_loc", [((c,), Y_tm[:, c * DI:(c + 1) * DI]) for c in range(CH)],
             (CH, 128, DI), BF16)
    dbg_dump("pstate", [(None, P[:])], (128, NH * HP))
    if upto == "ssd":
        return finish()

    barrier()
    ARC.reset(0)
    bbuf = ARC.alloc(4096, F32)
    cvb = ARC.alloc(64, F32)
    ARB.reset(0)
    coefb = ARB.alloc(4096, BF16)

    @block.gpsimd
    def _(g):
        g.wait_ge(sem["CC"], cc4_done)
        inc(g.dma_start(out=sb["svals"][:],
                        in_=scr["cc4_out"][:, 128 * 4096:128 * 4096 + 64]), "dG", 16)
        g.wait_ge(sem["dG"], cnt["dG"])

    @block.tensor
    def _(te):
        te.wait_ge(sem["dG"], cnt["dG"])
        inc(te.matmul(psA[0:8, 0:64], sb["selw"][:], sb["svals"][:], start=True, stop=True), "T")

    @block.scalar
    def _(s):
        s.wait_ge(sem["T"], cnt["T"])
        inc(s.activation(out=sb["cvals"][:], in_=psA[0:8, 0:64], func=AF.Exp), "S")

    @block.vector
    def _(v):
        v.wait_ge(sem["S"], cnt["S"])
        inc(v.tensor_scalar_mul(out=sb["svals"][:], in0=sb["cvals"][:],
                                scalar1=sb["loww"][:]), "V")

    @block.sync
    def _(sy):
        sy.wait_ge(sem["V"], cnt["V"])
        inc(sy.dma_start(out=scr["cvals_scr"], in_=sb["svals"][:]), "dW", 16)
    cv_w = cnt["dW"]

    @block.vector
    def _(v):
        v.wait_ge(sem["dW"], cv_w)
        inc(v.memset(P[:], 0.0), "V")

    for jj in range(NCORES):
        @block.sync
        def _(sy, jj=jj):
            sy.wait_ge(sem["V"], cnt["V"])
            inc(sy.dma_start(out=bbuf[:],
                             in_=scr["cc4_out"][jj, 0:128 * 4096]
                             .rearrange("(p f) -> p f", p=128)), "dW", 16)

        @block.gpsimd
        def _(g, jj=jj):
            g.wait_ge(sem["V"], cnt["V"])
            ins = g.dma_start(out=cvb[:, 0:64],
                              in_=scr["cvals_scr"][jj:jj + 1, :].broadcast_to([128, 64]))
            inc(ins, "dG", 16)
            g.wait_ge(sem["dG"], cnt["dG"])

        @block.vector
        def _(v, jj=jj):
            v.wait_ge(sem["dW"], cnt["dW"])
            v.wait_ge(sem["dG"], cnt["dG"])
            ins = None
            for h in range(NH):
                ins = v.scalar_tensor_tensor(out=P[:, h * 64:h * 64 + 64],
                                             in0=bbuf[:, h * 64:h * 64 + 64],
                                             scalar=cvb[:, h:h + 1],
                                             in1=P[:, h * 64:h * 64 + 64],
                                             op0=ALU.mult, op1=ALU.add)
            inc(ins, "V")

    @block.vector
    def _(v):
        v.wait_ge(sem["V"], cnt["V"])
        inc(v.tensor_copy(out=coefb[:], in_=P[:]), "V")
    pin_bf = cnt["V"]
    dbg_dump("pin", [(None, P[:])], (128, NH * HP))
    dbg_dump("cv", [(None, sb["svals"][:])], (8, 64))

    for c in range(CH):
        @block.tensor
        def _(te, c=c):
            te.wait_ge(sem["V"], cnt["V"])
            for nb in range(8):
                ps2 = psA if nb < 4 else psB
                inc(te.matmul(ps2[:, (nb % 4) * 512:(nb % 4) * 512 + 512],
                              sb["C_fm"][:, c * 128:c * 128 + 128],
                              coefb[:, nb * 512:nb * 512 + 512],
                              start=True, stop=True), "T")

        @block.vector
        def _(v, c=c):
            v.wait_ge(sem["T"], cnt["T"])
            ins = None
            for h in range(NH):
                ps2 = psA if h < 32 else psB
                ho = h % 32
                ins = v.scalar_tensor_tensor(out=Y_tm[:, c * DI + h * 64:c * DI + h * 64 + 64],
                                             in0=ps2[:, ho * 64:ho * 64 + 64],
                                             scalar=sb["ea2_tm"][:, c * 64 + h:c * 64 + h + 1],
                                             in1=Y_tm[:, c * DI + h * 64:c * DI + h * 64 + 64],
                                             op0=ALU.mult, op1=ALU.add)
            inc(ins, "V")

    dbg_dump("ytm", [((c,), Y_tm[:, c * DI:(c + 1) * DI]) for c in range(CH)],
             (CH, 128, DI), BF16)
    if upto == "dy":
        return finish()

    barrier()
    ARB.reset(0)
    Y_fm = ARB.alloc(KDI * T, BF16)

    for c in range(CH):
        for grp in range(2):
            ps = psAb if grp == 0 else psBb

            @block.tensor
            def _(te, c=c, grp=grp, ps=ps):
                te.wait_ge(sem["V"], cnt["V"])
                for q2 in range(16):
                    q = grp * 16 + q2
                    inc(te.transpose(ps[:, q2 * 128:q2 * 128 + 128],
                                     Y_tm[:, c * DI + q * 128:c * DI + q * 128 + 128],
                                     sb["ident_b"][:]), "T")

            @block.vector
            def _(v, c=c, grp=grp, ps=ps):
                v.wait_ge(sem["T"], cnt["T"])
                ins = None
                for q2 in range(16):
                    q = grp * 16 + q2
                    ins = v.tensor_copy(out=Y_fm[:, q * T + c * 128:q * T + c * 128 + 128],
                                        in_=ps[:, q2 * 128:q2 * 128 + 128])
                inc(ins, "V")
    ytrans_V = cnt["V"]

    barrier()
    ARA.reset(0)
    hin2 = ARA.alloc(KD * T, BF16)

    @block.sync
    def _(sy):
        sy.wait_ge(sem["V"], ytrans_V)
        for fc in range(KD):
            inc(sy.dma_start(out=hin2[:, fc * T:(fc + 1) * T],
                             in_=scr["hin_scr"][fc][:, 3:TH]), "dW", 16)
    hin_rel = cnt["dW"]

    ARC.reset(0)
    zwb = [[None, None, ARC.alloc(KD * 128, BF16), ARC.alloc(KD * 128, BF16)]]
    siluz = ARC.alloc(T, F32)
    sqb = ARC.alloc(T, F32)

    def z_evac(ncu, reg):
        ps, off = reg

        @block.scalar
        def _(s, ncu=ncu, ps=ps, off=off):
            s.wait_ge(sem["T"], cnt["T"])
            inc(s.activation(out=siluz[:, 0:T], in_=ps[:, off:off + 1024], func=AF.Silu), "S")

        @block.vector
        def _(v, ncu=ncu):
            v.wait_ge(sem["S"], cnt["S"])
            ins = v.tensor_tensor(out=Y_fm[:, ncu * T:(ncu + 1) * T],
                                  in0=siluz[:, 0:T],
                                  in1=Y_fm[:, ncu * T:(ncu + 1) * T], op=ALU.mult)
            m = inc(ins, "V")

        @block.scalar
        def _(s, ncu=ncu):
            s.wait_ge(sem["V"], cnt["V"])
            inc(s.activation(out=sqb[:, 0:T], in_=Y_fm[:, ncu * T:(ncu + 1) * T],
                             func=AF.Square), "S")

        @block.tensor
        def _(te, ncu=ncu):
            te.wait_ge(sem["S"], cnt["S"])
            st = (ncu == 0)
            inc(te.matmul(psB[0:1, 1536:2048], sb["onesc"][:, 0:1], sqb[:, 0:512],
                          start=st, stop=(ncu == KDI - 1)), "T")
            inc(te.matmul(psB[0:1, 1024:1536], sb["onesc"][:, 0:1], sqb[:, 512:1024],
                          start=st, stop=(ncu == KDI - 1)), "T")
        return cnt["V"]

    bgemm(NC=KDI, KTOT=KD, KG=KD, wsrcs=[lambda n, kg: dram["w_z"][n]
                                         .rearrange("k p n -> p k n")],
          wdt=BF16, cast=True,
          rhs_fn=lambda pi, k, cb: hin2[:, k * T + cb * 512:k * T + cb * 512 + 512],
          ncols=2, regions=[(psA, 0), (psA, 1024)], evac_fn=z_evac, bufs=[zwb[0]],
          pre_wait_T=lambda te: te.wait_ge(sem["dW"], hin_rel))
    zstat_T = cnt["T"]

    @block.vector
    def _(v):
        v.wait_ge(sem["T"], zstat_T)
        inc(v.tensor_copy(out=sb["rowbuf"][0:1, 0:512], in_=psB[0:1, 1536:2048]), "V")
        inc(v.tensor_copy(out=sb["rowbuf"][0:1, 512:1024], in_=psB[0:1, 1024:1536]), "V")

    @block.scalar
    def _(s):
        s.wait_ge(sem["V"], cnt["V"])
        inc(s.activation(out=sb["rowbuf"][0:1, 0:T], in_=sb["rowbuf"][0:1, 0:T],
                         func=AF.Sqrt, bias=sb["kc"][0:1, 1:2], scale=sb["kc"][0:1, 4:5]), "S")

    @block.vector
    def _(v):
        v.wait_ge(sem["S"], cnt["S"])
        inc(v.reciprocal(out=sb["rowbuf"][0:1, 0:T], in_=sb["rowbuf"][0:1, 0:T]), "V")

    @block.sync
    def _(sy):
        sy.wait_ge(sem["V"], cnt["V"])
        inc(sy.dma_start(out=scr["row_scr"][0], in_=sb["rowbuf"][0:1, 0:T]), "dW", 16)
    r3_w = cnt["dW"]

    ARC.reset(0)
    rho3bc = ARC.alloc(T, F32)
    mixcol = ARC.alloc(32, F32)

    @block.sync
    def _(sy):
        sy.wait_ge(sem["dW"], r3_w)
        inc(sy.dma_start(out=rho3bc[:], in_=scr["row_scr"][0:1, :].broadcast_to([128, T])),
            "dW", 16)
        inc(sy.dma_start(out=mixcol[:, 0:32],
                         in_=dram["mixw"].rearrange("a (c p) -> (a p) c", p=128),
                         allow_slow_non_contiguous=True), "dW", 16)
        sy.wait_ge(sem["dW"], cnt["dW"])
    r3_b = cnt["dW"]

    @block.vector
    def _(v):
        v.wait_ge(sem["dW"], r3_b)
        ins = None
        for zc in range(KDI):
            ins = v.scalar_tensor_tensor(out=Y_fm[:, zc * T:(zc + 1) * T],
                                         in0=rho3bc[:], scalar=mixcol[:, zc:zc + 1],
                                         in1=Y_fm[:, zc * T:(zc + 1) * T],
                                         op0=ALU.mult, op1=ALU.mult)
        inc(ins, "V")
    ymix_V = cnt["V"]

    dbg_dump("ymix", [((q,), Y_fm[:, q * T:(q + 1) * T]) for q in range(KDI)],
             (KDI, 128, T), BF16)
    if upto == "ymix":
        return finish()

    barrier()
    ARC.reset(0)
    owb = [[None, None, ARC.alloc(KD * 128, BF16), ARC.alloc(KD * 128, BF16)]]
    xnb2 = ARC.alloc(T, BF16)
    h1b = ARC.alloc(T, BF16)
    sqb2 = ARC.alloc(T, F32)

    def op_evac(ncu, reg):
        ps, off = reg

        @block.sync
        def _(sy, ncu=ncu):
            sy.wait_ge(sem["V"], cnt["V"])
            inc(sy.dma_start(out=xnb2[:], in_=scr["xn_scr"][ncu]), "dW", 16)

        @block.vector
        def _(v, ncu=ncu, ps=ps, off=off):
            v.wait_ge(sem["T"], cnt["T"])
            v.wait_ge(sem["dW"], cnt["dW"])
            inc(v.tensor_tensor(out=h1b[:, 0:T], in0=ps[:, off:off + 1024], in1=xnb2[:],
                                op=ALU.add), "V")
        vm = cnt["V"]

        @block.sync
        def _(sy, ncu=ncu, vm=vm):
            sy.wait_ge(sem["V"], vm)
            inc(sy.dma_start(out=scr["h1_scr"][ncu], in_=h1b[:, 0:T]), "dW", 16)

        @block.scalar
        def _(s, ncu=ncu, vm=vm):
            s.wait_ge(sem["V"], vm)
            inc(s.activation(out=sqb2[:, 0:T], in_=h1b[:, 0:T], func=AF.Square), "S")

        @block.tensor
        def _(te, ncu=ncu):
            te.wait_ge(sem["S"], cnt["S"])
            st = (ncu == 0)
            inc(te.matmul(psB[0:1, 1536:2048], sb["onesc"][:, 0:1], sqb2[:, 0:512],
                          start=st, stop=(ncu == KD - 1)), "T")
            inc(te.matmul(psB[0:1, 1024:1536], sb["onesc"][:, 0:1], sqb2[:, 512:1024],
                          start=st, stop=(ncu == KD - 1)), "T")
        return vm

    bgemm(NC=KD, KTOT=KDI, KG=KD, wsrcs=[lambda n, kg: dram["w_out"][n]
                                         .rearrange("k p n -> p k n")
                                         [:, kg * 16:kg * 16 + 16, :]],
          wdt=BF16, cast=True,
          rhs_fn=lambda pi, k, cb: Y_fm[:, k * T + cb * 512:k * T + cb * 512 + 512],
          ncols=2, regions=[(psA, 0), (psA, 1024)], evac_fn=op_evac, bufs=[owb[0]],
          pre_wait_T=lambda te: te.wait_ge(sem["V"], ymix_V))
    h1stat_T = cnt["T"]

    @block.vector
    def _(v):
        v.wait_ge(sem["T"], h1stat_T)
        inc(v.tensor_copy(out=sb["rowbuf"][0:1, 0:512], in_=psB[0:1, 1536:2048]), "V")
        inc(v.tensor_copy(out=sb["rowbuf"][0:1, 512:1024], in_=psB[0:1, 1024:1536]), "V")

    @block.scalar
    def _(s):
        s.wait_ge(sem["V"], cnt["V"])
        inc(s.activation(out=sb["rowbuf"][0:1, 0:T], in_=sb["rowbuf"][0:1, 0:T],
                         func=AF.Sqrt, bias=sb["kc"][0:1, 1:2], scale=sb["kc"][0:1, 5:6]), "S")

    @block.vector
    def _(v):
        v.wait_ge(sem["S"], cnt["S"])
        inc(v.reciprocal(out=sb["rowbuf"][0:1, 0:T], in_=sb["rowbuf"][0:1, 0:T]), "V")

    @block.sync
    def _(sy):
        sy.wait_ge(sem["V"], cnt["V"])
        inc(sy.dma_start(out=scr["row_scr"][1], in_=sb["rowbuf"][0:1, 0:T]), "dW", 16)
    r4_w = cnt["dW"]

    barrier()
    ARA.reset(0)
    h2_fm = ARA.alloc(KD * T, BF16)
    rho4bc = ARC.alloc(T, F32)
    n2col = ARC.alloc(16, F32)

    @block.sync
    def _(sy):
        sy.wait_ge(sem["dW"], r4_w)
        inc(sy.dma_start(out=rho4bc[:], in_=scr["row_scr"][1:2, :].broadcast_to([128, T])),
            "dW", 16)
        inc(sy.dma_start(out=n2col[:, 0:16],
                         in_=dram["norm2"].rearrange("a (c p) -> (a p) c", p=128),
                         allow_slow_non_contiguous=True), "dW", 16)
        sy.wait_ge(sem["dW"], cnt["dW"])
    r4_b = cnt["dW"]

    for fc in range(KD):
        @block.sync
        def _(sy, fc=fc):
            sy.wait_ge(sem["V"], cnt["V"])
            inc(sy.dma_start(out=h1b[:, 0:T], in_=scr["h1_scr"][fc]), "dW", 16)

        @block.vector
        def _(v, fc=fc):
            v.wait_ge(sem["dW"], max(cnt["dW"], r4_b))
            inc(v.scalar_tensor_tensor(out=h2_fm[:, fc * T:(fc + 1) * T],
                                       in0=rho4bc[:], scalar=n2col[:, fc:fc + 1],
                                       in1=h1b[:, 0:T], op0=ALU.mult, op1=ALU.mult), "V")
    h2_V = cnt["V"]

    dbg_dump("h2", [((fc,), h2_fm[:, fc * T:(fc + 1) * T]) for fc in range(KD)],
             (KD, 128, T), BF16)
    if upto == "h2":
        return finish()

    barrier()
    ARC.reset(0)
    gwb = [[None, None, ARC.alloc(8 * 128, BF16), ARC.alloc(8 * 128, BF16)],
           [None, None, ARC.alloc(8 * 128, BF16), ARC.alloc(8 * 128, BF16)]]
    silug = ARC.alloc(512, F32)
    ARB.reset(0)
    mid0 = ARB.alloc(KI * 512, BF16)
    ARD.reset(0)
    M1A = 32
    mid1a = ARA.alloc(M1A * 512, BF16)
    mid1b = ARD.alloc((KI - M1A) * 512, BF16)

    def mid_blk(half, k):
        if half == 0:
            return mid0[:, k * 512:k * 512 + 512]
        if k < M1A:
            return mid1a[:, k * 512:k * 512 + 512]
        return mid1b[:, (k - M1A) * 512:(k - M1A) * 512 + 512]

    def gu_evac(ncu, reg):
        ps, off = reg
        for half in range(2):
            @block.scalar
            def _(s, ncu=ncu, ps=ps, off=off, half=half):
                s.wait_ge(sem["T"], cnt["T"])
                s.wait_ge(sem["V"], cnt["V"])
                inc(s.activation(out=silug[:, 0:512],
                                 in_=ps[:, off + half * 512:off + half * 512 + 512],
                                 func=AF.Silu), "S")

            @block.vector
            def _(v, ncu=ncu, ps=ps, off=off, half=half):
                v.wait_ge(sem["S"], cnt["S"])
                inc(v.tensor_tensor(out=mid_blk(half, ncu), in0=silug[:, 0:512],
                                    in1=ps[:, off + 1024 + half * 512:
                                           off + 1024 + half * 512 + 512],
                                    op=ALU.mult), "V")
        return cnt["V"]

    bgemm(NC=KI, KTOT=KD, KG=8,
          wsrcs=[lambda n, kg: dram["w_gate"][n].rearrange("k p n -> p k n")
                 [:, kg * 8:kg * 8 + 8, :],
                 lambda n, kg: dram["w_up"][n].rearrange("k p n -> p k n")
                 [:, kg * 8:kg * 8 + 8, :]],
          wdt=BF16, cast=True,
          rhs_fn=lambda pi, k, cb: h2_fm[:, k * T + cb * 512:k * T + cb * 512 + 512],
          ncols=2, regions=[(psA, 0), (psB, 0)],
          evac_fn=gu_evac, bufs=gwb,
          pre_wait_T=lambda te: te.wait_ge(sem["V"], h2_V))

    barrier()
    ARC.reset(0)
    dwb = [[None, None, ARC.alloc(11 * 128, BF16), ARC.alloc(11 * 128, BF16)]]
    h1b2 = ARC.alloc(T, BF16)
    sout = ARC.alloc(T, BF16)

    def dn_evac(ncu, reg):
        ps, off = reg

        @block.sync
        def _(sy, ncu=ncu):
            sy.wait_ge(sem["V"], cnt["V"])
            inc(sy.dma_start(out=h1b2[:, 0:T], in_=scr["h1_scr"][ncu]), "dW", 16)

        @block.vector
        def _(v, ncu=ncu, ps=ps, off=off):
            v.wait_ge(sem["T"], cnt["T"])
            v.wait_ge(sem["dW"], cnt["dW"])
            inc(v.tensor_tensor(out=sout[:, 0:T], in0=ps[:, off:off + T],
                                in1=h1b2[:, 0:T], op=ALU.add), "V")
        vm = cnt["V"]

        @block.sync
        def _(sy, ncu=ncu, vm=vm):
            sy.wait_ge(sem["V"], vm)
            inc(sy.dma_start(out=scr["s_scr"][ncu], in_=sout[:, 0:T]), "dW", 16)
        return vm

    bgemm(NC=KD, KTOT=KI, KG=11,
          wsrcs=[lambda n, kg: dram["w_down"][n].rearrange("k p n -> p k n")
                 [:, kg * 11:kg * 11 + 11, :]],
          wdt=BF16, cast=True,
          rhs_fn=lambda pi, k, cb: mid_blk(cb, k),
          ncols=2, regions=[(psA, 0), (psA, 1024), (psB, 0), (psB, 1024)],
          evac_fn=dn_evac, bufs=dwb)

    dbg_dump("s_scr", [((fc,), scr["s_scr"][fc]) for fc in range(KD)],
             (KD, 128, T), BF16)
    if upto == "s":
        return finish()

    @block.sync
    def _(sy):
        sy.wait_ge(sem["V"], cnt["V"])
        inc(sy.dma_start(out=scr["row_scr"][2].rearrange("(j p) -> p j", p=128),
                         in_=sb["iqs_cols"][:],
                         allow_slow_non_contiguous=True), "dW", 16)
    iqs_w = cnt["dW"]

    dbgcap = None
    if "m2cap" in dbgset:
        dbgcap = nc.dram_tensor("dbg_m2cap", [KI, 128, 512], BF16,
                                kind="ExternalOutput").ap()

    for half in range(2):
        barrier()
        ARC.reset(0)
        iqsbc = ARC.alloc(512, F32)
        q2bc = ARC.alloc(512, F32)
        w3cbc = ARC.alloc(512, F32)
        combo1 = ARC.alloc(512, F32)
        combo2 = ARC.alloc(512, F32)
        sldb = ARC.alloc(512, BF16)
        ARC_M = ARC.cur
        fwb = [[ARC.alloc(8 * 128, BF16), ARC.alloc(8 * 128, BF16)],
               [ARC.alloc(8 * 128, BF16), ARC.alloc(8 * 128, BF16)]]

        ARA.reset(0)
        mid2 = ARA.alloc(KI * 512, BF16)
        q2c = mid2
        macc = ARA.alloc(512, F32)
        mtmp = ARA.alloc(512, F32)
        gsc = ARA.alloc(512, F32)
        usc = ARA.alloc(512, F32)
        sqb3 = ARA.alloc(512, F32)

        ARD.reset(0)
        xqh = ARD.alloc(KD * 512, BF16)

        @block.sync
        def _(sy, half=half):
            sy.wait_ge(sem["dW"], iqs_w)
            sy.wait_ge(sem["V"], cnt["V"])
            inc(sy.dma_start(out=iqsbc[:],
                             in_=scr["row_scr"][2:3, half * 512:half * 512 + 512]
                             .broadcast_to([128, 512])), "dW", 16)
            for fc in range(KD):
                inc(sy.dma_start(out=xqh[:, fc * 512:fc * 512 + 512],
                                 in_=scr["xq_scr"][fc][:, half * 512:half * 512 + 512]),
                    "dW", 16)
            sy.wait_ge(sem["dW"], cnt["dW"])
        xq_ld = cnt["dW"]

        @block.vector
        def _(v):
            v.wait_ge(sem["dW"], xq_ld)
            v.memset(macc[:, 0:512], 0.0)
            v.tensor_scalar_mul(out=combo1[:, 0:512], in0=iqsbc[:, 0:512],
                                scalar1=sb["wsci_col"][:, 0:1])
            inc(v.tensor_scalar_mul(out=combo2[:, 0:512], in0=iqsbc[:, 0:512],
                                    scalar1=sb["wsci_col"][:, 1:2]), "V")
        macc_V = cnt["V"]

        def f12_evac(ncu, reg, half=half, mid2=mid2, macc=macc, iqsbc=iqsbc,
                     gsc=gsc, usc=usc, sqb3=sqb3):
            ps, off = reg

            @block.vector
            def _(v, ncu=ncu, ps=ps, off=off):
                v.wait_ge(sem["T"], cnt["T"])
                inc(v.tensor_tensor(out=gsc[:, 0:512], in0=ps[:, off:off + 512],
                                    in1=combo1[:], op=ALU.mult), "V")
                inc(v.tensor_tensor(out=usc[:, 0:512], in0=ps[:, off + 512:off + 1024],
                                    in1=combo2[:], op=ALU.mult), "V")

            @block.scalar
            def _(s, ncu=ncu):
                s.wait_ge(sem["V"], cnt["V"])
                inc(s.activation(out=sqb3[:, 0:512], in_=gsc[:, 0:512], func=AF.Silu), "S")

            @block.vector
            def _(v, ncu=ncu):
                v.wait_ge(sem["S"], cnt["S"])
                vf(v, v.tensor_tensor(out=mtmp[:, 0:512], in0=sqb3[:, 0:512],
                                      in1=usc[:, 0:512], op=ALU.mult))
                inc(v.tensor_copy(out=mid2[:, ncu * 512:ncu * 512 + 512],
                                  in_=mtmp[:, 0:512]), "V")

            @block.scalar
            def _(s, ncu=ncu):
                s.wait_ge(sem["V"], cnt["V"])
                inc(s.activation(out=sqb3[:, 0:512], in_=mtmp[:, 0:512],
                                 func=AF.Square), "S")

            @block.vector
            def _(v, ncu=ncu):
                v.wait_ge(sem["S"], cnt["S"])
                inc(v.tensor_tensor(out=macc[:, 0:512], in0=sqb3[:, 0:512],
                                    in1=macc[:, 0:512], op=ALU.max), "V")

            @block.tensor
            def _(te, ncu=ncu):
                te.wait_ge(sem["S"], cnt["S"])
                inc(te.matmul(psB[0:1, 1536:2048], sb["onesc"][:, 0:1], sqb3[:, 0:512],
                              start=(ncu == 0), stop=(ncu == KI - 1)), "T")

            if half == 0 and "m2cap" in dbgset:
                @block.gpsimd
                def _(g, ncu=ncu):
                    g.wait_ge(sem["V"], cnt["V"])
                    inc(g.dma_start(out=dbgcap[ncu],
                                    in_=mid2[:, ncu * 512:ncu * 512 + 512]), "dG", 16)
            return cnt["V"]

        bgemm(NC=KI, KTOT=KD, KG=8,
              wsrcs=[lambda n, kg: wq_group(1, n, kg, 8),
                     lambda n, kg: wq_group(2, n, kg, 8)],
              wdt=BF16, cast=False,
              rhs_fn=lambda pi, k, cb, xqh=xqh: xqh[:, k * 512:k * 512 + 512],
              ncols=1, regions=[(psA, 0), (psA, 1024)],
              evac_fn=f12_evac, bufs=fwb,
              pre_wait_T=(lambda te: (te.wait_ge(sem["CC"], cc3_done),
                                      te.wait_ge(sem["V"], macc_V))[0]))
        f12_T = cnt["T"]
        if upto == "f12":
            dbg_dump("mid2", [((q,), mid2[:, q * 512:(q + 1) * 512]) for q in range(KI)],
                     (KI, 128, 512), BF16)
            return finish()

        @block.vector
        def _(v):
            v.wait_ge(sem["T"], f12_T)
            v.wait_ge(sem["V"], cnt["V"])
            inc(v.tensor_copy(out=sb["rowbuf"][0:1, 0:512], in_=psB[0:1, 1536:2048]), "V")

        @block.sync
        def _(sy):
            sy.wait_ge(sem["V"], cnt["V"])
            inc(sy.dma_start(out=scr["row_scr"][5, 0:512], in_=sb["rowbuf"][0:1, 0:512]),
                "dW", 16)
        ss_w = cnt["dW"]

        @block.sync
        def _(sy):
            sy.wait_ge(sem["dW"], ss_w)
            inc(sy.dma_start(out=sb["dump"][:, 8:12],
                             in_=scr["row_scr"][5, 0:512].rearrange("(q p) -> p q", p=128),
                             allow_slow_non_contiguous=True), "dW", 16)
        ss_r = cnt["dW"]

        @block.tensor
        def _(te):
            te.wait_ge(sem["V"], cnt["V"])
            for q in range(4):
                inc(te.transpose(psA[:, q * 128:q * 128 + 128],
                                 macc[:, q * 128:q * 128 + 128], sb["ident_f"][:]), "T")

        @block.vector
        def _(v):
            v.wait_ge(sem["T"], cnt["T"])
            v.wait_ge(sem["dW"], ss_r)
            D0 = sb["dump"]
            for q in range(4):
                vf(v, v.reduce_max(out=D0[:, 4 + q:5 + q], in_=psA[:, q * 128:q * 128 + 128],
                                   axis=AX))
            inc(v.tensor_scalar(out=D0[:, 0:4], in0=D0[:, 8:12], scalar1=1.0 / INTER,
                                scalar2=None, op0=ALU.mult), "V")

        @block.scalar
        def _(s):
            s.wait_ge(sem["V"], cnt["V"])
            sf(s, s.activation(out=sb["dump"][:, 12:16], in_=sb["dump"][:, 4:8],
                               func=AF.Sqrt))
            inc(s.activation(out=sb["dump"][:, 4:8], in_=sb["dump"][:, 0:4], func=AF.Sqrt,
                             bias=sb["kc"][:, 1:2]), "S")

        @block.vector
        def _(v):
            v.wait_ge(sem["S"], cnt["S"])
            D0 = sb["dump"]
            for q in range(4):
                vf(v, v.reciprocal(out=D0[:, 0 + q:1 + q], in_=D0[:, 4 + q:5 + q]))
            vf(v, v.tensor_mul(out=D0[:, 4:8], in0=D0[:, 0:4], in1=D0[:, 12:16]))
            vf(v, v.tensor_scalar_max(out=D0[:, 8:12], in0=D0[:, 4:8], scalar1=EPS_FFN))
            for q in range(4):
                vf(v, v.reciprocal(out=D0[:, 4 + q:5 + q], in_=D0[:, 8 + q:9 + q]))
            vf(v, v.tensor_scalar_mul(out=D0[:, 12:16], in0=D0[:, 4:8], scalar1=127.0))
            vf(v, v.tensor_mul(out=D0[:, 4:8], in0=D0[:, 0:4], in1=D0[:, 12:16]))
            vf(v, v.tensor_scalar_mul(out=D0[:, 0:4], in0=D0[:, 8:12], scalar1=1.0 / 127.0))
            inc(v.tensor_scalar_mul(out=D0[:, 8:12], in0=D0[:, 0:4],
                                    scalar1=sb["wsci_col"][:, 2:3]), "V")

        @block.sync
        def _(sy):
            sy.wait_ge(sem["V"], cnt["V"])
            inc(sy.dma_start(out=scr["row_scr"][3, 0:512].rearrange("(q p) -> p q", p=128),
                             in_=sb["dump"][:, 4:8], allow_slow_non_contiguous=True),
                "dW", 16)
            inc(sy.dma_start(out=scr["row_scr"][4, 0:512].rearrange("(q p) -> p q", p=128),
                             in_=sb["dump"][:, 8:12], allow_slow_non_contiguous=True),
                "dW", 16)
        q2r_w = cnt["dW"]

        @block.sync
        def _(sy):
            sy.wait_ge(sem["dW"], q2r_w)
            inc(sy.dma_start(out=q2bc[:], in_=scr["row_scr"][3:4, 0:512]
                             .broadcast_to([128, 512])), "dW", 16)
            inc(sy.dma_start(out=w3cbc[:], in_=scr["row_scr"][4:5, 0:512]
                             .broadcast_to([128, 512])), "dW", 16)
        q2b_b = cnt["dW"]

        for ncu in range(KI):
            @block.vector
            def _(v, ncu=ncu):
                v.wait_ge(sem["dW"], q2b_b)
                sl = mid2[:, ncu * 512:ncu * 512 + 512]
                vf(v, v.tensor_tensor(out=gsc[:, 0:512], in0=sl, in1=q2bc[:], op=ALU.mult))
                vf(v, v.tensor_copy(out=usc[:, 0:512].bitcast(I32), in_=gsc[:, 0:512]))
                vf(v, v.tensor_copy(out=gsc[:, 0:512], in_=usc[:, 0:512].bitcast(I32)))
                inc(v.tensor_scalar(out=q2c[:, ncu * 512:ncu * 512 + 512], in0=gsc[:, 0:512],
                                    scalar1=-128.0, scalar2=127.0, op0=ALU.max, op1=ALU.min),
                    "V")
        q2_V = cnt["V"]
        if upto == "q2":
            dbg_dump("d0", [(None, sb["dump"][:, 0:16])], (128, 16))
            dbg_dump("rs5", [(None, scr["row_scr"][5:6, 0:512])], (1, 512))
            dbg_dump("rs3", [(None, scr["row_scr"][3:4, 0:512])], (1, 512))
            dbg_dump("q2bcd", [(None, q2bc[:, 0:512])], (128, 512))
            dbg_dump("gscd", [(None, gsc[:, 0:512])], (128, 512))
            dbg_dump("uscd", [(None, usc[:, 0:512])], (128, 512))
            dbg_dump("mid2last", [(None, mid2[:, 43 * 512:44 * 512])], (128, 512), BF16)
            dbg_dump("q2c", [((q,), q2c[:, q * 512:(q + 1) * 512]) for q in range(KI)],
                     (KI, 128, 512), BF16)
            return finish()

        barrier()
        ARB.reset(0)
        gpark = ARB.alloc(KD * 512, F32)
        sgb = ARB.alloc(KD * 512, F32)
        ARC.reset(ARC_M)
        w3b = [[ARC.alloc(11 * 128, BF16), ARC.alloc(11 * 128, BF16)]]
        w3tmp = ARC.alloc(512, F32)

        def w3_evac(ncu, reg, half=half, gpark=gpark):
            ps, off = reg

            @block.vector
            def _(v, ncu=ncu, ps=ps, off=off):
                v.wait_ge(sem["T"], cnt["T"])
                v.wait_ge(sem["S"], cnt["S"])
                inc(v.tensor_tensor(out=w3tmp[:, 0:512],
                                    in0=ps[:, off:off + 512], in1=w3cbc[:], op=ALU.mult), "V")

            @block.scalar
            def _(s, ncu=ncu):
                s.wait_ge(sem["V"], cnt["V"])
                inc(s.activation(out=gpark[:, ncu * 512:ncu * 512 + 512],
                                 in_=w3tmp[:, 0:512], func=AF.Silu), "S")
            return cnt["V"]

        bgemm(NC=KD, KTOT=KI, KG=11,
              wsrcs=[lambda n, kg: wq_group(3, n, kg, 11)],
              wdt=BF16, cast=False,
              rhs_fn=lambda pi, k, cb, q2c=q2c: q2c[:, k * 512:k * 512 + 512],
              ncols=1, regions=[(psA, 0), (psA, 512), (psA, 1024), (psA, 1536)],
              evac_fn=w3_evac, bufs=w3b,
              pre_wait_T=lambda te, q2_V=q2_V: te.wait_ge(sem["V"], q2_V))
        w3_S = cnt["S"]

        if upto == "w3":
            dbg_dump("gpark", [((q,), gpark[:, q * 512:(q + 1) * 512]) for q in range(KD)],
                     (KD, 128, 512))
            return finish()

        for fc in range(KD):
            @block.sync
            def _(sy, fc=fc, half=half):
                sy.wait_ge(sem["V"], cnt["V"])
                inc(sy.dma_start(out=sldb[:, 0:512],
                                 in_=scr["s_scr"][fc][:, half * 512:half * 512 + 512]),
                    "dW", 16)

            @block.vector
            def _(v, fc=fc, w3_S=w3_S):
                v.wait_ge(sem["dW"], cnt["dW"])
                v.wait_ge(sem["S"], w3_S)
                inc(v.tensor_tensor(out=sgb[:, fc * 512:fc * 512 + 512],
                                    in0=gpark[:, fc * 512:fc * 512 + 512],
                                    in1=sldb[:, 0:512], op=ALU.mult), "V")
        sg_V = cnt["V"]

        ARC.reset(ARC_M)
        xld = ARC.alloc(2048, F32)
        for tb in range(4):
            trow = half * 512 + tb * 128

            @block.tensor
            def _(te, tb=tb, sg_V=sg_V):
                te.wait_ge(sem["V"], cnt["V"])
                te.wait_ge(sem["V"], sg_V)
                for fc in range(KD):
                    inc(te.transpose(psA[:, fc * 128:fc * 128 + 128],
                                     sgb[:, fc * 512 + tb * 128:fc * 512 + tb * 128 + 128],
                                     sb["ident_f"][:]), "T")
            t_tp = cnt["T"]

            @block.sync
            def _(sy, trow=trow):
                sy.wait_ge(sem["V"], cnt["V"])
                sy.wait_ge(sem["dW"], cnt["dW"])
                inc(sy.dma_start(out=xld[:, 0:2048],
                                 in_=dram["x_slice"][3 + trow:3 + trow + 128, :]), "dW", 16)

            @block.vector
            def _(v, t_tp=t_tp):
                v.wait_ge(sem["T"], t_tp)
                v.wait_ge(sem["dW"], cnt["dW"])
                inc(v.tensor_tensor(out=xld[:, 0:2048], in0=psA[:, 0:2048],
                                    in1=xld[:, 0:2048], op=ALU.add), "V")
            v_y = cnt["V"]

            @block.sync
            def _(sy, trow=trow, v_y=v_y):
                sy.wait_ge(sem["V"], v_y)
                inc(sy.dma_start(out=y_out[trow:trow + 128, :], in_=xld[:, 0:2048]), "dW", 16)

        if half == 0 and "cap" in dbgset:
            dbg_dump("cap_d0", [(None, sb["dump"][:, 0:16])], (128, 16))
            dbg_dump("cap_q2bc", [(None, q2bc[:, 0:512])], (128, 512))
            dbg_dump("cap_w3cbc", [(None, w3cbc[:, 0:512])], (128, 512))
            dbg_dump("cap_iqsbc", [(None, iqsbc[:, 0:512])], (128, 512))
            dbg_dump("cap_gpark", [(None, gpark[:, 0:512])], (128, 512))
            dbg_dump("cap_sgb", [(None, sgb[:, 0:512])], (128, 512))
            dbg_dump("cap_macc", [(None, macc[:, 0:512])], (128, 512))
            dbg_dump("cap_rowbuf", [(None, sb["rowbuf"][0:1, 0:512])], (1, 512))
            dbg_dump("cap_q2", [(None, mid2[:, 0:512])], (128, 512), BF16)
            dbg_dump("cap_xqh", [(None, xqh[:, 0:512])], (128, 512), BF16)

    return finish()


def kernel(**inputs):
    in_maps = _prep_inputs(inputs)
    nc = build_program()
    res = run_bass_kernel_spmd(nc, in_maps, core_ids=list(range(NCORES)))
    out = np.concatenate([np.asarray(r["y_out"]) for r in res.results], axis=0)
    return out.reshape(1, SEQ, D).astype(np.float32)


# version 03
# speedup vs baseline: 1.2377x; 1.0179x over previous
import numpy as np
from contextlib import ExitStack

import concourse.bass as bass
import concourse.mybir as mybir
from concourse.bass_utils import run_bass_kernel_spmd

F32 = mybir.dt.float32
F32R = mybir.dt.float32r
BF16 = mybir.dt.bfloat16
I32 = mybir.dt.int32
AF = mybir.ActivationFunctionType
ALU = mybir.AluOpType
AX = mybir.AxisListType.X

NCORES = 8
SEQ = 8192
T = 1024
TH = T + 3
D = 2048
DI = 4096
NH = 64
HP = 64
INTER = 5632
CH = 8
KD = 16
KDI = 32
KI = 44
KI_PAD = 48
EPS_LN = 1e-5
EPS_RMS = 1e-6
EPS_FFN = 1e-5
NCHX = 35
SH12 = 6 * KD * 128 * 128
SH3 = 2 * KI * 128 * 128

_cache = {}


def _prep_inputs(inp):
    import ml_dtypes
    x = np.asarray(inp["L0_attn_hidden_states"]).reshape(SEQ, D).astype(np.float32, copy=False)
    Win = np.asarray(inp["L0_attn_in_proj_w"], dtype=np.float32)
    Wout = np.asarray(inp["L0_attn_out_proj_w"], dtype=np.float32)
    Wg = np.asarray(inp["L0_attn_gate_w"], dtype=np.float32)
    Wu = np.asarray(inp["L0_attn_up_w"], dtype=np.float32)
    Wd = np.asarray(inp["L0_attn_down_w"], dtype=np.float32)
    w1 = np.asarray(inp["L0_ffn_w1"], dtype=np.float32)
    w2 = np.asarray(inp["L0_ffn_w2"], dtype=np.float32)
    w3 = np.asarray(inp["L0_ffn_w3"], dtype=np.float32)

    def btile(w, K, N):
        return np.ascontiguousarray(w.reshape(K // 128, 128, N // 128, 128).transpose(2, 0, 1, 3))

    w_xbcdt_cols = np.concatenate([
        Win[:, 8448:8512], np.zeros((D, 64), np.float32),
        Win[:, 8192:8320], Win[:, 8320:8448], Win[:, 4096:8192]], axis=1)
    w_xbcdt = btile(w_xbcdt_cols, D, NCHX * 128)
    w_z = btile(Win[:, 0:4096], D, DI)
    w_out = btile(Wout, DI, D)
    w_gate = btile(Wg, D, INTER)
    w_up = btile(Wu, D, INTER)
    w_down = btile(Wd, INTER, D)
    w1p = np.concatenate([np.ascontiguousarray(w1.T),
                          np.zeros((D, KI_PAD * 128 - INTER), np.float32)], 1)
    w2p = np.concatenate([np.ascontiguousarray(w2.T),
                          np.zeros((D, KI_PAD * 128 - INTER), np.float32)], 1)
    w1t = btile(w1p, D, KI_PAD * 128)
    w2t = btile(w2p, D, KI_PAD * 128)
    w3t = btile(np.ascontiguousarray(w3.T), INTER, D)

    conv_w = np.asarray(inp["L0_attn_conv_w"], dtype=np.float32)
    conv_b = np.asarray(inp["L0_attn_conv_b"], dtype=np.float32)
    cw = np.ascontiguousarray(conv_w.reshape(34, 128, 4).transpose(1, 0, 2).reshape(128, 136))
    cb = np.ascontiguousarray(conv_b.reshape(34, 128).T)

    consts = dict(
        ident_f=np.eye(128, dtype=np.float32),
        ident_b=np.eye(128, dtype=np.float32).astype(ml_dtypes.bfloat16),
        tri01=np.tril(np.ones((128, 128), np.float32)).T.astype(ml_dtypes.bfloat16),
        cw=cw, cb=cb,
        dtb=np.asarray(inp["L0_attn_dt_bias"], np.float32).reshape(64, 1),
        alog=np.asarray(inp["L0_attn_A_log"], np.float32).reshape(64, 1),
        drow=np.asarray(inp["L0_attn_D_param"], np.float32).reshape(1, 64),
        norm1=np.asarray(inp["L0_attn_norm1_w"], np.float32).reshape(1, D),
        norm2=np.asarray(inp["L0_attn_norm2_w"], np.float32).reshape(1, D),
        mixw=np.asarray(inp["L0_attn_mixer_norm_w"], np.float32).reshape(1, DI),
    )

    in_maps = []
    for k in range(NCORES):
        lo = k * T
        xs_sl = np.zeros((TH, D), np.float32)
        if k == 0:
            xs_sl[3:] = x[0:T]
        else:
            xs_sl[:] = x[lo - 3:lo + T]
        sel = np.zeros((8, 8), np.float32)
        low = np.zeros((8, 1), np.float32)
        for j in range(8):
            if j < k:
                low[j, 0] = 1.0
            for i in range(8):
                if j < i < k:
                    sel[i, j] = 1.0
        in_maps.append(dict(
            x_slice=np.ascontiguousarray(xs_sl),
            w_xbcdt=w_xbcdt, w_z=w_z, w_out=w_out,
            w_gate=w_gate, w_up=w_up, w_down=w_down,
            w1s=np.ascontiguousarray(w1t[6 * k:6 * k + 6]),
            w2s=np.ascontiguousarray(w2t[6 * k:6 * k + 6]),
            w3s=np.ascontiguousarray(w3t[2 * k:2 * k + 2]),
            sel=sel, low=low, **consts))
    return in_maps


class Arena:
    def __init__(self, t, elems, name):
        self.t, self.elems, self.cur, self.name = t, elems, 0, name

    def alloc(self, n_elems, dt):
        nb = n_elems * (2 if dt == BF16 else 4) // 2
        nb = (nb + 15) // 16 * 16
        off = self.cur
        assert off + nb <= self.elems, f"{self.name} overflow {off}+{nb}>{self.elems}"
        self.cur = off + nb
        sl = self.t[:, off:off + nb]
        return sl if dt == BF16 else sl.bitcast(dt)

    def reset(self, to=0):
        self.cur = to


def build_program(dbg=(), upto=None):
    key = (tuple(sorted(dbg)), upto)
    if key in _cache:
        return _cache[key]
    nc = bass.Bass("TRN2", target_bir_lowering=False)
    stack = ExitStack()
    sem, cnt = {}, {}
    for n in ["dW", "dG", "T", "V", "S", "G", "CC"]:
        sem[n] = stack.enter_context(nc.semaphore(n))
        cnt[n] = 0
    dbgset = set(dbg)

    def inc(ins, s, n=1):
        cnt[s] += n
        ins.then_inc(sem[s], n)
        return cnt[s]

    def vf(v, ins):
        val = inc(ins, "V")
        v.wait_ge(sem["V"], val)

    def sf(s, ins):
        val = inc(ins, "S")
        s.wait_ge(sem["S"], val)

    dram = {}
    for name, shape, dt in [
            ("x_slice", (TH, D), F32), ("w_xbcdt", (NCHX, KD, 128, 128), F32),
            ("w_z", (KDI, KD, 128, 128), F32), ("w_out", (KD, KDI, 128, 128), F32),
            ("w_gate", (KI, KD, 128, 128), F32), ("w_up", (KI, KD, 128, 128), F32),
            ("w_down", (KD, KI, 128, 128), F32),
            ("w1s", (6, KD, 128, 128), F32), ("w2s", (6, KD, 128, 128), F32),
            ("w3s", (2, KI, 128, 128), F32),
            ("sel", (8, 8), F32), ("low", (8, 1), F32),
            ("ident_f", (128, 128), F32), ("ident_b", (128, 128), BF16),
            ("tri01", (128, 128), BF16), ("cw", (128, 136), F32), ("cb", (128, 34), F32),
            ("dtb", (64, 1), F32), ("alog", (64, 1), F32), ("drow", (1, 64), F32),
            ("norm1", (1, D), F32), ("norm2", (1, D), F32), ("mixw", (1, DI), F32)]:
        dram[name] = nc.dram_tensor(name, list(shape), dt, kind="ExternalInput").ap()
    y_out = nc.dram_tensor("y_out", [T, D], F32, kind="ExternalOutput").ap()

    scr = {}
    for name, shape, dt in [
            ("xn_scr", (KD, 128, T), BF16), ("xq_scr", (KD, 128, T), BF16),
            ("hin_scr", (KD, 128, TH), BF16), ("h1_scr", (KD, 128, T), BF16),
            ("s_scr", (KD, 128, T), BF16), ("acum_scr", (64, T), F32),
            ("eal_scr", (8, 512), F32), ("row_scr", (8, 1024), F32),
            ("cvals_scr", (8, 64), F32),
            ("cc2_in", (1, 8), F32), ("cc2_out", (8, 8), F32),
            ("cc3_in", (1, 2 * SH12 + SH3), BF16), ("cc3_out", (8, 2 * SH12 + SH3), BF16),
            ("cc4_in", (1, 128 * 4096 + 64), F32), ("cc4_out", (8, 128 * 4096 + 64), F32)]:
        scr[name] = nc.dram_tensor(name, list(shape), dt).ap()
    cc3o = scr["cc3_out"]

    def wq_group(which, ncu, kg, KG):
        if which in (1, 2):
            core, loc = divmod(ncu, 6)
            off = (0 if which == 1 else SH12) + (loc * KD + kg * KG) * 16384
        else:
            core, loc = divmod(ncu, 2)
            off = 2 * SH12 + (loc * KI + kg * KG) * 16384
        n_el = KG * 16384
        return cc3o[core, off:off + n_el].rearrange("(k p f) -> p k f", p=128, k=KG)

    sb = {}
    def sbuf(name, shape, dt):
        sb[name] = stack.enter_context(nc.sbuf_tensor("sb_" + name, list(shape), dt))
        return sb[name]

    for name, shape, dt in [
            ("ident_f", (128, 128), F32), ("ident_b", (128, 128), BF16),
            ("tri01", (128, 128), BF16), ("cw", (128, 136), F32), ("cb", (128, 34), F32),
            ("dtb", (64, 1), F32), ("alog", (64, 1), F32), ("negA", (64, 1), F32),
            ("Dbc", (128, 64), F32), ("cols", (128, 16), F32),
            ("iqs_cols", (128, 8), F32),
            ("wsc", (4, 8), F32), ("wsc_col", (128, 4), F32), ("wsci_col", (128, 4), F32),
            ("lndt_tm", (128, CH * 64), F32), ("acum_tm", (128, CH * 64), F32),
            ("ea_tm", (128, CH * 64), F32), ("ea2_tm", (128, CH * 64), F32),
            ("sd_tm", (128, CH * 64), F32), ("ealrep", (128, CH * 64), F32),
            ("prefx", (64, 24), F32), ("ealst", (64, 8), F32),
            ("B_fm", (128, T), BF16), ("C_fm", (128, T), BF16), ("B_tm", (128, T), BF16),
            ("Gt", (128, CH * 128), BF16),
            ("Pbf", (128, 256), BF16), ("xdtd", (128, 2 * 512), BF16),
            ("ptmp", (128, 64), F32),
            ("selw", (8, 8), F32), ("loww", (8, 1), F32),
            ("svals", (8, 64), F32), ("cvals", (8, 64), F32),
            ("onesc", (128, 8), F32),
            ("rowbuf", (1, 1024), F32),
            ("kc", (128, 8), F32),
            ("dump", (128, 512), F32)]:
        sbuf(name, shape, dt)

    ARA = Arena(sbuf("ARA", (128, 32912), BF16), 32912, "A")
    ARB = Arena(sbuf("ARB", (128, 32800), BF16), 32800, "B")
    ARC = Arena(sbuf("ARC", (128, 11400), BF16), 11400, "C")
    ARD = Arena(sbuf("ARD", (128, 8200), BF16), 8200, "D")

    psA = stack.enter_context(nc.psum_tensor("psA", [128, 2048], F32))
    psB = stack.enter_context(nc.psum_tensor("psB", [128, 2048], F32))
    psAb = psA.bitcast(BF16)
    psBb = psB.bitcast(BF16)

    block = stack.enter_context(nc.Block())

    def barrier():
        vals = dict(cnt)

        def w(eng):
            for s in ["T", "V", "S", "G", "CC", "dW", "dG"]:
                if vals[s]:
                    eng.wait_ge(sem[s], vals[s])

        @block.tensor
        def _(te):
            w(te)

        @block.vector
        def _(v):
            w(v)

        @block.scalar
        def _(s):
            w(s)

        @block.gpsimd
        def _(g):
            w(g)

        @block.sync
        def _(sy):
            w(sy)

    def dbg_dump(tag, srcs, shape, dt=F32):
        if tag not in dbgset:
            return
        o = nc.dram_tensor("dbg_" + tag, list(shape), dt, kind="ExternalOutput").ap()

        @block.sync
        def _(sy):
            for s in ["T", "V", "S", "G", "CC"]:
                if cnt[s]:
                    sy.wait_ge(sem[s], cnt[s])
            for osl, src in srcs:
                inc(sy.dma_start(out=o[osl] if osl is not None else o, in_=src), "dW", 16)

    def finish():
        @block.sync
        def _(sy):
            for s in ["T", "V", "S", "G", "CC", "dG", "dW"]:
                if cnt[s]:
                    sy.wait_ge(sem[s], cnt[s])
        stack.close()
        _cache[key] = nc
        return nc

    def bgemm(NC, KTOT, KG, wsrcs, wdt, cast, rhs_fn, ncols, regions, evac_fn,
              bufs, pre_wait_T=None, skip_n=None):
        NGR = KTOT // KG
        npair = len(wsrcs)
        t_grp = {}
        v_ncu = {}
        width = ncols * 512 * npair
        gidx = 0
        for ncu in range(NC):
            if skip_n and ncu in skip_n:
                continue
            reg = regions[ncu % len(regions)]
            ps, off = reg
            for kg in range(NGR):
                par = gidx % 2
                if cast:
                    for pi in range(npair):
                        @block.gpsimd
                        def _(g, ncu=ncu, kg=kg, par=par, pi=pi, gidx=gidx):
                            if gidx >= 2:
                                g.wait_ge(sem["T"], t_grp[gidx - 2])
                            ins = g.dma_start(out=bufs[pi][2 + par][:], in_=wsrcs[pi](ncu, kg))
                            inc(ins, "dG", 16)
                            g.wait_ge(sem["dG"], cnt["dG"])
                else:
                    for pi in range(npair):
                        @block.sync
                        def _(sy, ncu=ncu, kg=kg, par=par, pi=pi, gidx=gidx):
                            if gidx >= 2:
                                sy.wait_ge(sem["T"], t_grp[gidx - 2])
                            if gidx == 0 and pre_wait_T is not None:
                                sy.wait_ge(sem["CC"], cnt["CC"])
                            ins = sy.dma_start(out=bufs[pi][par][:], in_=wsrcs[pi](ncu, kg))
                            inc(ins, "dW", 16)
                            sy.wait_ge(sem["dW"], cnt["dW"])
                ld = cnt["dW"]
                gld = cnt["dG"]

                @block.tensor
                def _(te, ncu=ncu, kg=kg, par=par, ps=ps, off=off, ld=ld, gld=gld,
                      gidx=gidx):
                    if cast:
                        te.wait_ge(sem["dG"], gld)
                    else:
                        te.wait_ge(sem["dW"], ld)
                    if pre_wait_T is not None and ncu == 0 and kg == 0:
                        pre_wait_T(te)
                    if kg == 0 and len(v_ncu) >= len(regions):
                        prev = ncu - len(regions)
                        while prev not in v_ncu and prev >= 0:
                            prev -= 1
                        if prev in v_ncu:
                            te.wait_ge(sem["V"], v_ncu[prev])
                    for ko in range(KG):
                        k = kg * KG + ko
                        st, sp = (k == 0), (k == KTOT - 1)
                        last = (ko == KG - 1)
                        for pi in range(npair):
                            if cast:
                                lt = bufs[pi][2 + par][:, ko * 128:ko * 128 + 128]
                            elif wdt == F32R:
                                lt = bufs[pi][par].bitcast(F32R)[:, ko * 128:ko * 128 + 128]
                            else:
                                lt = bufs[pi][par][:, ko * 128:ko * 128 + 128]
                            for cb in range(ncols):
                                o2 = off + (pi * ncols + cb) * 512
                                ins = te.matmul(ps[:, o2:o2 + 512], lt, rhs_fn(pi, k, cb),
                                                start=st, stop=sp)
                                if sp or last:
                                    inc(ins, "T")
                    t_grp[gidx] = cnt["T"]
                gidx += 1
            v_ncu[ncu] = evac_fn(ncu, reg)

    @block.sync
    def _(sy):
        for n in ["ident_f", "ident_b", "tri01", "cw", "cb", "dtb", "alog"]:
            inc(sy.dma_start(out=sb[n][:], in_=dram[n]), "dW", 16)
        inc(sy.dma_start(out=sb["selw"][:], in_=dram["sel"]), "dW", 16)
        inc(sy.dma_start(out=sb["loww"][:], in_=dram["low"]), "dW", 16)
        inc(sy.dma_start(out=sb["Dbc"][:], in_=dram["drow"].broadcast_to([128, 64])), "dW", 16)
        sy.wait_ge(sem["dW"], cnt["dW"])
    ph0 = cnt["dW"]

    norm1bc = ARC.alloc(D, F32)
    qt_in = ARC.alloc(2 * 512, F32)
    qt_bf = ARC.alloc(2 * 512, BF16)
    acc_abs = ARC.alloc(8, F32)

    @block.sync
    def _(sy):
        inc(sy.dma_start(out=norm1bc[:], in_=dram["norm1"].broadcast_to([128, D])), "dW", 16)
        sy.wait_ge(sem["dW"], cnt["dW"])

    @block.vector
    def _(v):
        v.wait_ge(sem["dW"], ph0)
        v.memset(sb["cols"][:], 0.0)
        v.memset(acc_abs[:, 0:8], 0.0)
        v.memset(sb["onesc"][:], 1.0)
        KC = sb["kc"]
        v.memset(KC[:, 0:1], EPS_LN)
        v.memset(KC[:, 1:2], EPS_RMS)
        v.memset(KC[:, 2:3], 1.0)
        v.memset(KC[:, 3:4], -1.0)
        v.memset(KC[:, 4:5], 1.0 / DI)
        v.memset(KC[:, 5:6], 1.0 / D)
        v.memset(KC[:, 6:7], 1.0 / INTER)
        inc(v.memset(sb["Pbf"][:], 0.0), "V")

    @block.scalar
    def _(s):
        s.wait_ge(sem["dW"], ph0)
        inc(s.activation(out=sb["dump"][0:64, 0:1], in_=sb["alog"][:], func=AF.Exp), "S")

    @block.vector
    def _(v):
        v.wait_ge(sem["S"], cnt["S"])
        inc(v.tensor_scalar_mul(out=sb["negA"][:], in0=sb["dump"][0:64, 0:1],
                                scalar1=-1.0), "V")

    shard_list = [("w1s", 24, 0), ("w2s", 24, 1), ("w3s", 22, 2)]
    vmi = []
    for name, ntile, ti in shard_list:
        for i in range(ntile):
            gi = len(vmi)

            @block.sync
            def _(sy, name=name, i=i, gi=gi):
                if gi >= 2:
                    sy.wait_ge(sem["V"], vmi[gi - 2])
                ins = sy.dma_start(out=qt_in[:, (gi % 2) * 512:(gi % 2) * 512 + 512],
                                   in_=dram[name].rearrange("a b p f -> p (a b) f")
                                   [:, i * 4:i * 4 + 4, :])
                inc(ins, "dW", 16)
                sy.wait_ge(sem["dW"], cnt["dW"])

            @block.vector
            def _(v, gi=gi, ti=ti):
                v.wait_ge(sem["dW"], cnt["dW"])
                sl = qt_in[:, (gi % 2) * 512:(gi % 2) * 512 + 512]
                vf(v, v.reduce_sum(out=sb["dump"][:, 0:1], in_=sl, axis=AX,
                                   apply_absolute_value=True))
                ins = v.tensor_add(out=acc_abs[:, ti:ti + 1], in0=sb["dump"][:, 0:1],
                                   in1=acc_abs[:, ti:ti + 1])
                vmi.append(inc(ins, "V"))

    @block.tensor
    def _(te):
        te.wait_ge(sem["V"], cnt["V"])
        inc(te.matmul(psA[0:4, 0:1], acc_abs[:, 0:4], sb["onesc"][:, 0:1],
                      start=True, stop=True), "T")

    @block.vector
    def _(v):
        v.wait_ge(sem["T"], cnt["T"])
        inc(v.tensor_copy(out=sb["wsc"][0:4, 0:1], in_=psA[0:4, 0:1]), "V")

    @block.gpsimd
    def _(g):
        g.wait_ge(sem["V"], cnt["V"])
        inc(g.dma_start(out=scr["cc2_in"][:, 0:4], in_=sb["wsc"][0:4, 0:1]), "dG", 16)
        g.wait_ge(sem["dG"], cnt["dG"])
        inc(g.collective_compute("AllGather", ALU.bypass,
                                 replica_groups=[list(range(NCORES))],
                                 ins=[scr["cc2_in"]], outs=[scr["cc2_out"]]), "CC")
        g.wait_ge(sem["CC"], cnt["CC"])
        inc(g.dma_start(out=sb["wsc"][0:4, 0:8],
                        in_=scr["cc2_out"].rearrange("a b -> b a")[0:4, :],
                        allow_slow_non_contiguous=True), "dG", 16)

    @block.vector
    def _(v):
        v.wait_ge(sem["dG"], cnt["dG"])
        vf(v, v.reduce_sum(out=sb["dump"][0:4, 0:1], in_=sb["wsc"][0:4, 0:8], axis=AX))
        vf(v, v.tensor_scalar(out=sb["wsc"][0:3, 1:2], in0=sb["dump"][0:3, 0:1],
                              scalar1=1.0 / (INTER * D), scalar2=EPS_FFN,
                              op0=ALU.mult, op1=ALU.max))
        inc(v.reciprocal(out=sb["wsc"][0:3, 2:3], in_=sb["wsc"][0:3, 1:2]), "V")

    @block.gpsimd
    def _(g):
        g.wait_ge(sem["V"], cnt["V"])
        inc(g.dma_start(out=scr["eal_scr"][0, 0:3], in_=sb["wsc"][0:3, 2:3]), "dG", 16)
        inc(g.dma_start(out=scr["eal_scr"][1, 0:3], in_=sb["wsc"][0:3, 1:2]), "dG", 16)
        g.wait_ge(sem["dG"], cnt["dG"])
        for t in range(3):
            inc(g.dma_start(out=sb["wsc_col"][:, t:t + 1],
                            in_=scr["eal_scr"][0:1, t:t + 1].broadcast_to([128, 1])), "dG", 16)
            inc(g.dma_start(out=sb["wsci_col"][:, t:t + 1],
                            in_=scr["eal_scr"][1:2, t:t + 1].broadcast_to([128, 1])), "dG", 16)
    g_scales = cnt["dG"]

    qoff = [0, SH12, 2 * SH12]
    vmi2 = []
    for name, ntile, ti in shard_list:
        for i in range(ntile):
            gi = len(vmi2)

            @block.sync
            def _(sy, name=name, i=i, gi=gi):
                if gi >= 2:
                    sy.wait_ge(sem["V"], vmi2[gi - 2])
                ins = sy.dma_start(out=qt_in[:, (gi % 2) * 512:(gi % 2) * 512 + 512],
                                   in_=dram[name].rearrange("a b p f -> p (a b) f")
                                   [:, i * 4:i * 4 + 4, :])
                inc(ins, "dW", 16)
                sy.wait_ge(sem["dW"], cnt["dW"])

            @block.vector
            def _(v, gi=gi, ti=ti):
                v.wait_ge(sem["dW"], cnt["dW"])
                if gi == 0:
                    v.wait_ge(sem["dG"], g_scales)
                sl = qt_in[:, (gi % 2) * 512:(gi % 2) * 512 + 512]
                vf(v, v.tensor_scalar_mul(out=sb["dump"][:], in0=sl,
                                          scalar1=sb["wsc_col"][:, ti:ti + 1]))
                vf(v, v.tensor_copy(out=sl.bitcast(I32), in_=sb["dump"][:]))
                vf(v, v.tensor_copy(out=sb["dump"][:], in_=sl.bitcast(I32)))
                ins = v.tensor_scalar(out=qt_bf[:, (gi % 2) * 512:(gi % 2) * 512 + 512],
                                      in0=sb["dump"][:], scalar1=-1.0, scalar2=1.0,
                                      op0=ALU.max, op1=ALU.min)
                vmi2.append(inc(ins, "V"))

            @block.gpsimd
            def _(g, i=i, gi=gi, ti=ti):
                g.wait_ge(sem["V"], vmi2[gi])
                off = qoff[ti] + i * 65536
                ins = g.dma_start(
                    out=scr["cc3_in"][0, off:off + 65536]
                    .rearrange("(g p f) -> p g f", g=4, p=128),
                    in_=qt_bf[:, (gi % 2) * 512:(gi % 2) * 512 + 512])
                inc(ins, "dG", 16)

    @block.gpsimd
    def _(g):
        g.wait_ge(sem["dG"], cnt["dG"])
        inc(g.collective_compute("AllGather", ALU.bypass,
                                 replica_groups=[list(range(NCORES))],
                                 ins=[scr["cc3_in"]], outs=[scr["cc3_out"]]), "CC")
    cc3_done = cnt["CC"]
    dbg_dump("tern", [((t,), cc3o[0, t * 16384:(t + 1) * 16384]
                       .rearrange("(p f) -> p f", p=128)) for t in range(16)],
             (16, 128, 128), BF16)
    dbg_dump("wsc", [((0,), sb["wsc_col"][:, 0:4]), ((1,), sb["wsci_col"][:, 0:4])],
             (2, 128, 4))

    xa_s = [ARD.alloc(D, F32), ARB.alloc(D, F32)]
    xb_s = [ARD.alloc(D, F32), ARB.alloc(D, F32)]
    xqb_s = [ARC.alloc(D, BF16), ARB.alloc(D, BF16)]
    xnb_s = [ARC.alloc(D, BF16), ARB.alloc(D, BF16)]

    hin = ARA.alloc(KD * TH + 8, BF16)
    hinr = hin
    CO = sb["cols"]
    vlast, tlast = {}, {}

    def p0_tile(j, rows, row0):
        colb = 3 + 128 * j if j < 8 else 0
        r = slice(0, rows)
        xa, xb = xa_s[j % 2], xb_s[j % 2]
        xqb, xnb = xqb_s[j % 2], xnb_s[j % 2]

        @block.sync
        def _(sy):
            if j >= 2:
                sy.wait_ge(sem["V"], vlast[j - 2])
                sy.wait_ge(sem["T"], tlast[j - 2])
            inc(sy.dma_start(out=xa[r, :], in_=dram["x_slice"][row0:row0 + rows, :]), "dW", 16)

        @block.vector
        def _(v):
            v.wait_ge(sem["dW"], cnt["dW"])
            vf(v, v.reduce_sum(out=CO[r, 0:1], in_=xa[r, :], axis=AX))
            vf(v, v.tensor_scalar_mul(out=CO[r, 1:2], in0=CO[r, 0:1], scalar1=1.0 / D))
            v.tensor_scalar(out=xb[r, :], in0=xa[r, :], scalar1=CO[r, 1:2],
                            scalar2=None, op0=ALU.subtract)
            v.tensor_mul(out=xa[r, 0:1024], in0=xb[r, 0:1024], in1=xb[r, 0:1024])
            vf(v, v.reduce_sum(out=CO[r, 0:1], in_=xa[r, 0:1024], axis=AX))
            v.tensor_mul(out=xa[r, 0:1024], in0=xb[r, 1024:2048], in1=xb[r, 1024:2048])
            vf(v, v.reduce_sum(out=CO[r, 2:3], in_=xa[r, 0:1024], axis=AX))
            vf(v, v.tensor_add(out=CO[r, 6:7], in0=CO[r, 2:3], in1=CO[r, 0:1]))
            inc(v.tensor_scalar(out=CO[r, 3:4], in0=CO[r, 6:7], scalar1=1.0 / D,
                                scalar2=None, op0=ALU.mult), "V")

        @block.scalar
        def _(s):
            s.wait_ge(sem["V"], cnt["V"])
            inc(s.activation(out=CO[r, 8:9], in_=CO[r, 3:4], func=AF.Sqrt,
                             bias=sb["kc"][r, 0:1]), "S")

        @block.vector
        def _(v):
            v.wait_ge(sem["S"], cnt["S"])
            vf(v, v.reciprocal(out=CO[r, 2:3], in_=CO[r, 8:9]))
            vf(v, v.tensor_mul(out=CO[r, 9:10], in0=CO[r, 2:3], in1=CO[r, 2:3]))
            inc(v.tensor_mul(out=CO[r, 10:11], in0=CO[r, 9:10], in1=CO[r, 3:4]), "V")

        @block.scalar
        def _(s):
            s.wait_ge(sem["V"], cnt["V"])
            inc(s.activation(out=CO[r, 11:12], in_=CO[r, 10:11], func=AF.Sqrt,
                             bias=sb["kc"][r, 1:2]), "S")

        @block.vector
        def _(v):
            v.wait_ge(sem["S"], cnt["S"])
            v.wait_ge(sem["dW"], cnt["dW"])
            vf(v, v.reciprocal(out=CO[r, 4:5], in_=CO[r, 11:12]))
            vf(v, v.tensor_mul(out=CO[r, 5:6], in0=CO[r, 2:3], in1=CO[r, 4:5]))
            if j < 8:
                v.tensor_scalar_mul(out=xnb[r, :], in0=xb[r, :], scalar1=CO[r, 2:3])
                vf(v, v.reduce_max(out=CO[r, 6:7], in_=xb[r, :], axis=AX,
                                   apply_absolute_value=True))
                vf(v, v.tensor_mul(out=CO[r, 12:13], in0=CO[r, 6:7], in1=CO[r, 5:6]))
                vf(v, v.tensor_scalar_max(out=CO[r, 13:14], in0=CO[r, 12:13], scalar1=EPS_FFN))
                vf(v, v.reciprocal(out=CO[r, 14:15], in_=CO[r, 13:14]))
                v.tensor_scalar_mul(out=sb["iqs_cols"][r, j:j + 1], in0=CO[r, 13:14],
                                    scalar1=1.0 / 127.0)
                vf(v, v.tensor_scalar_mul(out=CO[r, 15:16], in0=CO[r, 14:15], scalar1=127.0))
                vf(v, v.tensor_mul(out=CO[r, 7:8], in0=CO[r, 5:6], in1=CO[r, 15:16]))
                for hh in range(4):
                    hsl = slice(hh * 512, hh * 512 + 512)
                    vf(v, v.tensor_scalar_mul(out=xa[r, 0:512], in0=xb[r, hsl],
                                              scalar1=CO[r, 7:8]))
                    vf(v, v.tensor_copy(out=xa[r, 512:1024].bitcast(I32), in_=xa[r, 0:512]))
                    vf(v, v.tensor_copy(out=xa[r, 0:512], in_=xa[r, 512:1024].bitcast(I32)))
                    vf(v, v.tensor_scalar(out=xqb[r, hsl], in0=xa[r, 0:512], scalar1=-128.0,
                                          scalar2=127.0, op0=ALU.max, op1=ALU.min))
            v.tensor_scalar_mul(out=xa[r, :], in0=xb[r, :], scalar1=CO[r, 5:6])
            inc(v.tensor_mul(out=xb[r, :], in0=xa[r, :], in1=norm1bc[r, :]), "V")
        v_done = cnt["V"]

        @block.tensor
        def _(te):
            te.wait_ge(sem["V"], v_done)
            for fc in range(KD):
                inc(te.transpose(psA[:, fc * 128:fc * 128 + rows],
                                 xb[r, fc * 128:fc * 128 + 128],
                                 sb["ident_f"][0:rows, 0:rows]), "T")

        @block.vector
        def _(v):
            v.wait_ge(sem["T"], cnt["T"])
            ins = None
            for fc in range(KD):
                ins = v.tensor_copy(out=hinr[:, fc * TH + colb:fc * TH + colb + rows],
                                    in_=psA[:, fc * 128:fc * 128 + rows])
            inc(ins, "V")

        if j < 8:
            @block.tensor
            def _(te):
                te.wait_ge(sem["V"], cnt["V"])
                for fc in range(KD):
                    inc(te.transpose(psBb[:, fc * 128:fc * 128 + 128],
                                     xnb[:, fc * 128:fc * 128 + 128],
                                     sb["ident_b"][:]), "T")

            @block.vector
            def _(v):
                v.wait_ge(sem["T"], cnt["T"])
                inc(v.tensor_copy(out=xnb[:], in_=psBb[:, 0:2048]), "V")
            v_xn = cnt["V"]

            @block.sync
            def _(sy):
                sy.wait_ge(sem["V"], v_xn)
                for fc in range(KD):
                    inc(sy.dma_start(out=scr["xn_scr"][fc, :, j * 128:j * 128 + 128],
                                     in_=xnb[:, fc * 128:fc * 128 + 128]), "dW", 16)

            @block.tensor
            def _(te):
                te.wait_ge(sem["V"], cnt["V"])
                te.wait_ge(sem["dW"], cnt["dW"])
                for fc in range(KD):
                    inc(te.transpose(psBb[:, fc * 128:fc * 128 + 128],
                                     xqb[:, fc * 128:fc * 128 + 128],
                                     sb["ident_b"][:]), "T")

            @block.vector
            def _(v):
                v.wait_ge(sem["T"], cnt["T"])
                inc(v.tensor_copy(out=xqb[:], in_=psBb[:, 0:2048]), "V")
            v_xq = cnt["V"]

            @block.sync
            def _(sy):
                sy.wait_ge(sem["V"], v_xq)
                for fc in range(KD):
                    inc(sy.dma_start(out=scr["xq_scr"][fc, :, j * 128:j * 128 + 128],
                                     in_=xqb[:, fc * 128:fc * 128 + 128]), "dW", 16)
        vlast[j] = cnt["V"]
        tlast[j] = cnt["T"]

    for j in range(8):
        p0_tile(j, 128, 3 + 128 * j)
    p0_tile(8, 3, 0)
    p0_hin_V = cnt["V"]

    dbg_dump("hin", [((fc,), hin[:, fc * TH:(fc + 1) * TH]) for fc in range(KD)],
             (KD, 128, TH), BF16)
    dbg_dump("xn", [((fc,), scr["xn_scr"][fc]) for fc in range(KD)], (KD, 128, T), BF16)
    dbg_dump("co", [(None, CO[:])], (128, 16))
    dbg_dump("xq", [((fc,), scr["xq_scr"][fc]) for fc in range(KD)], (KD, 128, T), BF16)
    dbg_dump("iqs", [(None, sb["iqs_cols"][:])], (128, 8))
    if upto == "p0":
        return finish()

    barrier()
    ARD.reset(0)
    dt_fm = ARD.alloc(T, F32)
    acum_fm = ARD.alloc(T, F32)
    tmp_fm = ARD.alloc(T, F32)
    tmp2_fm = ARD.alloc(T, F32)

    ARC.reset(0)
    wbuf = [ARC.alloc(8 * 128, BF16), ARC.alloc(8 * 128, BF16)]
    convacc = ARC.alloc(T, F32)
    convac2 = ARC.alloc(T, F32)
    convo = ARC.alloc(T, BF16)

    ARB.reset(0)
    xs_tm = ARB.alloc(CH * DI, BF16)

    t_ms = {}
    v_free = {}

    def emit_tmcols(src_fm, dst, wait_sem, wait_val):
        @block.tensor
        def _(te):
            te.wait_ge(sem[wait_sem], wait_val)
            te.wait_ge(sem["V"], cnt["V"])
            for c in range(CH):
                inc(te.transpose(psB[:, 1536 + c * 64:1536 + c * 64 + 64],
                                 src_fm[0:64, c * 128:c * 128 + 128],
                                 sb["ident_f"][0:64, 0:64]), "T")

        @block.vector
        def _(v):
            v.wait_ge(sem["T"], cnt["T"])
            inc(v.tensor_copy(out=dst[:], in_=psB[:, 1536:2048]), "V")

    tg_ms = {}
    for j in range(NCHX):
        par = j % 2
        ps = psA if par == 0 else psB

        for kh in range(2):
            gi = 2 * j + kh

            @block.gpsimd
            def _(g, j=j, kh=kh, gi=gi):
                if gi >= 2:
                    g.wait_ge(sem["T"], tg_ms[gi - 2])
                ins = g.dma_start(
                    out=wbuf[gi % 2][:],
                    in_=dram["w_xbcdt"][j].rearrange("k p n -> p k n")
                    [:, kh * 8:kh * 8 + 8, :])
                inc(ins, "dG", 16)
                g.wait_ge(sem["dG"], cnt["dG"])
            w_ld = cnt["dG"]

            @block.tensor
            def _(te, j=j, kh=kh, gi=gi, ps=ps, w_ld=w_ld):
                te.wait_ge(sem["dG"], w_ld)
                if gi == 0:
                    te.wait_ge(sem["V"], p0_hin_V)
                if kh == 0 and j >= 2:
                    te.wait_ge(sem["V"], v_free[j - 2])
                for ko in range(8):
                    k = kh * 8 + ko
                    lt = wbuf[gi % 2][:, ko * 128:ko * 128 + 128]
                    st, sp = (k == 0), (k == KD - 1)
                    i1 = te.matmul(ps[:, 0:512], lt, hinr[:, k * TH:k * TH + 512],
                                   start=st, stop=sp)
                    i2 = te.matmul(ps[:, 512:1024], lt, hinr[:, k * TH + 512:k * TH + 1024],
                                   start=st, stop=sp)
                    i3 = te.matmul(ps[:, 1024:1028], lt,
                                   hinr[:, k * TH + 1024:k * TH + 1028], start=st, stop=sp)
                    if sp or ko == 7:
                        inc(i1, "T")
                        inc(i2, "T")
                        inc(i3, "T")
                tg_ms[gi] = cnt["T"]
                if kh == 1:
                    t_ms[j] = tg_ms[gi]

        if j == 0:
            @block.scalar
            def _(s):
                s.wait_ge(sem["T"], t_ms[0])
                sf(s, s.activation(out=tmp_fm[0:64, :], in_=psA[0:64, 3:1027], func=AF.Abs,
                                   bias=sb["dtb"][:], scale=1.0))
                inc(s.activation(out=acum_fm[0:64, :], in_=tmp_fm[0:64, :],
                                 func=AF.Exp, scale=sb["kc"][0:64, 3:4]), "S")

            @block.vector
            def _(v):
                v.wait_ge(sem["S"], cnt["S"])
                ins = v.tensor_scalar(out=dt_fm[0:64, :], in0=psA[0:64, 3:1027],
                                      scalar1=sb["dtb"][:], scalar2=0.0,
                                      op0=ALU.add, op1=ALU.max)
                v_free[0] = inc(ins, "V")

            @block.scalar
            def _(s):
                s.wait_ge(sem["V"], cnt["V"])
                inc(s.activation(out=tmp_fm[0:64, :], in_=acum_fm[0:64, :],
                                 func=AF.Ln, bias=sb["kc"][0:64, 2:3]), "S")

            @block.vector
            def _(v):
                v.wait_ge(sem["S"], cnt["S"])
                v.tensor_add(out=dt_fm[0:64, :], in0=tmp_fm[0:64, :], in1=dt_fm[0:64, :])
                v.tensor_scalar_mul(out=tmp2_fm[0:64, :], in0=dt_fm[0:64, :],
                                    scalar1=sb["negA"][:])
                ins = None
                for c in range(CH):
                    ins = v.tensor_tensor_scan(out=acum_fm[0:64, c * 128:c * 128 + 128],
                                               data0=tmp2_fm[0:64, c * 128:c * 128 + 128],
                                               data1=tmp2_fm[0:64, c * 128:c * 128 + 128],
                                               initial=0.0, op0=ALU.add, op1=ALU.bypass)
                inc(ins, "V")
            v_acum = cnt["V"]

            @block.sync
            def _(sy):
                sy.wait_ge(sem["V"], v_acum)
                inc(sy.dma_start(out=scr["acum_scr"], in_=acum_fm[0:64, :]), "dW", 16)

            @block.vector
            def _(v):
                v.wait_ge(sem["V"], v_acum)
                ins = None
                for c in range(CH):
                    ins = v.tensor_copy(out=sb["prefx"][:, c:c + 1],
                                        in_=acum_fm[0:64, c * 128 + 127:c * 128 + 128])
                vf(v, ins)
                vf(v, v.tensor_tensor_scan(out=sb["prefx"][:, 8:16], data0=sb["prefx"][:, 0:8],
                                           data1=sb["prefx"][:, 0:8], initial=0.0,
                                           op0=ALU.add, op1=ALU.bypass))
                v.memset(sb["prefx"][:, 16:17], 0.0)
                inc(v.tensor_copy(out=sb["prefx"][:, 17:24], in_=sb["prefx"][:, 8:15]), "V")

            @block.scalar
            def _(s):
                s.wait_ge(sem["V"], cnt["V"])
                inc(s.activation(out=sb["ealst"][:], in_=sb["prefx"][:, 0:8],
                                 func=AF.Exp), "S")

            @block.sync
            def _(sy):
                sy.wait_ge(sem["S"], cnt["S"])
                inc(sy.dma_start(out=scr["eal_scr"][2:3, :]
                                 .rearrange("a (c h) -> (a h) c", h=64),
                                 in_=sb["ealst"][:],
                                 allow_slow_non_contiguous=True), "dW", 16)
            eal_w = cnt["dW"]

            @block.gpsimd
            def _(g):
                g.wait_ge(sem["dW"], eal_w)
                inc(g.dma_start(out=sb["ealrep"][:],
                                in_=scr["eal_scr"][2:3, :].broadcast_to([128, 512])), "dG", 16)

            @block.scalar
            def _(s):
                s.wait_ge(sem["V"], v_acum)
                inc(s.activation(out=tmp_fm[0:64, :], in_=dt_fm[0:64, :], func=AF.Ln), "S")
            emit_tmcols(tmp_fm, sb["lndt_tm"], "S", cnt["S"])
            emit_tmcols(acum_fm, sb["acum_tm"], "V", v_acum)

            @block.scalar
            def _(s):
                s.wait_ge(sem["V"], cnt["V"])
                inc(s.activation(out=tmp_fm[0:64, :], in_=acum_fm[0:64, :], func=AF.Exp), "S")
            emit_tmcols(tmp_fm, sb["ea_tm"], "S", cnt["S"])

            @block.vector
            def _(v):
                v.wait_ge(sem["V"], cnt["V"])
                ins = None
                for c in range(CH):
                    ins = v.tensor_scalar(out=tmp_fm[0:64, c * 128:c * 128 + 128],
                                          in0=acum_fm[0:64, c * 128:c * 128 + 128],
                                          scalar1=acum_fm[0:64, c * 128 + 127:c * 128 + 128],
                                          scalar2=None, op0=ALU.subtract)
                inc(ins, "V")

            @block.scalar
            def _(s):
                s.wait_ge(sem["V"], cnt["V"])
                inc(s.activation(out=tmp2_fm[0:64, :], in_=tmp_fm[0:64, :],
                                 func=AF.Exp, scale=sb["kc"][0:64, 3:4]), "S")

            @block.vector
            def _(v):
                v.wait_ge(sem["S"], cnt["S"])
                inc(v.tensor_mul(out=tmp_fm[0:64, :], in0=tmp2_fm[0:64, :],
                                 in1=dt_fm[0:64, :]), "V")
            emit_tmcols(tmp_fm, sb["sd_tm"], "V", cnt["V"])

            @block.vector
            def _(v):
                v.wait_ge(sem["V"], cnt["V"])
                ins = None
                for c in range(CH):
                    ins = v.tensor_scalar(out=tmp_fm[0:64, c * 128:c * 128 + 128],
                                          in0=acum_fm[0:64, c * 128:c * 128 + 128],
                                          scalar1=sb["prefx"][:, 16 + c:17 + c],
                                          scalar2=None, op0=ALU.add)
                inc(ins, "V")

            @block.scalar
            def _(s):
                s.wait_ge(sem["V"], cnt["V"])
                inc(s.activation(out=tmp2_fm[0:64, :], in_=tmp_fm[0:64, :], func=AF.Exp), "S")
            emit_tmcols(tmp2_fm, sb["ea2_tm"], "S", cnt["S"])

        elif j in (1, 2):
            tgt = sb["B_fm"] if j == 1 else sb["C_fm"]
            cch = 32 if j == 1 else 33

            @block.vector
            def _(v, ps=ps, cch=cch, j=j):
                v.wait_ge(sem["T"], t_ms[j])
                accs = [convacc[:, 0:T], convac2[:, 0:T]]
                v.tensor_scalar_mul(out=accs[0], in0=ps[:, 0:1024],
                                    scalar1=sb["cw"][:, cch * 4:cch * 4 + 1])
                ins = None
                for tap in range(1, 4):
                    ins = v.scalar_tensor_tensor(out=accs[tap % 2], in0=ps[:, tap:tap + 1024],
                                                 scalar=sb["cw"][:, cch * 4 + tap:cch * 4 + tap + 1],
                                                 in1=accs[(tap - 1) % 2], op0=ALU.mult, op1=ALU.add)
                v_free[j] = inc(ins, "V")

            @block.scalar
            def _(s, tgt=tgt, cch=cch):
                s.wait_ge(sem["V"], cnt["V"])
                inc(s.activation(out=tgt[:, 0:T], in_=convac2[:, 0:T], func=AF.Silu,
                                 bias=sb["cb"][:, cch:cch + 1], scale=1.0), "S")
            s_cv = cnt["S"]

            if j == 1:
                for c in range(CH):
                    @block.tensor
                    def _(te, c=c, s_cv=s_cv):
                        te.wait_ge(sem["S"], s_cv)
                        if c > 0:
                            te.wait_ge(sem["V"], cnt["V"])
                        inc(te.transpose(psAb[:, 3840:3968],
                                         sb["B_fm"][:, c * 128:c * 128 + 128],
                                         sb["ident_b"][:]), "T")

                    @block.vector
                    def _(v, c=c):
                        v.wait_ge(sem["T"], cnt["T"])
                        inc(v.tensor_copy(out=sb["B_tm"][:, c * 128:c * 128 + 128],
                                          in_=psAb[:, 3840:3968]), "V")
        else:
            ci = j - 3

            @block.vector
            def _(v, ps=ps, ci=ci, j=j):
                v.wait_ge(sem["T"], t_ms[j])
                accs = [convacc[:, 0:T], convac2[:, 0:T]]
                v.tensor_scalar_mul(out=accs[0], in0=ps[:, 0:1024],
                                    scalar1=sb["cw"][:, ci * 4:ci * 4 + 1])
                ins = None
                for tap in range(1, 4):
                    ins = v.scalar_tensor_tensor(out=accs[tap % 2], in0=ps[:, tap:tap + 1024],
                                                 scalar=sb["cw"][:, ci * 4 + tap:ci * 4 + tap + 1],
                                                 in1=accs[(tap - 1) % 2], op0=ALU.mult, op1=ALU.add)
                inc(ins, "V")

            @block.scalar
            def _(s, ci=ci):
                s.wait_ge(sem["V"], cnt["V"])
                inc(s.activation(out=convo[:, 0:T], in_=convac2[:, 0:T], func=AF.Silu,
                                 bias=sb["cb"][:, ci:ci + 1], scale=1.0), "S")

            @block.tensor
            def _(te, ps=ps):
                te.wait_ge(sem["S"], cnt["S"])
                pb = ps.bitcast(BF16)
                for c in range(CH):
                    inc(te.transpose(pb[:, c * 128:c * 128 + 128],
                                     convo[:, c * 128:c * 128 + 128],
                                     sb["ident_b"][:]), "T")

            @block.vector
            def _(v, ps=ps, ci=ci, j=j):
                v.wait_ge(sem["T"], cnt["T"])
                pb = ps.bitcast(BF16)
                ins = None
                for c in range(CH):
                    ins = v.tensor_copy(out=xs_tm[:, c * DI + ci * 128:c * DI + ci * 128 + 128],
                                        in_=pb[:, c * 128:c * 128 + 128])
                v_free[j] = inc(ins, "V")

    dbg_dump("dt", [(None, dt_fm[0:64, :])], (64, T))
    dbg_dump("acum", [(None, acum_fm[0:64, :])], (64, T))
    dbg_dump("bfm", [(None, sb["B_fm"][:, 0:T])], (128, T), BF16)
    dbg_dump("cfm", [(None, sb["C_fm"][:, 0:T])], (128, T), BF16)
    dbg_dump("xstm", [((c,), xs_tm[:, c * DI:(c + 1) * DI]) for c in range(CH)],
             (CH, 128, DI), BF16)
    if upto == "p1":
        return finish()

    @block.sync
    def _(sy):
        sy.wait_ge(sem["T"], cnt["T"])
        sy.wait_ge(sem["V"], cnt["V"])
        for fc in range(KD):
            inc(sy.dma_start(out=scr["hin_scr"][fc], in_=hin[:, fc * TH:(fc + 1) * TH]),
                "dW", 16)
    hin_spill = cnt["dW"]

    barrier()
    ARA.reset(0)
    Y_tm = ARA.alloc(CH * DI, BF16)
    ARD.reset(0)
    P = ARD.alloc(NH * HP, F32)
    ARC.reset(0)
    bc = [ARC.alloc(T, F32), ARC.alloc(T, F32)]
    Db = [ARC.alloc(128, F32), ARC.alloc(128, F32)]
    Eb = [ARC.alloc(128, F32), ARC.alloc(128, F32)]
    T1 = [ARC.alloc(128, BF16), ARC.alloc(128, BF16)]
    dgD = [ARC.alloc(128, BF16), ARC.alloc(128, BF16)]

    @block.vector
    def _(v):
        v.wait_ge(sem["dW"], hin_spill)
        inc(v.memset(P[:], 0.0), "V")
    p_init = cnt["V"]

    for c in range(CH):
        @block.tensor
        def _(te, c=c):
            te.wait_ge(sem["V"], p_init if c == 0 else cnt["V"])
            inc(te.matmul(psA[:, 0:128], sb["B_fm"][:, c * 128:c * 128 + 128],
                          sb["C_fm"][:, c * 128:c * 128 + 128], start=True, stop=True), "T")

        @block.vector
        def _(v, c=c):
            v.wait_ge(sem["T"], cnt["T"])
            inc(v.tensor_tensor(out=sb["Gt"][:, c * 128:c * 128 + 128],
                                in0=psA[:, 0:128], in1=sb["tri01"][:], op=ALU.mult), "V")
    gt_V = cnt["V"]

    hv = {}
    ht = {}
    t_mm = {}

    for h in range(NH):
        par = h % 2
        ps = psA if par == 0 else psB
        pbf = sb["Pbf"][:, par * 128:par * 128 + 64]

        @block.gpsimd
        def _(g, h=h, par=par):
            if h >= 2:
                g.wait_ge(sem["V"], hv[h - 2])
            if h == 0:
                g.wait_ge(sem["V"], gt_V)
            inc(g.dma_start(out=bc[par][:, 0:T],
                            in_=scr["acum_scr"][h:h + 1, :].broadcast_to([128, T])), "dG", 16)
            g.wait_ge(sem["dG"], cnt["dG"])
        bc_ld = cnt["dG"]

        @block.vector
        def _(v, h=h, par=par):
            if h >= 2:
                v.wait_ge(sem["V"], hv[h - 2])
                v.wait_ge(sem["T"], ht[h - 2])
            inc(v.tensor_scalar_mul(out=dgD[par][:, 0:128], in0=sb["ident_b"][:],
                                    scalar1=sb["Dbc"][:, h:h + 1]), "V")

        @block.vector
        def _(v, h=h, par=par):
            if h >= 2:
                v.wait_ge(sem["T"], ht[h - 2])
            for c in range(CH):
                inc(v.tensor_scalar_mul(
                    out=sb["xdtd"][:, par * 512 + c * 64:par * 512 + c * 64 + 64],
                    in0=xs_tm[:, c * DI + h * 64:c * DI + h * 64 + 64],
                    scalar1=sb["sd_tm"][:, c * 64 + h:c * 64 + h + 1]), "V")
        xdtd_done = cnt["V"]

        pbf_prev = 0
        for c in range(CH):
            i = h * CH + c
            p2 = c % 2

            @block.vector
            def _(v, h=h, c=c, par=par, p2=p2, bc_ld=bc_ld, i=i):
                v.wait_ge(sem["dG"], bc_ld)
                if i >= 2 and (i - 2) in t_mm:
                    v.wait_ge(sem["T"], t_mm[i - 2])
                ins = v.tensor_scalar(out=Db[p2][:, 0:128],
                                      in0=bc[par][:, c * 128:c * 128 + 128],
                                      scalar1=sb["acum_tm"][:, c * 64 + h:c * 64 + h + 1],
                                      scalar2=0.0, op0=ALU.subtract, op1=ALU.min)
                inc(ins, "V")

            @block.scalar
            def _(s, h=h, c=c, p2=p2):
                s.wait_ge(sem["V"], cnt["V"])
                inc(s.activation(out=Eb[p2][:, 0:128], in_=Db[p2][:, 0:128], func=AF.Exp,
                                 bias=sb["lndt_tm"][:, c * 64 + h:c * 64 + h + 1], scale=1.0),
                    "S")

            @block.gpsimd
            def _(g, h=h, c=c, p2=p2):
                g.wait_ge(sem["S"], cnt["S"])
                inc(g.tensor_tensor(out=T1[p2][:, 0:128], in0=Eb[p2][:, 0:128],
                                    in1=sb["Gt"][:, c * 128:c * 128 + 128], op=ALU.mult), "G")
            t1_done = cnt["G"]

            @block.tensor
            def _(te, h=h, c=c, par=par, p2=p2, ps=ps, pbf=pbf, t1_done=t1_done,
                  xdtd_done=xdtd_done, i=i, pbf_prev=pbf_prev):
                te.wait_ge(sem["G"], t1_done)
                if c == 0 and h >= 2:
                    te.wait_ge(sem["V"], hv[h - 2])
                xsl = xs_tm[:, c * DI + h * 64:c * DI + h * 64 + 64]
                te.matmul(ps[:, c * 64:c * 64 + 64], T1[p2][:, 0:128], xsl,
                          start=True, stop=False)
                inc(te.matmul(ps[:, c * 64:c * 64 + 64], dgD[par][:, 0:128], xsl,
                              start=False, stop=True), "T")
                if c > 0:
                    te.wait_ge(sem["V"], pbf_prev)
                    inc(te.matmul(ps[:, 512 + c * 64:512 + c * 64 + 64],
                                  sb["C_fm"][:, c * 128:c * 128 + 128], pbf,
                                  start=True, stop=True), "T")
                te.wait_ge(sem["V"], xdtd_done)
                ins = te.matmul(ps[:, 1024 + c * 64:1024 + c * 64 + 64],
                                sb["B_tm"][:, c * 128:c * 128 + 128],
                                sb["xdtd"][:, par * 512 + c * 64:par * 512 + c * 64 + 64],
                                start=True, stop=True)
                t_mm[i] = inc(ins, "T")

            @block.vector
            def _(v, h=h, c=c, ps=ps, i=i, pbf=pbf):
                v.wait_ge(sem["T"], t_mm[i])
                vf(v, v.tensor_scalar_mul(out=sb["ptmp"][:], in0=P[:, h * 64:h * 64 + 64],
                                          scalar1=sb["ealrep"][:, c * 64 + h:c * 64 + h + 1]))
                vf(v, v.tensor_add(out=P[:, h * 64:h * 64 + 64],
                                   in0=ps[:, 1024 + c * 64:1024 + c * 64 + 64],
                                   in1=sb["ptmp"][:]))
                if c < CH - 1:
                    inc(v.tensor_copy(out=pbf, in_=P[:, h * 64:h * 64 + 64]), "V")
            pbf_prev = cnt["V"]

        ht[h] = cnt["T"]

        @block.vector
        def _(v, h=h, ps=ps):
            v.wait_ge(sem["T"], ht[h])
            yv = Y_tm.rearrange("p (c f) -> p c f", c=CH)[:, :, h * 64:h * 64 + 64]
            v.tensor_copy(out=yv, in_=ps[:, 0:512].rearrange("p (c f) -> p c f", c=CH))
            ins = None
            for c in range(1, CH):
                ins = v.scalar_tensor_tensor(out=Y_tm[:, c * DI + h * 64:c * DI + h * 64 + 64],
                                             in0=ps[:, 512 + c * 64:512 + c * 64 + 64],
                                             scalar=sb["ea_tm"][:, c * 64 + h:c * 64 + h + 1],
                                             in1=Y_tm[:, c * DI + h * 64:c * DI + h * 64 + 64],
                                             op0=ALU.mult, op1=ALU.add)
            hv[h] = inc(ins, "V")

    ssd_V = cnt["V"]

    @block.gpsimd
    def _(g):
        g.wait_ge(sem["V"], ssd_V)
        inc(g.dma_start(out=scr["cc4_in"][0, 0:128 * 4096].rearrange("(p f) -> p f", p=128),
                        in_=P[:]), "dG", 16)
        inc(g.dma_start(out=scr["cc4_in"][0, 128 * 4096:128 * 4096 + 64],
                        in_=sb["prefx"][:, 15:16]), "dG", 16)
        g.wait_ge(sem["dG"], cnt["dG"])
        inc(g.collective_compute("AllGather", ALU.bypass,
                                 replica_groups=[list(range(NCORES))],
                                 ins=[scr["cc4_in"]], outs=[scr["cc4_out"]]), "CC")
    cc4_done = cnt["CC"]

    dbg_dump("ytm_loc", [((c,), Y_tm[:, c * DI:(c + 1) * DI]) for c in range(CH)],
             (CH, 128, DI), BF16)
    dbg_dump("pstate", [(None, P[:])], (128, NH * HP))
    if upto == "ssd":
        return finish()

    barrier()
    ARC.reset(0)
    bbuf = ARC.alloc(4096, F32)
    cvb = ARC.alloc(64, F32)
    ARB.reset(0)
    coefb = ARB.alloc(4096, BF16)

    @block.gpsimd
    def _(g):
        g.wait_ge(sem["CC"], cc4_done)
        inc(g.dma_start(out=sb["svals"][:],
                        in_=scr["cc4_out"][:, 128 * 4096:128 * 4096 + 64]), "dG", 16)
        g.wait_ge(sem["dG"], cnt["dG"])

    @block.tensor
    def _(te):
        te.wait_ge(sem["dG"], cnt["dG"])
        inc(te.matmul(psA[0:8, 0:64], sb["selw"][:], sb["svals"][:], start=True, stop=True), "T")

    @block.scalar
    def _(s):
        s.wait_ge(sem["T"], cnt["T"])
        inc(s.activation(out=sb["cvals"][:], in_=psA[0:8, 0:64], func=AF.Exp), "S")

    @block.vector
    def _(v):
        v.wait_ge(sem["S"], cnt["S"])
        inc(v.tensor_scalar_mul(out=sb["svals"][:], in0=sb["cvals"][:],
                                scalar1=sb["loww"][:]), "V")

    @block.sync
    def _(sy):
        sy.wait_ge(sem["V"], cnt["V"])
        inc(sy.dma_start(out=scr["cvals_scr"], in_=sb["svals"][:]), "dW", 16)
    cv_w = cnt["dW"]

    @block.vector
    def _(v):
        v.wait_ge(sem["dW"], cv_w)
        inc(v.memset(P[:], 0.0), "V")

    for jj in range(NCORES):
        @block.sync
        def _(sy, jj=jj):
            sy.wait_ge(sem["V"], cnt["V"])
            inc(sy.dma_start(out=bbuf[:],
                             in_=scr["cc4_out"][jj, 0:128 * 4096]
                             .rearrange("(p f) -> p f", p=128)), "dW", 16)

        @block.gpsimd
        def _(g, jj=jj):
            g.wait_ge(sem["V"], cnt["V"])
            ins = g.dma_start(out=cvb[:, 0:64],
                              in_=scr["cvals_scr"][jj:jj + 1, :].broadcast_to([128, 64]))
            inc(ins, "dG", 16)
            g.wait_ge(sem["dG"], cnt["dG"])

        @block.vector
        def _(v, jj=jj):
            v.wait_ge(sem["dW"], cnt["dW"])
            v.wait_ge(sem["dG"], cnt["dG"])
            ins = None
            for h in range(NH):
                ins = v.scalar_tensor_tensor(out=P[:, h * 64:h * 64 + 64],
                                             in0=bbuf[:, h * 64:h * 64 + 64],
                                             scalar=cvb[:, h:h + 1],
                                             in1=P[:, h * 64:h * 64 + 64],
                                             op0=ALU.mult, op1=ALU.add)
            inc(ins, "V")

    @block.vector
    def _(v):
        v.wait_ge(sem["V"], cnt["V"])
        inc(v.tensor_copy(out=coefb[:], in_=P[:]), "V")
    pin_bf = cnt["V"]
    dbg_dump("pin", [(None, P[:])], (128, NH * HP))
    dbg_dump("cv", [(None, sb["svals"][:])], (8, 64))

    for c in range(CH):
        @block.tensor
        def _(te, c=c):
            te.wait_ge(sem["V"], cnt["V"])
            for nb in range(8):
                ps2 = psA if nb < 4 else psB
                inc(te.matmul(ps2[:, (nb % 4) * 512:(nb % 4) * 512 + 512],
                              sb["C_fm"][:, c * 128:c * 128 + 128],
                              coefb[:, nb * 512:nb * 512 + 512],
                              start=True, stop=True), "T")

        @block.vector
        def _(v, c=c):
            v.wait_ge(sem["T"], cnt["T"])
            ins = None
            for h in range(NH):
                ps2 = psA if h < 32 else psB
                ho = h % 32
                ins = v.scalar_tensor_tensor(out=Y_tm[:, c * DI + h * 64:c * DI + h * 64 + 64],
                                             in0=ps2[:, ho * 64:ho * 64 + 64],
                                             scalar=sb["ea2_tm"][:, c * 64 + h:c * 64 + h + 1],
                                             in1=Y_tm[:, c * DI + h * 64:c * DI + h * 64 + 64],
                                             op0=ALU.mult, op1=ALU.add)
            inc(ins, "V")

    dbg_dump("ytm", [((c,), Y_tm[:, c * DI:(c + 1) * DI]) for c in range(CH)],
             (CH, 128, DI), BF16)
    if upto == "dy":
        return finish()

    barrier()
    ARB.reset(0)
    Y_fm = ARB.alloc(KDI * T, BF16)

    for c in range(CH):
        for grp in range(2):
            ps = psAb if grp == 0 else psBb

            @block.tensor
            def _(te, c=c, grp=grp, ps=ps):
                te.wait_ge(sem["V"], cnt["V"])
                for q2 in range(16):
                    q = grp * 16 + q2
                    inc(te.transpose(ps[:, q2 * 128:q2 * 128 + 128],
                                     Y_tm[:, c * DI + q * 128:c * DI + q * 128 + 128],
                                     sb["ident_b"][:]), "T")

            @block.vector
            def _(v, c=c, grp=grp, ps=ps):
                v.wait_ge(sem["T"], cnt["T"])
                ins = None
                for q2 in range(16):
                    q = grp * 16 + q2
                    ins = v.tensor_copy(out=Y_fm[:, q * T + c * 128:q * T + c * 128 + 128],
                                        in_=ps[:, q2 * 128:q2 * 128 + 128])
                inc(ins, "V")
    ytrans_V = cnt["V"]

    barrier()
    ARA.reset(0)
    hin2 = ARA.alloc(KD * T, BF16)

    @block.sync
    def _(sy):
        sy.wait_ge(sem["V"], ytrans_V)
        for fc in range(KD):
            inc(sy.dma_start(out=hin2[:, fc * T:(fc + 1) * T],
                             in_=scr["hin_scr"][fc][:, 3:TH]), "dW", 16)
    hin_rel = cnt["dW"]

    ARC.reset(0)
    zwb = [[None, None, ARC.alloc(KD * 128, BF16), ARC.alloc(KD * 128, BF16)]]
    siluz = ARC.alloc(T, F32)
    sqb = ARC.alloc(T, F32)

    def z_evac(ncu, reg):
        ps, off = reg

        @block.scalar
        def _(s, ncu=ncu, ps=ps, off=off):
            s.wait_ge(sem["T"], cnt["T"])
            inc(s.activation(out=siluz[:, 0:T], in_=ps[:, off:off + 1024], func=AF.Silu), "S")

        @block.vector
        def _(v, ncu=ncu):
            v.wait_ge(sem["S"], cnt["S"])
            ins = v.tensor_tensor(out=Y_fm[:, ncu * T:(ncu + 1) * T],
                                  in0=siluz[:, 0:T],
                                  in1=Y_fm[:, ncu * T:(ncu + 1) * T], op=ALU.mult)
            m = inc(ins, "V")

        @block.scalar
        def _(s, ncu=ncu):
            s.wait_ge(sem["V"], cnt["V"])
            inc(s.activation(out=sqb[:, 0:T], in_=Y_fm[:, ncu * T:(ncu + 1) * T],
                             func=AF.Square), "S")

        @block.tensor
        def _(te, ncu=ncu):
            te.wait_ge(sem["S"], cnt["S"])
            st = (ncu == 0)
            inc(te.matmul(psB[0:1, 1536:2048], sb["onesc"][:, 0:1], sqb[:, 0:512],
                          start=st, stop=(ncu == KDI - 1)), "T")
            inc(te.matmul(psB[0:1, 1024:1536], sb["onesc"][:, 0:1], sqb[:, 512:1024],
                          start=st, stop=(ncu == KDI - 1)), "T")
        return cnt["V"]

    bgemm(NC=KDI, KTOT=KD, KG=KD, wsrcs=[lambda n, kg: dram["w_z"][n]
                                         .rearrange("k p n -> p k n")],
          wdt=BF16, cast=True,
          rhs_fn=lambda pi, k, cb: hin2[:, k * T + cb * 512:k * T + cb * 512 + 512],
          ncols=2, regions=[(psA, 0), (psA, 1024)], evac_fn=z_evac, bufs=[zwb[0]],
          pre_wait_T=lambda te: te.wait_ge(sem["dW"], hin_rel))
    zstat_T = cnt["T"]

    @block.vector
    def _(v):
        v.wait_ge(sem["T"], zstat_T)
        inc(v.tensor_copy(out=sb["rowbuf"][0:1, 0:512], in_=psB[0:1, 1536:2048]), "V")
        inc(v.tensor_copy(out=sb["rowbuf"][0:1, 512:1024], in_=psB[0:1, 1024:1536]), "V")

    @block.scalar
    def _(s):
        s.wait_ge(sem["V"], cnt["V"])
        inc(s.activation(out=sb["rowbuf"][0:1, 0:T], in_=sb["rowbuf"][0:1, 0:T],
                         func=AF.Sqrt, bias=sb["kc"][0:1, 1:2], scale=sb["kc"][0:1, 4:5]), "S")

    @block.vector
    def _(v):
        v.wait_ge(sem["S"], cnt["S"])
        inc(v.reciprocal(out=sb["rowbuf"][0:1, 0:T], in_=sb["rowbuf"][0:1, 0:T]), "V")

    @block.sync
    def _(sy):
        sy.wait_ge(sem["V"], cnt["V"])
        inc(sy.dma_start(out=scr["row_scr"][0], in_=sb["rowbuf"][0:1, 0:T]), "dW", 16)
    r3_w = cnt["dW"]

    ARC.reset(0)
    rho3bc = ARC.alloc(T, F32)
    mixcol = ARC.alloc(32, F32)

    @block.sync
    def _(sy):
        sy.wait_ge(sem["dW"], r3_w)
        inc(sy.dma_start(out=rho3bc[:], in_=scr["row_scr"][0:1, :].broadcast_to([128, T])),
            "dW", 16)
        inc(sy.dma_start(out=mixcol[:, 0:32],
                         in_=dram["mixw"].rearrange("a (c p) -> (a p) c", p=128),
                         allow_slow_non_contiguous=True), "dW", 16)
        sy.wait_ge(sem["dW"], cnt["dW"])
    r3_b = cnt["dW"]

    @block.vector
    def _(v):
        v.wait_ge(sem["dW"], r3_b)
        ins = None
        for zc in range(KDI):
            ins = v.scalar_tensor_tensor(out=Y_fm[:, zc * T:(zc + 1) * T],
                                         in0=rho3bc[:], scalar=mixcol[:, zc:zc + 1],
                                         in1=Y_fm[:, zc * T:(zc + 1) * T],
                                         op0=ALU.mult, op1=ALU.mult)
        inc(ins, "V")
    ymix_V = cnt["V"]

    dbg_dump("ymix", [((q,), Y_fm[:, q * T:(q + 1) * T]) for q in range(KDI)],
             (KDI, 128, T), BF16)
    if upto == "ymix":
        return finish()

    barrier()
    ARC.reset(0)
    owb = [[None, None, ARC.alloc(KD * 128, BF16), ARC.alloc(KD * 128, BF16)]]
    xnb2 = ARC.alloc(T, BF16)
    h1b = ARC.alloc(T, BF16)
    sqb2 = ARC.alloc(T, F32)

    def op_evac(ncu, reg):
        ps, off = reg

        @block.sync
        def _(sy, ncu=ncu):
            sy.wait_ge(sem["V"], cnt["V"])
            inc(sy.dma_start(out=xnb2[:], in_=scr["xn_scr"][ncu]), "dW", 16)

        @block.vector
        def _(v, ncu=ncu, ps=ps, off=off):
            v.wait_ge(sem["T"], cnt["T"])
            v.wait_ge(sem["dW"], cnt["dW"])
            inc(v.tensor_tensor(out=h1b[:, 0:T], in0=ps[:, off:off + 1024], in1=xnb2[:],
                                op=ALU.add), "V")
        vm = cnt["V"]

        @block.sync
        def _(sy, ncu=ncu, vm=vm):
            sy.wait_ge(sem["V"], vm)
            inc(sy.dma_start(out=scr["h1_scr"][ncu], in_=h1b[:, 0:T]), "dW", 16)

        @block.scalar
        def _(s, ncu=ncu, vm=vm):
            s.wait_ge(sem["V"], vm)
            inc(s.activation(out=sqb2[:, 0:T], in_=h1b[:, 0:T], func=AF.Square), "S")

        @block.tensor
        def _(te, ncu=ncu):
            te.wait_ge(sem["S"], cnt["S"])
            st = (ncu == 0)
            inc(te.matmul(psB[0:1, 1536:2048], sb["onesc"][:, 0:1], sqb2[:, 0:512],
                          start=st, stop=(ncu == KD - 1)), "T")
            inc(te.matmul(psB[0:1, 1024:1536], sb["onesc"][:, 0:1], sqb2[:, 512:1024],
                          start=st, stop=(ncu == KD - 1)), "T")
        return vm

    bgemm(NC=KD, KTOT=KDI, KG=KD, wsrcs=[lambda n, kg: dram["w_out"][n]
                                         .rearrange("k p n -> p k n")
                                         [:, kg * 16:kg * 16 + 16, :]],
          wdt=BF16, cast=True,
          rhs_fn=lambda pi, k, cb: Y_fm[:, k * T + cb * 512:k * T + cb * 512 + 512],
          ncols=2, regions=[(psA, 0), (psA, 1024)], evac_fn=op_evac, bufs=[owb[0]],
          pre_wait_T=lambda te: te.wait_ge(sem["V"], ymix_V))
    h1stat_T = cnt["T"]

    @block.vector
    def _(v):
        v.wait_ge(sem["T"], h1stat_T)
        inc(v.tensor_copy(out=sb["rowbuf"][0:1, 0:512], in_=psB[0:1, 1536:2048]), "V")
        inc(v.tensor_copy(out=sb["rowbuf"][0:1, 512:1024], in_=psB[0:1, 1024:1536]), "V")

    @block.scalar
    def _(s):
        s.wait_ge(sem["V"], cnt["V"])
        inc(s.activation(out=sb["rowbuf"][0:1, 0:T], in_=sb["rowbuf"][0:1, 0:T],
                         func=AF.Sqrt, bias=sb["kc"][0:1, 1:2], scale=sb["kc"][0:1, 5:6]), "S")

    @block.vector
    def _(v):
        v.wait_ge(sem["S"], cnt["S"])
        inc(v.reciprocal(out=sb["rowbuf"][0:1, 0:T], in_=sb["rowbuf"][0:1, 0:T]), "V")

    @block.sync
    def _(sy):
        sy.wait_ge(sem["V"], cnt["V"])
        inc(sy.dma_start(out=scr["row_scr"][1], in_=sb["rowbuf"][0:1, 0:T]), "dW", 16)
    r4_w = cnt["dW"]

    barrier()
    ARA.reset(0)
    h2_fm = ARA.alloc(KD * T, BF16)
    rho4bc = ARC.alloc(T, F32)
    n2col = ARC.alloc(16, F32)

    @block.sync
    def _(sy):
        sy.wait_ge(sem["dW"], r4_w)
        inc(sy.dma_start(out=rho4bc[:], in_=scr["row_scr"][1:2, :].broadcast_to([128, T])),
            "dW", 16)
        inc(sy.dma_start(out=n2col[:, 0:16],
                         in_=dram["norm2"].rearrange("a (c p) -> (a p) c", p=128),
                         allow_slow_non_contiguous=True), "dW", 16)
        sy.wait_ge(sem["dW"], cnt["dW"])
    r4_b = cnt["dW"]

    for fc in range(KD):
        @block.sync
        def _(sy, fc=fc):
            sy.wait_ge(sem["V"], cnt["V"])
            inc(sy.dma_start(out=h1b[:, 0:T], in_=scr["h1_scr"][fc]), "dW", 16)

        @block.vector
        def _(v, fc=fc):
            v.wait_ge(sem["dW"], max(cnt["dW"], r4_b))
            inc(v.scalar_tensor_tensor(out=h2_fm[:, fc * T:(fc + 1) * T],
                                       in0=rho4bc[:], scalar=n2col[:, fc:fc + 1],
                                       in1=h1b[:, 0:T], op0=ALU.mult, op1=ALU.mult), "V")
    h2_V = cnt["V"]

    dbg_dump("h2", [((fc,), h2_fm[:, fc * T:(fc + 1) * T]) for fc in range(KD)],
             (KD, 128, T), BF16)
    if upto == "h2":
        return finish()

    barrier()
    ARC.reset(0)
    gwb = [[None, None, ARC.alloc(8 * 128, BF16), ARC.alloc(8 * 128, BF16)],
           [None, None, ARC.alloc(8 * 128, BF16), ARC.alloc(8 * 128, BF16)]]
    silug = ARC.alloc(512, F32)
    ARB.reset(0)
    mid0 = ARB.alloc(KI * 512, BF16)
    ARD.reset(0)
    M1A = 32
    mid1a = ARA.alloc(M1A * 512, BF16)
    mid1b = ARD.alloc((KI - M1A) * 512, BF16)

    def mid_blk(half, k):
        if half == 0:
            return mid0[:, k * 512:k * 512 + 512]
        if k < M1A:
            return mid1a[:, k * 512:k * 512 + 512]
        return mid1b[:, (k - M1A) * 512:(k - M1A) * 512 + 512]

    def gu_evac(ncu, reg):
        ps, off = reg
        for half in range(2):
            @block.scalar
            def _(s, ncu=ncu, ps=ps, off=off, half=half):
                s.wait_ge(sem["T"], cnt["T"])
                s.wait_ge(sem["V"], cnt["V"])
                inc(s.activation(out=silug[:, 0:512],
                                 in_=ps[:, off + half * 512:off + half * 512 + 512],
                                 func=AF.Silu), "S")

            @block.vector
            def _(v, ncu=ncu, ps=ps, off=off, half=half):
                v.wait_ge(sem["S"], cnt["S"])
                inc(v.tensor_tensor(out=mid_blk(half, ncu), in0=silug[:, 0:512],
                                    in1=ps[:, off + 1024 + half * 512:
                                           off + 1024 + half * 512 + 512],
                                    op=ALU.mult), "V")
        return cnt["V"]

    bgemm(NC=KI, KTOT=KD, KG=8,
          wsrcs=[lambda n, kg: dram["w_gate"][n].rearrange("k p n -> p k n")
                 [:, kg * 8:kg * 8 + 8, :],
                 lambda n, kg: dram["w_up"][n].rearrange("k p n -> p k n")
                 [:, kg * 8:kg * 8 + 8, :]],
          wdt=BF16, cast=True,
          rhs_fn=lambda pi, k, cb: h2_fm[:, k * T + cb * 512:k * T + cb * 512 + 512],
          ncols=2, regions=[(psA, 0), (psB, 0)],
          evac_fn=gu_evac, bufs=gwb,
          pre_wait_T=lambda te: te.wait_ge(sem["V"], h2_V))

    barrier()
    ARC.reset(0)
    dwb = [[None, None, ARC.alloc(11 * 128, BF16), ARC.alloc(11 * 128, BF16)]]
    h1b2 = ARC.alloc(T, BF16)
    sout = ARC.alloc(T, BF16)

    def dn_evac(ncu, reg):
        ps, off = reg

        @block.sync
        def _(sy, ncu=ncu):
            sy.wait_ge(sem["V"], cnt["V"])
            inc(sy.dma_start(out=h1b2[:, 0:T], in_=scr["h1_scr"][ncu]), "dW", 16)

        @block.vector
        def _(v, ncu=ncu, ps=ps, off=off):
            v.wait_ge(sem["T"], cnt["T"])
            v.wait_ge(sem["dW"], cnt["dW"])
            inc(v.tensor_tensor(out=sout[:, 0:T], in0=ps[:, off:off + T],
                                in1=h1b2[:, 0:T], op=ALU.add), "V")
        vm = cnt["V"]

        @block.sync
        def _(sy, ncu=ncu, vm=vm):
            sy.wait_ge(sem["V"], vm)
            inc(sy.dma_start(out=scr["s_scr"][ncu], in_=sout[:, 0:T]), "dW", 16)
        return vm

    bgemm(NC=KD, KTOT=KI, KG=11,
          wsrcs=[lambda n, kg: dram["w_down"][n].rearrange("k p n -> p k n")
                 [:, kg * 11:kg * 11 + 11, :]],
          wdt=BF16, cast=True,
          rhs_fn=lambda pi, k, cb: mid_blk(cb, k),
          ncols=2, regions=[(psA, 0), (psA, 1024), (psB, 0), (psB, 1024)],
          evac_fn=dn_evac, bufs=dwb)

    dbg_dump("s_scr", [((fc,), scr["s_scr"][fc]) for fc in range(KD)],
             (KD, 128, T), BF16)
    if upto == "s":
        return finish()

    @block.sync
    def _(sy):
        sy.wait_ge(sem["V"], cnt["V"])
        inc(sy.dma_start(out=scr["row_scr"][2].rearrange("(j p) -> p j", p=128),
                         in_=sb["iqs_cols"][:],
                         allow_slow_non_contiguous=True), "dW", 16)
    iqs_w = cnt["dW"]

    dbgcap = None
    if "m2cap" in dbgset:
        dbgcap = nc.dram_tensor("dbg_m2cap", [KI, 128, 512], BF16,
                                kind="ExternalOutput").ap()

    for half in range(2):
        barrier()
        ARC.reset(0)
        iqsbc = ARC.alloc(512, F32)
        q2bc = ARC.alloc(512, F32)
        w3cbc = ARC.alloc(512, F32)
        combo1 = ARC.alloc(512, F32)
        combo2 = ARC.alloc(512, F32)
        sldb = ARC.alloc(512, BF16)
        ARC_M = ARC.cur
        fwb = [[ARC.alloc(8 * 128, BF16), ARC.alloc(8 * 128, BF16)],
               [ARC.alloc(8 * 128, BF16), ARC.alloc(8 * 128, BF16)]]

        ARA.reset(0)
        mid2 = ARA.alloc(KI * 512, BF16)
        q2c = mid2
        macc = ARA.alloc(512, F32)
        mtmp = ARA.alloc(512, F32)
        gsc = ARA.alloc(512, F32)
        usc = ARA.alloc(512, F32)
        sqb3 = ARA.alloc(512, F32)

        ARD.reset(0)
        xqh = ARD.alloc(KD * 512, BF16)

        @block.sync
        def _(sy, half=half):
            sy.wait_ge(sem["dW"], iqs_w)
            sy.wait_ge(sem["V"], cnt["V"])
            inc(sy.dma_start(out=iqsbc[:],
                             in_=scr["row_scr"][2:3, half * 512:half * 512 + 512]
                             .broadcast_to([128, 512])), "dW", 16)
            for fc in range(KD):
                inc(sy.dma_start(out=xqh[:, fc * 512:fc * 512 + 512],
                                 in_=scr["xq_scr"][fc][:, half * 512:half * 512 + 512]),
                    "dW", 16)
            sy.wait_ge(sem["dW"], cnt["dW"])
        xq_ld = cnt["dW"]

        @block.vector
        def _(v):
            v.wait_ge(sem["dW"], xq_ld)
            v.memset(macc[:, 0:512], 0.0)
            v.tensor_scalar_mul(out=combo1[:, 0:512], in0=iqsbc[:, 0:512],
                                scalar1=sb["wsci_col"][:, 0:1])
            inc(v.tensor_scalar_mul(out=combo2[:, 0:512], in0=iqsbc[:, 0:512],
                                    scalar1=sb["wsci_col"][:, 1:2]), "V")
        macc_V = cnt["V"]

        def f12_evac(ncu, reg, half=half, mid2=mid2, macc=macc, iqsbc=iqsbc,
                     gsc=gsc, usc=usc, sqb3=sqb3):
            ps, off = reg

            @block.vector
            def _(v, ncu=ncu, ps=ps, off=off):
                v.wait_ge(sem["T"], cnt["T"])
                inc(v.tensor_tensor(out=gsc[:, 0:512], in0=ps[:, off:off + 512],
                                    in1=combo1[:], op=ALU.mult), "V")
                inc(v.tensor_tensor(out=usc[:, 0:512], in0=ps[:, off + 512:off + 1024],
                                    in1=combo2[:], op=ALU.mult), "V")

            @block.scalar
            def _(s, ncu=ncu):
                s.wait_ge(sem["V"], cnt["V"])
                inc(s.activation(out=sqb3[:, 0:512], in_=gsc[:, 0:512], func=AF.Silu), "S")

            @block.vector
            def _(v, ncu=ncu):
                v.wait_ge(sem["S"], cnt["S"])
                vf(v, v.tensor_tensor(out=mtmp[:, 0:512], in0=sqb3[:, 0:512],
                                      in1=usc[:, 0:512], op=ALU.mult))
                inc(v.tensor_copy(out=mid2[:, ncu * 512:ncu * 512 + 512],
                                  in_=mtmp[:, 0:512]), "V")

            @block.scalar
            def _(s, ncu=ncu):
                s.wait_ge(sem["V"], cnt["V"])
                inc(s.activation(out=sqb3[:, 0:512], in_=mtmp[:, 0:512],
                                 func=AF.Square), "S")

            @block.vector
            def _(v, ncu=ncu):
                v.wait_ge(sem["S"], cnt["S"])
                inc(v.tensor_tensor(out=macc[:, 0:512], in0=sqb3[:, 0:512],
                                    in1=macc[:, 0:512], op=ALU.max), "V")

            @block.tensor
            def _(te, ncu=ncu):
                te.wait_ge(sem["S"], cnt["S"])
                inc(te.matmul(psB[0:1, 1536:2048], sb["onesc"][:, 0:1], sqb3[:, 0:512],
                              start=(ncu == 0), stop=(ncu == KI - 1)), "T")

            if half == 0 and "m2cap" in dbgset:
                @block.gpsimd
                def _(g, ncu=ncu):
                    g.wait_ge(sem["V"], cnt["V"])
                    inc(g.dma_start(out=dbgcap[ncu],
                                    in_=mid2[:, ncu * 512:ncu * 512 + 512]), "dG", 16)
            return cnt["V"]

        bgemm(NC=KI, KTOT=KD, KG=8,
              wsrcs=[lambda n, kg: wq_group(1, n, kg, 8),
                     lambda n, kg: wq_group(2, n, kg, 8)],
              wdt=BF16, cast=False,
              rhs_fn=lambda pi, k, cb, xqh=xqh: xqh[:, k * 512:k * 512 + 512],
              ncols=1, regions=[(psA, 0), (psA, 1024)],
              evac_fn=f12_evac, bufs=fwb,
              pre_wait_T=(lambda te: (te.wait_ge(sem["CC"], cc3_done),
                                      te.wait_ge(sem["V"], macc_V))[0]))
        f12_T = cnt["T"]
        if upto == "f12":
            dbg_dump("mid2", [((q,), mid2[:, q * 512:(q + 1) * 512]) for q in range(KI)],
                     (KI, 128, 512), BF16)
            return finish()

        @block.vector
        def _(v):
            v.wait_ge(sem["T"], f12_T)
            v.wait_ge(sem["V"], cnt["V"])
            inc(v.tensor_copy(out=sb["rowbuf"][0:1, 0:512], in_=psB[0:1, 1536:2048]), "V")

        @block.sync
        def _(sy):
            sy.wait_ge(sem["V"], cnt["V"])
            inc(sy.dma_start(out=scr["row_scr"][5, 0:512], in_=sb["rowbuf"][0:1, 0:512]),
                "dW", 16)
        ss_w = cnt["dW"]

        @block.sync
        def _(sy):
            sy.wait_ge(sem["dW"], ss_w)
            inc(sy.dma_start(out=sb["dump"][:, 8:12],
                             in_=scr["row_scr"][5, 0:512].rearrange("(q p) -> p q", p=128),
                             allow_slow_non_contiguous=True), "dW", 16)
        ss_r = cnt["dW"]

        @block.tensor
        def _(te):
            te.wait_ge(sem["V"], cnt["V"])
            for q in range(4):
                inc(te.transpose(psA[:, q * 128:q * 128 + 128],
                                 macc[:, q * 128:q * 128 + 128], sb["ident_f"][:]), "T")

        @block.vector
        def _(v):
            v.wait_ge(sem["T"], cnt["T"])
            v.wait_ge(sem["dW"], ss_r)
            D0 = sb["dump"]
            for q in range(4):
                vf(v, v.reduce_max(out=D0[:, 4 + q:5 + q], in_=psA[:, q * 128:q * 128 + 128],
                                   axis=AX))
            inc(v.tensor_scalar(out=D0[:, 0:4], in0=D0[:, 8:12], scalar1=1.0 / INTER,
                                scalar2=None, op0=ALU.mult), "V")

        @block.scalar
        def _(s):
            s.wait_ge(sem["V"], cnt["V"])
            sf(s, s.activation(out=sb["dump"][:, 12:16], in_=sb["dump"][:, 4:8],
                               func=AF.Sqrt))
            inc(s.activation(out=sb["dump"][:, 4:8], in_=sb["dump"][:, 0:4], func=AF.Sqrt,
                             bias=sb["kc"][:, 1:2]), "S")

        @block.vector
        def _(v):
            v.wait_ge(sem["S"], cnt["S"])
            D0 = sb["dump"]
            for q in range(4):
                vf(v, v.reciprocal(out=D0[:, 0 + q:1 + q], in_=D0[:, 4 + q:5 + q]))
            vf(v, v.tensor_mul(out=D0[:, 4:8], in0=D0[:, 0:4], in1=D0[:, 12:16]))
            vf(v, v.tensor_scalar_max(out=D0[:, 8:12], in0=D0[:, 4:8], scalar1=EPS_FFN))
            for q in range(4):
                vf(v, v.reciprocal(out=D0[:, 4 + q:5 + q], in_=D0[:, 8 + q:9 + q]))
            vf(v, v.tensor_scalar_mul(out=D0[:, 12:16], in0=D0[:, 4:8], scalar1=127.0))
            vf(v, v.tensor_mul(out=D0[:, 4:8], in0=D0[:, 0:4], in1=D0[:, 12:16]))
            vf(v, v.tensor_scalar_mul(out=D0[:, 0:4], in0=D0[:, 8:12], scalar1=1.0 / 127.0))
            inc(v.tensor_scalar_mul(out=D0[:, 8:12], in0=D0[:, 0:4],
                                    scalar1=sb["wsci_col"][:, 2:3]), "V")

        @block.sync
        def _(sy):
            sy.wait_ge(sem["V"], cnt["V"])
            inc(sy.dma_start(out=scr["row_scr"][3, 0:512].rearrange("(q p) -> p q", p=128),
                             in_=sb["dump"][:, 4:8], allow_slow_non_contiguous=True),
                "dW", 16)
            inc(sy.dma_start(out=scr["row_scr"][4, 0:512].rearrange("(q p) -> p q", p=128),
                             in_=sb["dump"][:, 8:12], allow_slow_non_contiguous=True),
                "dW", 16)
        q2r_w = cnt["dW"]

        @block.sync
        def _(sy):
            sy.wait_ge(sem["dW"], q2r_w)
            inc(sy.dma_start(out=q2bc[:], in_=scr["row_scr"][3:4, 0:512]
                             .broadcast_to([128, 512])), "dW", 16)
            inc(sy.dma_start(out=w3cbc[:], in_=scr["row_scr"][4:5, 0:512]
                             .broadcast_to([128, 512])), "dW", 16)
        q2b_b = cnt["dW"]

        for ncu in range(KI):
            @block.vector
            def _(v, ncu=ncu):
                v.wait_ge(sem["dW"], q2b_b)
                sl = mid2[:, ncu * 512:ncu * 512 + 512]
                vf(v, v.tensor_tensor(out=gsc[:, 0:512], in0=sl, in1=q2bc[:], op=ALU.mult))
                vf(v, v.tensor_copy(out=usc[:, 0:512].bitcast(I32), in_=gsc[:, 0:512]))
                vf(v, v.tensor_copy(out=gsc[:, 0:512], in_=usc[:, 0:512].bitcast(I32)))
                inc(v.tensor_scalar(out=q2c[:, ncu * 512:ncu * 512 + 512], in0=gsc[:, 0:512],
                                    scalar1=-128.0, scalar2=127.0, op0=ALU.max, op1=ALU.min),
                    "V")
        q2_V = cnt["V"]
        if upto == "q2":
            dbg_dump("d0", [(None, sb["dump"][:, 0:16])], (128, 16))
            dbg_dump("rs5", [(None, scr["row_scr"][5:6, 0:512])], (1, 512))
            dbg_dump("rs3", [(None, scr["row_scr"][3:4, 0:512])], (1, 512))
            dbg_dump("q2bcd", [(None, q2bc[:, 0:512])], (128, 512))
            dbg_dump("gscd", [(None, gsc[:, 0:512])], (128, 512))
            dbg_dump("uscd", [(None, usc[:, 0:512])], (128, 512))
            dbg_dump("mid2last", [(None, mid2[:, 43 * 512:44 * 512])], (128, 512), BF16)
            dbg_dump("q2c", [((q,), q2c[:, q * 512:(q + 1) * 512]) for q in range(KI)],
                     (KI, 128, 512), BF16)
            return finish()

        barrier()
        ARB.reset(0)
        gpark = ARB.alloc(KD * 512, F32)
        sgb = ARB.alloc(KD * 512, F32)
        ARC.reset(ARC_M)
        w3b = [[ARC.alloc(11 * 128, BF16), ARC.alloc(11 * 128, BF16)]]
        w3tmp = ARC.alloc(512, F32)

        def w3_evac(ncu, reg, half=half, gpark=gpark):
            ps, off = reg

            @block.vector
            def _(v, ncu=ncu, ps=ps, off=off):
                v.wait_ge(sem["T"], cnt["T"])
                v.wait_ge(sem["S"], cnt["S"])
                inc(v.tensor_tensor(out=w3tmp[:, 0:512],
                                    in0=ps[:, off:off + 512], in1=w3cbc[:], op=ALU.mult), "V")

            @block.scalar
            def _(s, ncu=ncu):
                s.wait_ge(sem["V"], cnt["V"])
                inc(s.activation(out=gpark[:, ncu * 512:ncu * 512 + 512],
                                 in_=w3tmp[:, 0:512], func=AF.Silu), "S")
            return cnt["V"]

        bgemm(NC=KD, KTOT=KI, KG=11,
              wsrcs=[lambda n, kg: wq_group(3, n, kg, 11)],
              wdt=BF16, cast=False,
              rhs_fn=lambda pi, k, cb, q2c=q2c: q2c[:, k * 512:k * 512 + 512],
              ncols=1, regions=[(psA, 0), (psA, 512), (psA, 1024), (psA, 1536)],
              evac_fn=w3_evac, bufs=w3b,
              pre_wait_T=lambda te, q2_V=q2_V: te.wait_ge(sem["V"], q2_V))
        w3_S = cnt["S"]

        if upto == "w3":
            dbg_dump("gpark", [((q,), gpark[:, q * 512:(q + 1) * 512]) for q in range(KD)],
                     (KD, 128, 512))
            return finish()

        for fc in range(KD):
            @block.sync
            def _(sy, fc=fc, half=half):
                sy.wait_ge(sem["V"], cnt["V"])
                inc(sy.dma_start(out=sldb[:, 0:512],
                                 in_=scr["s_scr"][fc][:, half * 512:half * 512 + 512]),
                    "dW", 16)

            @block.vector
            def _(v, fc=fc, w3_S=w3_S):
                v.wait_ge(sem["dW"], cnt["dW"])
                v.wait_ge(sem["S"], w3_S)
                inc(v.tensor_tensor(out=sgb[:, fc * 512:fc * 512 + 512],
                                    in0=gpark[:, fc * 512:fc * 512 + 512],
                                    in1=sldb[:, 0:512], op=ALU.mult), "V")
        sg_V = cnt["V"]

        ARC.reset(ARC_M)
        xld = ARC.alloc(2048, F32)
        for tb in range(4):
            trow = half * 512 + tb * 128

            @block.tensor
            def _(te, tb=tb, sg_V=sg_V):
                te.wait_ge(sem["V"], cnt["V"])
                te.wait_ge(sem["V"], sg_V)
                for fc in range(KD):
                    inc(te.transpose(psA[:, fc * 128:fc * 128 + 128],
                                     sgb[:, fc * 512 + tb * 128:fc * 512 + tb * 128 + 128],
                                     sb["ident_f"][:]), "T")
            t_tp = cnt["T"]

            @block.sync
            def _(sy, trow=trow):
                sy.wait_ge(sem["V"], cnt["V"])
                sy.wait_ge(sem["dW"], cnt["dW"])
                inc(sy.dma_start(out=xld[:, 0:2048],
                                 in_=dram["x_slice"][3 + trow:3 + trow + 128, :]), "dW", 16)

            @block.vector
            def _(v, t_tp=t_tp):
                v.wait_ge(sem["T"], t_tp)
                v.wait_ge(sem["dW"], cnt["dW"])
                inc(v.tensor_tensor(out=xld[:, 0:2048], in0=psA[:, 0:2048],
                                    in1=xld[:, 0:2048], op=ALU.add), "V")
            v_y = cnt["V"]

            @block.sync
            def _(sy, trow=trow, v_y=v_y):
                sy.wait_ge(sem["V"], v_y)
                inc(sy.dma_start(out=y_out[trow:trow + 128, :], in_=xld[:, 0:2048]), "dW", 16)

        if half == 0 and "cap" in dbgset:
            dbg_dump("cap_d0", [(None, sb["dump"][:, 0:16])], (128, 16))
            dbg_dump("cap_q2bc", [(None, q2bc[:, 0:512])], (128, 512))
            dbg_dump("cap_w3cbc", [(None, w3cbc[:, 0:512])], (128, 512))
            dbg_dump("cap_iqsbc", [(None, iqsbc[:, 0:512])], (128, 512))
            dbg_dump("cap_gpark", [(None, gpark[:, 0:512])], (128, 512))
            dbg_dump("cap_sgb", [(None, sgb[:, 0:512])], (128, 512))
            dbg_dump("cap_macc", [(None, macc[:, 0:512])], (128, 512))
            dbg_dump("cap_rowbuf", [(None, sb["rowbuf"][0:1, 0:512])], (1, 512))
            dbg_dump("cap_q2", [(None, mid2[:, 0:512])], (128, 512), BF16)
            dbg_dump("cap_xqh", [(None, xqh[:, 0:512])], (128, 512), BF16)

    return finish()


def kernel(**inputs):
    in_maps = _prep_inputs(inputs)
    nc = build_program()
    res = run_bass_kernel_spmd(nc, in_maps, core_ids=list(range(NCORES)))
    out = np.concatenate([np.asarray(r["y_out"]) for r in res.results], axis=0)
    return out.reshape(1, SEQ, D).astype(np.float32)


# version 04
# speedup vs baseline: 1.2422x; 1.0036x over previous
import numpy as np
from contextlib import ExitStack

import concourse.bass as bass
import concourse.mybir as mybir
from concourse.bass_utils import run_bass_kernel_spmd

F32 = mybir.dt.float32
F32R = mybir.dt.float32r
BF16 = mybir.dt.bfloat16
I32 = mybir.dt.int32
AF = mybir.ActivationFunctionType
ALU = mybir.AluOpType
AX = mybir.AxisListType.X

NCORES = 8
SEQ = 8192
T = 1024
TH = T + 3
D = 2048
DI = 4096
NH = 64
HP = 64
INTER = 5632
CH = 8
KD = 16
KDI = 32
KI = 44
KI_PAD = 48
EPS_LN = 1e-5
EPS_RMS = 1e-6
EPS_FFN = 1e-5
NCHX = 35
SH12 = 6 * KD * 128 * 128
SH3 = 2 * KI * 128 * 128

_cache = {}


def _prep_inputs(inp):
    import ml_dtypes
    x = np.asarray(inp["L0_attn_hidden_states"]).reshape(SEQ, D).astype(np.float32, copy=False)
    Win = np.asarray(inp["L0_attn_in_proj_w"], dtype=np.float32)
    Wout = np.asarray(inp["L0_attn_out_proj_w"], dtype=np.float32)
    Wg = np.asarray(inp["L0_attn_gate_w"], dtype=np.float32)
    Wu = np.asarray(inp["L0_attn_up_w"], dtype=np.float32)
    Wd = np.asarray(inp["L0_attn_down_w"], dtype=np.float32)
    w1 = np.asarray(inp["L0_ffn_w1"], dtype=np.float32)
    w2 = np.asarray(inp["L0_ffn_w2"], dtype=np.float32)
    w3 = np.asarray(inp["L0_ffn_w3"], dtype=np.float32)

    def btile(w, K, N):
        return np.ascontiguousarray(w.reshape(K // 128, 128, N // 128, 128).transpose(2, 0, 1, 3))

    w_xbcdt_cols = np.concatenate([
        Win[:, 8448:8512], np.zeros((D, 64), np.float32),
        Win[:, 8192:8320], Win[:, 8320:8448], Win[:, 4096:8192]], axis=1)
    w_xbcdt = btile(w_xbcdt_cols, D, NCHX * 128)
    w_z = btile(Win[:, 0:4096], D, DI)
    w_out = btile(Wout, DI, D)
    w_gate = btile(Wg, D, INTER)
    w_up = btile(Wu, D, INTER)
    w_down = btile(Wd, INTER, D)
    w1p = np.concatenate([np.ascontiguousarray(w1.T),
                          np.zeros((D, KI_PAD * 128 - INTER), np.float32)], 1)
    w2p = np.concatenate([np.ascontiguousarray(w2.T),
                          np.zeros((D, KI_PAD * 128 - INTER), np.float32)], 1)
    w1t = btile(w1p, D, KI_PAD * 128)
    w2t = btile(w2p, D, KI_PAD * 128)
    w3t = btile(np.ascontiguousarray(w3.T), INTER, D)

    conv_w = np.asarray(inp["L0_attn_conv_w"], dtype=np.float32)
    conv_b = np.asarray(inp["L0_attn_conv_b"], dtype=np.float32)
    cw = np.ascontiguousarray(conv_w.reshape(34, 128, 4).transpose(1, 0, 2).reshape(128, 136))
    cb = np.ascontiguousarray(conv_b.reshape(34, 128).T)

    consts = dict(
        ident_f=np.eye(128, dtype=np.float32),
        ident_b=np.eye(128, dtype=np.float32).astype(ml_dtypes.bfloat16),
        tri01=np.tril(np.ones((128, 128), np.float32)).T.astype(ml_dtypes.bfloat16),
        cw=cw, cb=cb,
        dtb=np.asarray(inp["L0_attn_dt_bias"], np.float32).reshape(64, 1),
        alog=np.asarray(inp["L0_attn_A_log"], np.float32).reshape(64, 1),
        drow=np.asarray(inp["L0_attn_D_param"], np.float32).reshape(1, 64),
        norm1=np.asarray(inp["L0_attn_norm1_w"], np.float32).reshape(1, D),
        norm2=np.asarray(inp["L0_attn_norm2_w"], np.float32).reshape(1, D),
        mixw=np.asarray(inp["L0_attn_mixer_norm_w"], np.float32).reshape(1, DI),
    )

    in_maps = []
    for k in range(NCORES):
        lo = k * T
        xs_sl = np.zeros((TH, D), np.float32)
        if k == 0:
            xs_sl[3:] = x[0:T]
        else:
            xs_sl[:] = x[lo - 3:lo + T]
        sel = np.zeros((8, 8), np.float32)
        low = np.zeros((8, 1), np.float32)
        for j in range(8):
            if j < k:
                low[j, 0] = 1.0
            for i in range(8):
                if j < i < k:
                    sel[i, j] = 1.0
        in_maps.append(dict(
            x_slice=np.ascontiguousarray(xs_sl),
            w_xbcdt=w_xbcdt, w_z=w_z, w_out=w_out,
            w_gate=w_gate, w_up=w_up, w_down=w_down,
            w1s=np.ascontiguousarray(w1t[6 * k:6 * k + 6]),
            w2s=np.ascontiguousarray(w2t[6 * k:6 * k + 6]),
            w3s=np.ascontiguousarray(w3t[2 * k:2 * k + 2]),
            sel=sel, low=low, **consts))
    return in_maps


class Arena:
    def __init__(self, t, elems, name):
        self.t, self.elems, self.cur, self.name = t, elems, 0, name

    def alloc(self, n_elems, dt):
        nb = n_elems * (2 if dt == BF16 else 4) // 2
        nb = (nb + 15) // 16 * 16
        off = self.cur
        assert off + nb <= self.elems, f"{self.name} overflow {off}+{nb}>{self.elems}"
        self.cur = off + nb
        sl = self.t[:, off:off + nb]
        return sl if dt == BF16 else sl.bitcast(dt)

    def reset(self, to=0):
        self.cur = to


def build_program(dbg=(), upto=None):
    key = (tuple(sorted(dbg)), upto)
    if key in _cache:
        return _cache[key]
    nc = bass.Bass("TRN2", target_bir_lowering=False)
    stack = ExitStack()
    sem, cnt = {}, {}
    for n in ["dW", "dG", "T", "V", "S", "G", "CC"]:
        sem[n] = stack.enter_context(nc.semaphore(n))
        cnt[n] = 0
    dbgset = set(dbg)

    def inc(ins, s, n=1):
        cnt[s] += n
        ins.then_inc(sem[s], n)
        return cnt[s]

    def vf(v, ins):
        val = inc(ins, "V")
        v.wait_ge(sem["V"], val)

    def sf(s, ins):
        val = inc(ins, "S")
        s.wait_ge(sem["S"], val)

    dram = {}
    for name, shape, dt in [
            ("x_slice", (TH, D), F32), ("w_xbcdt", (NCHX, KD, 128, 128), F32),
            ("w_z", (KDI, KD, 128, 128), F32), ("w_out", (KD, KDI, 128, 128), F32),
            ("w_gate", (KI, KD, 128, 128), F32), ("w_up", (KI, KD, 128, 128), F32),
            ("w_down", (KD, KI, 128, 128), F32),
            ("w1s", (6, KD, 128, 128), F32), ("w2s", (6, KD, 128, 128), F32),
            ("w3s", (2, KI, 128, 128), F32),
            ("sel", (8, 8), F32), ("low", (8, 1), F32),
            ("ident_f", (128, 128), F32), ("ident_b", (128, 128), BF16),
            ("tri01", (128, 128), BF16), ("cw", (128, 136), F32), ("cb", (128, 34), F32),
            ("dtb", (64, 1), F32), ("alog", (64, 1), F32), ("drow", (1, 64), F32),
            ("norm1", (1, D), F32), ("norm2", (1, D), F32), ("mixw", (1, DI), F32)]:
        dram[name] = nc.dram_tensor(name, list(shape), dt, kind="ExternalInput").ap()
    y_out = nc.dram_tensor("y_out", [T, D], F32, kind="ExternalOutput").ap()

    scr = {}
    for name, shape, dt in [
            ("xn_scr", (KD, 128, T), BF16), ("xq_scr", (KD, 128, T), BF16),
            ("hin_scr", (KD, 128, TH), BF16), ("h1_scr", (KD, 128, T), BF16),
            ("s_scr", (KD, 128, T), BF16), ("acum_scr", (64, T), F32),
            ("eal_scr", (8, 512), F32), ("row_scr", (8, 1024), F32),
            ("cvals_scr", (8, 64), F32),
            ("cc2_in", (1, 8), F32), ("cc2_out", (8, 8), F32),
            ("cc3_in", (1, 2 * SH12 + SH3), BF16), ("cc3_out", (8, 2 * SH12 + SH3), BF16),
            ("cc4_in", (1, 128 * 4096 + 64), F32), ("cc4_out", (8, 128 * 4096 + 64), F32)]:
        scr[name] = nc.dram_tensor(name, list(shape), dt).ap()
    cc3o = scr["cc3_out"]

    def wq_group(which, ncu, kg, KG):
        if which in (1, 2):
            core, loc = divmod(ncu, 6)
            off = (0 if which == 1 else SH12) + (loc * KD + kg * KG) * 16384
        else:
            core, loc = divmod(ncu, 2)
            off = 2 * SH12 + (loc * KI + kg * KG) * 16384
        n_el = KG * 16384
        return cc3o[core, off:off + n_el].rearrange("(k p f) -> p k f", p=128, k=KG)

    sb = {}
    def sbuf(name, shape, dt):
        sb[name] = stack.enter_context(nc.sbuf_tensor("sb_" + name, list(shape), dt))
        return sb[name]

    for name, shape, dt in [
            ("ident_f", (128, 128), F32), ("ident_b", (128, 128), BF16),
            ("tri01", (128, 128), BF16), ("cw", (128, 136), F32), ("cb", (128, 34), F32),
            ("dtb", (64, 1), F32), ("alog", (64, 1), F32), ("negA", (64, 1), F32),
            ("Dbc", (128, 64), F32), ("cols", (128, 16), F32),
            ("iqs_cols", (128, 8), F32),
            ("wsc", (4, 8), F32), ("wsc_col", (128, 4), F32), ("wsci_col", (128, 4), F32),
            ("lndt_tm", (128, CH * 64), F32), ("acum_tm", (128, CH * 64), F32),
            ("ea_tm", (128, CH * 64), F32), ("ea2_tm", (128, CH * 64), F32),
            ("sd_tm", (128, CH * 64), F32), ("ealrep", (128, CH * 64), F32),
            ("prefx", (64, 24), F32), ("ealst", (64, 8), F32),
            ("B_fm", (128, T), BF16), ("C_fm", (128, T), BF16), ("B_tm", (128, T), BF16),
            ("Gt", (128, CH * 128), BF16),
            ("Pbf", (128, 256), BF16), ("xdtd", (128, 2 * 512), BF16),
            ("ptmp", (128, 64), F32),
            ("selw", (8, 8), F32), ("loww", (8, 1), F32),
            ("svals", (8, 64), F32), ("cvals", (8, 64), F32),
            ("onesc", (128, 8), F32),
            ("rowbuf", (1, 1024), F32),
            ("kc", (128, 8), F32),
            ("dump", (128, 512), F32)]:
        sbuf(name, shape, dt)

    ARA = Arena(sbuf("ARA", (128, 32912), BF16), 32912, "A")
    ARB = Arena(sbuf("ARB", (128, 32800), BF16), 32800, "B")
    ARC = Arena(sbuf("ARC", (128, 11400), BF16), 11400, "C")
    ARD = Arena(sbuf("ARD", (128, 8200), BF16), 8200, "D")

    psA = stack.enter_context(nc.psum_tensor("psA", [128, 2048], F32))
    psB = stack.enter_context(nc.psum_tensor("psB", [128, 2048], F32))
    psAb = psA.bitcast(BF16)
    psBb = psB.bitcast(BF16)

    block = stack.enter_context(nc.Block())

    def barrier():
        vals = dict(cnt)

        def w(eng):
            for s in ["T", "V", "S", "G", "CC", "dW", "dG"]:
                if vals[s]:
                    eng.wait_ge(sem[s], vals[s])

        @block.tensor
        def _(te):
            w(te)

        @block.vector
        def _(v):
            w(v)

        @block.scalar
        def _(s):
            w(s)

        @block.gpsimd
        def _(g):
            w(g)

        @block.sync
        def _(sy):
            w(sy)

    def dbg_dump(tag, srcs, shape, dt=F32):
        if tag not in dbgset:
            return
        o = nc.dram_tensor("dbg_" + tag, list(shape), dt, kind="ExternalOutput").ap()

        @block.sync
        def _(sy):
            for s in ["T", "V", "S", "G", "CC"]:
                if cnt[s]:
                    sy.wait_ge(sem[s], cnt[s])
            for osl, src in srcs:
                inc(sy.dma_start(out=o[osl] if osl is not None else o, in_=src), "dW", 16)

    def finish():
        @block.sync
        def _(sy):
            for s in ["T", "V", "S", "G", "CC", "dG", "dW"]:
                if cnt[s]:
                    sy.wait_ge(sem[s], cnt[s])
        stack.close()
        _cache[key] = nc
        return nc

    def bgemm(NC, KTOT, KG, wsrcs, wdt, cast, rhs_fn, ncols, regions, evac_fn,
              bufs, pre_wait_T=None, skip_n=None):
        NGR = KTOT // KG
        npair = len(wsrcs)
        t_grp = {}
        v_ncu = {}
        width = ncols * 512 * npair
        gidx = 0
        for ncu in range(NC):
            if skip_n and ncu in skip_n:
                continue
            reg = regions[ncu % len(regions)]
            ps, off = reg
            for kg in range(NGR):
                par = gidx % 2
                if cast:
                    for pi in range(npair):
                        @block.gpsimd
                        def _(g, ncu=ncu, kg=kg, par=par, pi=pi, gidx=gidx):
                            if gidx >= 2:
                                g.wait_ge(sem["T"], t_grp[gidx - 2])
                            ins = g.dma_start(out=bufs[pi][2 + par][:], in_=wsrcs[pi](ncu, kg))
                            inc(ins, "dG", 16)
                            g.wait_ge(sem["dG"], cnt["dG"])
                else:
                    for pi in range(npair):
                        @block.sync
                        def _(sy, ncu=ncu, kg=kg, par=par, pi=pi, gidx=gidx):
                            if gidx >= 2:
                                sy.wait_ge(sem["T"], t_grp[gidx - 2])
                            if gidx == 0 and pre_wait_T is not None:
                                sy.wait_ge(sem["CC"], cnt["CC"])
                            ins = sy.dma_start(out=bufs[pi][par][:], in_=wsrcs[pi](ncu, kg))
                            inc(ins, "dW", 16)
                            sy.wait_ge(sem["dW"], cnt["dW"])
                ld = cnt["dW"]
                gld = cnt["dG"]

                @block.tensor
                def _(te, ncu=ncu, kg=kg, par=par, ps=ps, off=off, ld=ld, gld=gld,
                      gidx=gidx):
                    if cast:
                        te.wait_ge(sem["dG"], gld)
                    else:
                        te.wait_ge(sem["dW"], ld)
                    if pre_wait_T is not None and ncu < len(regions) and kg == 0:
                        pre_wait_T(te)
                    if kg == 0 and len(v_ncu) >= len(regions):
                        prev = ncu - len(regions)
                        while prev not in v_ncu and prev >= 0:
                            prev -= 1
                        if prev in v_ncu:
                            te.wait_ge(sem["V"], v_ncu[prev])
                    for ko in range(KG):
                        k = kg * KG + ko
                        st, sp = (k == 0), (k == KTOT - 1)
                        last = (ko == KG - 1)
                        for pi in range(npair):
                            if cast:
                                lt = bufs[pi][2 + par][:, ko * 128:ko * 128 + 128]
                            elif wdt == F32R:
                                lt = bufs[pi][par].bitcast(F32R)[:, ko * 128:ko * 128 + 128]
                            else:
                                lt = bufs[pi][par][:, ko * 128:ko * 128 + 128]
                            for cb in range(ncols):
                                o2 = off + (pi * ncols + cb) * 512
                                ins = te.matmul(ps[:, o2:o2 + 512], lt, rhs_fn(pi, k, cb),
                                                start=st, stop=sp)
                                if sp or last:
                                    inc(ins, "T")
                    t_grp[gidx] = cnt["T"]
                gidx += 1
            v_ncu[ncu] = evac_fn(ncu, reg)

    @block.sync
    def _(sy):
        for n in ["ident_f", "ident_b", "tri01", "cw", "cb", "dtb", "alog"]:
            inc(sy.dma_start(out=sb[n][:], in_=dram[n]), "dW", 16)
        inc(sy.dma_start(out=sb["selw"][:], in_=dram["sel"]), "dW", 16)
        inc(sy.dma_start(out=sb["loww"][:], in_=dram["low"]), "dW", 16)
        inc(sy.dma_start(out=sb["Dbc"][:], in_=dram["drow"].broadcast_to([128, 64])), "dW", 16)
        sy.wait_ge(sem["dW"], cnt["dW"])
    ph0 = cnt["dW"]

    norm1bc = ARC.alloc(D, F32)
    qt_in = ARC.alloc(2 * 512, F32)
    qt_bf = ARC.alloc(2 * 512, BF16)
    acc_abs = ARC.alloc(8, F32)

    @block.sync
    def _(sy):
        inc(sy.dma_start(out=norm1bc[:], in_=dram["norm1"].broadcast_to([128, D])), "dW", 16)
        sy.wait_ge(sem["dW"], cnt["dW"])

    @block.vector
    def _(v):
        v.wait_ge(sem["dW"], ph0)
        v.memset(sb["cols"][:], 0.0)
        v.memset(acc_abs[:, 0:8], 0.0)
        v.memset(sb["onesc"][:], 1.0)
        KC = sb["kc"]
        v.memset(KC[:, 0:1], EPS_LN)
        v.memset(KC[:, 1:2], EPS_RMS)
        v.memset(KC[:, 2:3], 1.0)
        v.memset(KC[:, 3:4], -1.0)
        v.memset(KC[:, 4:5], 1.0 / DI)
        v.memset(KC[:, 5:6], 1.0 / D)
        v.memset(KC[:, 6:7], 1.0 / INTER)
        inc(v.memset(sb["Pbf"][:], 0.0), "V")

    @block.scalar
    def _(s):
        s.wait_ge(sem["dW"], ph0)
        inc(s.activation(out=sb["dump"][0:64, 0:1], in_=sb["alog"][:], func=AF.Exp), "S")

    @block.vector
    def _(v):
        v.wait_ge(sem["S"], cnt["S"])
        inc(v.tensor_scalar_mul(out=sb["negA"][:], in0=sb["dump"][0:64, 0:1],
                                scalar1=-1.0), "V")

    shard_list = [("w1s", 24, 0), ("w2s", 24, 1), ("w3s", 22, 2)]
    vmi = []
    for name, ntile, ti in shard_list:
        for i in range(ntile):
            gi = len(vmi)

            @block.sync
            def _(sy, name=name, i=i, gi=gi):
                if gi >= 2:
                    sy.wait_ge(sem["V"], vmi[gi - 2])
                ins = sy.dma_start(out=qt_in[:, (gi % 2) * 512:(gi % 2) * 512 + 512],
                                   in_=dram[name].rearrange("a b p f -> p (a b) f")
                                   [:, i * 4:i * 4 + 4, :])
                inc(ins, "dW", 16)
                sy.wait_ge(sem["dW"], cnt["dW"])

            @block.vector
            def _(v, gi=gi, ti=ti):
                v.wait_ge(sem["dW"], cnt["dW"])
                sl = qt_in[:, (gi % 2) * 512:(gi % 2) * 512 + 512]
                vf(v, v.reduce_sum(out=sb["dump"][:, 0:1], in_=sl, axis=AX,
                                   apply_absolute_value=True))
                ins = v.tensor_add(out=acc_abs[:, ti:ti + 1], in0=sb["dump"][:, 0:1],
                                   in1=acc_abs[:, ti:ti + 1])
                vmi.append(inc(ins, "V"))

    @block.tensor
    def _(te):
        te.wait_ge(sem["V"], cnt["V"])
        inc(te.matmul(psA[0:4, 0:1], acc_abs[:, 0:4], sb["onesc"][:, 0:1],
                      start=True, stop=True), "T")

    @block.vector
    def _(v):
        v.wait_ge(sem["T"], cnt["T"])
        inc(v.tensor_copy(out=sb["wsc"][0:4, 0:1], in_=psA[0:4, 0:1]), "V")

    @block.gpsimd
    def _(g):
        g.wait_ge(sem["V"], cnt["V"])
        inc(g.dma_start(out=scr["cc2_in"][:, 0:4], in_=sb["wsc"][0:4, 0:1]), "dG", 16)
        g.wait_ge(sem["dG"], cnt["dG"])
        inc(g.collective_compute("AllGather", ALU.bypass,
                                 replica_groups=[list(range(NCORES))],
                                 ins=[scr["cc2_in"]], outs=[scr["cc2_out"]]), "CC")
        g.wait_ge(sem["CC"], cnt["CC"])
        inc(g.dma_start(out=sb["wsc"][0:4, 0:8],
                        in_=scr["cc2_out"].rearrange("a b -> b a")[0:4, :],
                        allow_slow_non_contiguous=True), "dG", 16)

    @block.vector
    def _(v):
        v.wait_ge(sem["dG"], cnt["dG"])
        vf(v, v.reduce_sum(out=sb["dump"][0:4, 0:1], in_=sb["wsc"][0:4, 0:8], axis=AX))
        vf(v, v.tensor_scalar(out=sb["wsc"][0:3, 1:2], in0=sb["dump"][0:3, 0:1],
                              scalar1=1.0 / (INTER * D), scalar2=EPS_FFN,
                              op0=ALU.mult, op1=ALU.max))
        inc(v.reciprocal(out=sb["wsc"][0:3, 2:3], in_=sb["wsc"][0:3, 1:2]), "V")

    @block.gpsimd
    def _(g):
        g.wait_ge(sem["V"], cnt["V"])
        inc(g.dma_start(out=scr["eal_scr"][0, 0:3], in_=sb["wsc"][0:3, 2:3]), "dG", 16)
        inc(g.dma_start(out=scr["eal_scr"][1, 0:3], in_=sb["wsc"][0:3, 1:2]), "dG", 16)
        g.wait_ge(sem["dG"], cnt["dG"])
        for t in range(3):
            inc(g.dma_start(out=sb["wsc_col"][:, t:t + 1],
                            in_=scr["eal_scr"][0:1, t:t + 1].broadcast_to([128, 1])), "dG", 16)
            inc(g.dma_start(out=sb["wsci_col"][:, t:t + 1],
                            in_=scr["eal_scr"][1:2, t:t + 1].broadcast_to([128, 1])), "dG", 16)
    g_scales = cnt["dG"]

    qoff = [0, SH12, 2 * SH12]
    vmi2 = []
    for name, ntile, ti in shard_list:
        for i in range(ntile):
            gi = len(vmi2)

            @block.sync
            def _(sy, name=name, i=i, gi=gi):
                if gi >= 2:
                    sy.wait_ge(sem["V"], vmi2[gi - 2])
                ins = sy.dma_start(out=qt_in[:, (gi % 2) * 512:(gi % 2) * 512 + 512],
                                   in_=dram[name].rearrange("a b p f -> p (a b) f")
                                   [:, i * 4:i * 4 + 4, :])
                inc(ins, "dW", 16)
                sy.wait_ge(sem["dW"], cnt["dW"])

            @block.vector
            def _(v, gi=gi, ti=ti):
                v.wait_ge(sem["dW"], cnt["dW"])
                if gi == 0:
                    v.wait_ge(sem["dG"], g_scales)
                sl = qt_in[:, (gi % 2) * 512:(gi % 2) * 512 + 512]
                vf(v, v.tensor_scalar_mul(out=sb["dump"][:], in0=sl,
                                          scalar1=sb["wsc_col"][:, ti:ti + 1]))
                vf(v, v.tensor_copy(out=sl.bitcast(I32), in_=sb["dump"][:]))
                vf(v, v.tensor_copy(out=sb["dump"][:], in_=sl.bitcast(I32)))
                ins = v.tensor_scalar(out=qt_bf[:, (gi % 2) * 512:(gi % 2) * 512 + 512],
                                      in0=sb["dump"][:], scalar1=-1.0, scalar2=1.0,
                                      op0=ALU.max, op1=ALU.min)
                vmi2.append(inc(ins, "V"))

            @block.gpsimd
            def _(g, i=i, gi=gi, ti=ti):
                g.wait_ge(sem["V"], vmi2[gi])
                off = qoff[ti] + i * 65536
                ins = g.dma_start(
                    out=scr["cc3_in"][0, off:off + 65536]
                    .rearrange("(g p f) -> p g f", g=4, p=128),
                    in_=qt_bf[:, (gi % 2) * 512:(gi % 2) * 512 + 512])
                inc(ins, "dG", 16)

    @block.gpsimd
    def _(g):
        g.wait_ge(sem["dG"], cnt["dG"])
        inc(g.collective_compute("AllGather", ALU.bypass,
                                 replica_groups=[list(range(NCORES))],
                                 ins=[scr["cc3_in"]], outs=[scr["cc3_out"]]), "CC")
    cc3_done = cnt["CC"]
    dbg_dump("tern", [((t,), cc3o[0, t * 16384:(t + 1) * 16384]
                       .rearrange("(p f) -> p f", p=128)) for t in range(16)],
             (16, 128, 128), BF16)
    dbg_dump("wsc", [((0,), sb["wsc_col"][:, 0:4]), ((1,), sb["wsci_col"][:, 0:4])],
             (2, 128, 4))

    xa_s = [ARD.alloc(D, F32), ARB.alloc(D, F32)]
    xb_s = [ARD.alloc(D, F32), ARB.alloc(D, F32)]
    xqb_s = [ARC.alloc(D, BF16), ARB.alloc(D, BF16)]
    xnb_s = [ARC.alloc(D, BF16), ARB.alloc(D, BF16)]

    hin = ARA.alloc(KD * TH + 8, BF16)
    hinr = hin
    CO = sb["cols"]
    vlast, tlast = {}, {}

    def p0_tile(j, rows, row0):
        colb = 3 + 128 * j if j < 8 else 0
        r = slice(0, rows)
        xa, xb = xa_s[j % 2], xb_s[j % 2]
        xqb, xnb = xqb_s[j % 2], xnb_s[j % 2]

        @block.sync
        def _(sy):
            if j >= 2:
                sy.wait_ge(sem["V"], vlast[j - 2])
                sy.wait_ge(sem["T"], tlast[j - 2])
            inc(sy.dma_start(out=xa[r, :], in_=dram["x_slice"][row0:row0 + rows, :]), "dW", 16)

        @block.vector
        def _(v):
            v.wait_ge(sem["dW"], cnt["dW"])
            vf(v, v.reduce_sum(out=CO[r, 0:1], in_=xa[r, :], axis=AX))
            vf(v, v.tensor_scalar_mul(out=CO[r, 1:2], in0=CO[r, 0:1], scalar1=1.0 / D))
            v.tensor_scalar(out=xb[r, :], in0=xa[r, :], scalar1=CO[r, 1:2],
                            scalar2=None, op0=ALU.subtract)
            v.tensor_mul(out=xa[r, 0:1024], in0=xb[r, 0:1024], in1=xb[r, 0:1024])
            vf(v, v.reduce_sum(out=CO[r, 0:1], in_=xa[r, 0:1024], axis=AX))
            v.tensor_mul(out=xa[r, 0:1024], in0=xb[r, 1024:2048], in1=xb[r, 1024:2048])
            vf(v, v.reduce_sum(out=CO[r, 2:3], in_=xa[r, 0:1024], axis=AX))
            vf(v, v.tensor_add(out=CO[r, 6:7], in0=CO[r, 2:3], in1=CO[r, 0:1]))
            inc(v.tensor_scalar(out=CO[r, 3:4], in0=CO[r, 6:7], scalar1=1.0 / D,
                                scalar2=None, op0=ALU.mult), "V")

        @block.scalar
        def _(s):
            s.wait_ge(sem["V"], cnt["V"])
            inc(s.activation(out=CO[r, 8:9], in_=CO[r, 3:4], func=AF.Sqrt,
                             bias=sb["kc"][r, 0:1]), "S")

        @block.vector
        def _(v):
            v.wait_ge(sem["S"], cnt["S"])
            vf(v, v.reciprocal(out=CO[r, 2:3], in_=CO[r, 8:9]))
            vf(v, v.tensor_mul(out=CO[r, 9:10], in0=CO[r, 2:3], in1=CO[r, 2:3]))
            inc(v.tensor_mul(out=CO[r, 10:11], in0=CO[r, 9:10], in1=CO[r, 3:4]), "V")

        @block.scalar
        def _(s):
            s.wait_ge(sem["V"], cnt["V"])
            inc(s.activation(out=CO[r, 11:12], in_=CO[r, 10:11], func=AF.Sqrt,
                             bias=sb["kc"][r, 1:2]), "S")

        @block.vector
        def _(v):
            v.wait_ge(sem["S"], cnt["S"])
            v.wait_ge(sem["dW"], cnt["dW"])
            vf(v, v.reciprocal(out=CO[r, 4:5], in_=CO[r, 11:12]))
            vf(v, v.tensor_mul(out=CO[r, 5:6], in0=CO[r, 2:3], in1=CO[r, 4:5]))
            if j < 8:
                v.tensor_scalar_mul(out=xnb[r, :], in0=xb[r, :], scalar1=CO[r, 2:3])
                vf(v, v.reduce_max(out=CO[r, 6:7], in_=xb[r, :], axis=AX,
                                   apply_absolute_value=True))
                vf(v, v.tensor_mul(out=CO[r, 12:13], in0=CO[r, 6:7], in1=CO[r, 5:6]))
                vf(v, v.tensor_scalar_max(out=CO[r, 13:14], in0=CO[r, 12:13], scalar1=EPS_FFN))
                vf(v, v.reciprocal(out=CO[r, 14:15], in_=CO[r, 13:14]))
                v.tensor_scalar_mul(out=sb["iqs_cols"][r, j:j + 1], in0=CO[r, 13:14],
                                    scalar1=1.0 / 127.0)
                vf(v, v.tensor_scalar_mul(out=CO[r, 15:16], in0=CO[r, 14:15], scalar1=127.0))
                vf(v, v.tensor_mul(out=CO[r, 7:8], in0=CO[r, 5:6], in1=CO[r, 15:16]))
                for hh in range(4):
                    hsl = slice(hh * 512, hh * 512 + 512)
                    vf(v, v.tensor_scalar_mul(out=xa[r, 0:512], in0=xb[r, hsl],
                                              scalar1=CO[r, 7:8]))
                    vf(v, v.tensor_copy(out=xa[r, 512:1024].bitcast(I32), in_=xa[r, 0:512]))
                    vf(v, v.tensor_copy(out=xa[r, 0:512], in_=xa[r, 512:1024].bitcast(I32)))
                    vf(v, v.tensor_scalar(out=xqb[r, hsl], in0=xa[r, 0:512], scalar1=-128.0,
                                          scalar2=127.0, op0=ALU.max, op1=ALU.min))
            v.tensor_scalar_mul(out=xa[r, :], in0=xb[r, :], scalar1=CO[r, 5:6])
            inc(v.tensor_mul(out=xb[r, :], in0=xa[r, :], in1=norm1bc[r, :]), "V")
        v_done = cnt["V"]

        @block.tensor
        def _(te):
            te.wait_ge(sem["V"], v_done)
            for fc in range(KD):
                inc(te.transpose(psA[:, fc * 128:fc * 128 + rows],
                                 xb[r, fc * 128:fc * 128 + 128],
                                 sb["ident_f"][0:rows, 0:rows]), "T")

        @block.vector
        def _(v):
            v.wait_ge(sem["T"], cnt["T"])
            ins = None
            for fc in range(KD):
                ins = v.tensor_copy(out=hinr[:, fc * TH + colb:fc * TH + colb + rows],
                                    in_=psA[:, fc * 128:fc * 128 + rows])
            inc(ins, "V")

        if j < 8:
            @block.tensor
            def _(te):
                te.wait_ge(sem["V"], cnt["V"])
                for fc in range(KD):
                    inc(te.transpose(psBb[:, fc * 128:fc * 128 + 128],
                                     xnb[:, fc * 128:fc * 128 + 128],
                                     sb["ident_b"][:]), "T")

            @block.vector
            def _(v):
                v.wait_ge(sem["T"], cnt["T"])
                inc(v.tensor_copy(out=xnb[:], in_=psBb[:, 0:2048]), "V")
            v_xn = cnt["V"]

            @block.sync
            def _(sy):
                sy.wait_ge(sem["V"], v_xn)
                for fc in range(KD):
                    inc(sy.dma_start(out=scr["xn_scr"][fc, :, j * 128:j * 128 + 128],
                                     in_=xnb[:, fc * 128:fc * 128 + 128]), "dW", 16)

            @block.tensor
            def _(te):
                te.wait_ge(sem["V"], cnt["V"])
                te.wait_ge(sem["dW"], cnt["dW"])
                for fc in range(KD):
                    inc(te.transpose(psBb[:, fc * 128:fc * 128 + 128],
                                     xqb[:, fc * 128:fc * 128 + 128],
                                     sb["ident_b"][:]), "T")

            @block.vector
            def _(v):
                v.wait_ge(sem["T"], cnt["T"])
                inc(v.tensor_copy(out=xqb[:], in_=psBb[:, 0:2048]), "V")
            v_xq = cnt["V"]

            @block.sync
            def _(sy):
                sy.wait_ge(sem["V"], v_xq)
                for fc in range(KD):
                    inc(sy.dma_start(out=scr["xq_scr"][fc, :, j * 128:j * 128 + 128],
                                     in_=xqb[:, fc * 128:fc * 128 + 128]), "dW", 16)
        vlast[j] = cnt["V"]
        tlast[j] = cnt["T"]

    for j in range(8):
        p0_tile(j, 128, 3 + 128 * j)
    p0_tile(8, 3, 0)
    p0_hin_V = cnt["V"]

    dbg_dump("hin", [((fc,), hin[:, fc * TH:(fc + 1) * TH]) for fc in range(KD)],
             (KD, 128, TH), BF16)
    dbg_dump("xn", [((fc,), scr["xn_scr"][fc]) for fc in range(KD)], (KD, 128, T), BF16)
    dbg_dump("co", [(None, CO[:])], (128, 16))
    dbg_dump("xq", [((fc,), scr["xq_scr"][fc]) for fc in range(KD)], (KD, 128, T), BF16)
    dbg_dump("iqs", [(None, sb["iqs_cols"][:])], (128, 8))
    if upto == "p0":
        return finish()

    barrier()
    ARD.reset(0)
    dt_fm = ARD.alloc(T, F32)
    acum_fm = ARD.alloc(T, F32)
    tmp_fm = ARD.alloc(T, F32)
    tmp2_fm = ARD.alloc(T, F32)

    ARC.reset(0)
    wbuf = [ARC.alloc(8 * 128, BF16), ARC.alloc(8 * 128, BF16)]
    convacc = ARC.alloc(T, F32)
    convac2 = ARC.alloc(T, F32)
    convo = ARC.alloc(T, BF16)

    ARB.reset(0)
    xs_tm = ARB.alloc(CH * DI, BF16)

    t_ms = {}
    v_free = {}

    def emit_tmcols(src_fm, dst, wait_sem, wait_val):
        @block.tensor
        def _(te):
            te.wait_ge(sem[wait_sem], wait_val)
            te.wait_ge(sem["V"], cnt["V"])
            for c in range(CH):
                inc(te.transpose(psB[:, 1536 + c * 64:1536 + c * 64 + 64],
                                 src_fm[0:64, c * 128:c * 128 + 128],
                                 sb["ident_f"][0:64, 0:64]), "T")

        @block.vector
        def _(v):
            v.wait_ge(sem["T"], cnt["T"])
            inc(v.tensor_copy(out=dst[:], in_=psB[:, 1536:2048]), "V")

    tg_ms = {}
    for j in range(NCHX):
        par = j % 2
        ps = psA if par == 0 else psB

        for kh in range(2):
            gi = 2 * j + kh

            @block.gpsimd
            def _(g, j=j, kh=kh, gi=gi):
                if gi >= 2:
                    g.wait_ge(sem["T"], tg_ms[gi - 2])
                ins = g.dma_start(
                    out=wbuf[gi % 2][:],
                    in_=dram["w_xbcdt"][j].rearrange("k p n -> p k n")
                    [:, kh * 8:kh * 8 + 8, :])
                inc(ins, "dG", 16)
                g.wait_ge(sem["dG"], cnt["dG"])
            w_ld = cnt["dG"]

            @block.tensor
            def _(te, j=j, kh=kh, gi=gi, ps=ps, w_ld=w_ld):
                te.wait_ge(sem["dG"], w_ld)
                if gi == 0:
                    te.wait_ge(sem["V"], p0_hin_V)
                if kh == 0 and j >= 2:
                    te.wait_ge(sem["V"], v_free[j - 2])
                for ko in range(8):
                    k = kh * 8 + ko
                    lt = wbuf[gi % 2][:, ko * 128:ko * 128 + 128]
                    st, sp = (k == 0), (k == KD - 1)
                    i1 = te.matmul(ps[:, 0:512], lt, hinr[:, k * TH:k * TH + 512],
                                   start=st, stop=sp)
                    i2 = te.matmul(ps[:, 512:1024], lt, hinr[:, k * TH + 512:k * TH + 1024],
                                   start=st, stop=sp)
                    i3 = te.matmul(ps[:, 1024:1028], lt,
                                   hinr[:, k * TH + 1024:k * TH + 1028], start=st, stop=sp)
                    if sp or ko == 7:
                        inc(i1, "T")
                        inc(i2, "T")
                        inc(i3, "T")
                tg_ms[gi] = cnt["T"]
                if kh == 1:
                    t_ms[j] = tg_ms[gi]

        if j == 0:
            @block.scalar
            def _(s):
                s.wait_ge(sem["T"], t_ms[0])
                sf(s, s.activation(out=tmp_fm[0:64, :], in_=psA[0:64, 3:1027], func=AF.Abs,
                                   bias=sb["dtb"][:], scale=1.0))
                inc(s.activation(out=acum_fm[0:64, :], in_=tmp_fm[0:64, :],
                                 func=AF.Exp, scale=sb["kc"][0:64, 3:4]), "S")

            @block.vector
            def _(v):
                v.wait_ge(sem["S"], cnt["S"])
                ins = v.tensor_scalar(out=dt_fm[0:64, :], in0=psA[0:64, 3:1027],
                                      scalar1=sb["dtb"][:], scalar2=0.0,
                                      op0=ALU.add, op1=ALU.max)
                v_free[0] = inc(ins, "V")

            @block.scalar
            def _(s):
                s.wait_ge(sem["V"], cnt["V"])
                inc(s.activation(out=tmp_fm[0:64, :], in_=acum_fm[0:64, :],
                                 func=AF.Ln, bias=sb["kc"][0:64, 2:3]), "S")

            @block.vector
            def _(v):
                v.wait_ge(sem["S"], cnt["S"])
                v.tensor_add(out=dt_fm[0:64, :], in0=tmp_fm[0:64, :], in1=dt_fm[0:64, :])
                v.tensor_scalar_mul(out=tmp2_fm[0:64, :], in0=dt_fm[0:64, :],
                                    scalar1=sb["negA"][:])
                ins = None
                for c in range(CH):
                    ins = v.tensor_tensor_scan(out=acum_fm[0:64, c * 128:c * 128 + 128],
                                               data0=tmp2_fm[0:64, c * 128:c * 128 + 128],
                                               data1=tmp2_fm[0:64, c * 128:c * 128 + 128],
                                               initial=0.0, op0=ALU.add, op1=ALU.bypass)
                inc(ins, "V")
            v_acum = cnt["V"]

            @block.sync
            def _(sy):
                sy.wait_ge(sem["V"], v_acum)
                inc(sy.dma_start(out=scr["acum_scr"], in_=acum_fm[0:64, :]), "dW", 16)

            @block.vector
            def _(v):
                v.wait_ge(sem["V"], v_acum)
                ins = None
                for c in range(CH):
                    ins = v.tensor_copy(out=sb["prefx"][:, c:c + 1],
                                        in_=acum_fm[0:64, c * 128 + 127:c * 128 + 128])
                vf(v, ins)
                vf(v, v.tensor_tensor_scan(out=sb["prefx"][:, 8:16], data0=sb["prefx"][:, 0:8],
                                           data1=sb["prefx"][:, 0:8], initial=0.0,
                                           op0=ALU.add, op1=ALU.bypass))
                v.memset(sb["prefx"][:, 16:17], 0.0)
                inc(v.tensor_copy(out=sb["prefx"][:, 17:24], in_=sb["prefx"][:, 8:15]), "V")

            @block.scalar
            def _(s):
                s.wait_ge(sem["V"], cnt["V"])
                inc(s.activation(out=sb["ealst"][:], in_=sb["prefx"][:, 0:8],
                                 func=AF.Exp), "S")

            @block.sync
            def _(sy):
                sy.wait_ge(sem["S"], cnt["S"])
                inc(sy.dma_start(out=scr["eal_scr"][2:3, :]
                                 .rearrange("a (c h) -> (a h) c", h=64),
                                 in_=sb["ealst"][:],
                                 allow_slow_non_contiguous=True), "dW", 16)
            eal_w = cnt["dW"]

            @block.gpsimd
            def _(g):
                g.wait_ge(sem["dW"], eal_w)
                inc(g.dma_start(out=sb["ealrep"][:],
                                in_=scr["eal_scr"][2:3, :].broadcast_to([128, 512])), "dG", 16)

            @block.scalar
            def _(s):
                s.wait_ge(sem["V"], v_acum)
                inc(s.activation(out=tmp_fm[0:64, :], in_=dt_fm[0:64, :], func=AF.Ln), "S")
            emit_tmcols(tmp_fm, sb["lndt_tm"], "S", cnt["S"])
            emit_tmcols(acum_fm, sb["acum_tm"], "V", v_acum)

            @block.scalar
            def _(s):
                s.wait_ge(sem["V"], cnt["V"])
                inc(s.activation(out=tmp_fm[0:64, :], in_=acum_fm[0:64, :], func=AF.Exp), "S")
            emit_tmcols(tmp_fm, sb["ea_tm"], "S", cnt["S"])

            @block.vector
            def _(v):
                v.wait_ge(sem["V"], cnt["V"])
                ins = None
                for c in range(CH):
                    ins = v.tensor_scalar(out=tmp_fm[0:64, c * 128:c * 128 + 128],
                                          in0=acum_fm[0:64, c * 128:c * 128 + 128],
                                          scalar1=acum_fm[0:64, c * 128 + 127:c * 128 + 128],
                                          scalar2=None, op0=ALU.subtract)
                inc(ins, "V")

            @block.scalar
            def _(s):
                s.wait_ge(sem["V"], cnt["V"])
                inc(s.activation(out=tmp2_fm[0:64, :], in_=tmp_fm[0:64, :],
                                 func=AF.Exp, scale=sb["kc"][0:64, 3:4]), "S")

            @block.vector
            def _(v):
                v.wait_ge(sem["S"], cnt["S"])
                inc(v.tensor_mul(out=tmp_fm[0:64, :], in0=tmp2_fm[0:64, :],
                                 in1=dt_fm[0:64, :]), "V")
            emit_tmcols(tmp_fm, sb["sd_tm"], "V", cnt["V"])

            @block.vector
            def _(v):
                v.wait_ge(sem["V"], cnt["V"])
                ins = None
                for c in range(CH):
                    ins = v.tensor_scalar(out=tmp_fm[0:64, c * 128:c * 128 + 128],
                                          in0=acum_fm[0:64, c * 128:c * 128 + 128],
                                          scalar1=sb["prefx"][:, 16 + c:17 + c],
                                          scalar2=None, op0=ALU.add)
                inc(ins, "V")

            @block.scalar
            def _(s):
                s.wait_ge(sem["V"], cnt["V"])
                inc(s.activation(out=tmp2_fm[0:64, :], in_=tmp_fm[0:64, :], func=AF.Exp), "S")
            emit_tmcols(tmp2_fm, sb["ea2_tm"], "S", cnt["S"])

        elif j in (1, 2):
            tgt = sb["B_fm"] if j == 1 else sb["C_fm"]
            cch = 32 if j == 1 else 33

            @block.vector
            def _(v, ps=ps, cch=cch, j=j):
                v.wait_ge(sem["T"], t_ms[j])
                accs = [convacc[:, 0:T], convac2[:, 0:T]]
                v.tensor_scalar_mul(out=accs[0], in0=ps[:, 0:1024],
                                    scalar1=sb["cw"][:, cch * 4:cch * 4 + 1])
                ins = None
                for tap in range(1, 4):
                    ins = v.scalar_tensor_tensor(out=accs[tap % 2], in0=ps[:, tap:tap + 1024],
                                                 scalar=sb["cw"][:, cch * 4 + tap:cch * 4 + tap + 1],
                                                 in1=accs[(tap - 1) % 2], op0=ALU.mult, op1=ALU.add)
                v_free[j] = inc(ins, "V")

            @block.scalar
            def _(s, tgt=tgt, cch=cch):
                s.wait_ge(sem["V"], cnt["V"])
                inc(s.activation(out=tgt[:, 0:T], in_=convac2[:, 0:T], func=AF.Silu,
                                 bias=sb["cb"][:, cch:cch + 1], scale=1.0), "S")
            s_cv = cnt["S"]

            if j == 1:
                for c in range(CH):
                    @block.tensor
                    def _(te, c=c, s_cv=s_cv):
                        te.wait_ge(sem["S"], s_cv)
                        if c > 0:
                            te.wait_ge(sem["V"], cnt["V"])
                        inc(te.transpose(psAb[:, 3840:3968],
                                         sb["B_fm"][:, c * 128:c * 128 + 128],
                                         sb["ident_b"][:]), "T")

                    @block.vector
                    def _(v, c=c):
                        v.wait_ge(sem["T"], cnt["T"])
                        inc(v.tensor_copy(out=sb["B_tm"][:, c * 128:c * 128 + 128],
                                          in_=psAb[:, 3840:3968]), "V")
        else:
            ci = j - 3

            @block.vector
            def _(v, ps=ps, ci=ci, j=j):
                v.wait_ge(sem["T"], t_ms[j])
                accs = [convacc[:, 0:T], convac2[:, 0:T]]
                v.tensor_scalar_mul(out=accs[0], in0=ps[:, 0:1024],
                                    scalar1=sb["cw"][:, ci * 4:ci * 4 + 1])
                ins = None
                for tap in range(1, 4):
                    ins = v.scalar_tensor_tensor(out=accs[tap % 2], in0=ps[:, tap:tap + 1024],
                                                 scalar=sb["cw"][:, ci * 4 + tap:ci * 4 + tap + 1],
                                                 in1=accs[(tap - 1) % 2], op0=ALU.mult, op1=ALU.add)
                inc(ins, "V")

            @block.scalar
            def _(s, ci=ci):
                s.wait_ge(sem["V"], cnt["V"])
                inc(s.activation(out=convo[:, 0:T], in_=convac2[:, 0:T], func=AF.Silu,
                                 bias=sb["cb"][:, ci:ci + 1], scale=1.0), "S")

            @block.tensor
            def _(te, ps=ps):
                te.wait_ge(sem["S"], cnt["S"])
                pb = ps.bitcast(BF16)
                for c in range(CH):
                    inc(te.transpose(pb[:, c * 128:c * 128 + 128],
                                     convo[:, c * 128:c * 128 + 128],
                                     sb["ident_b"][:]), "T")

            @block.vector
            def _(v, ps=ps, ci=ci, j=j):
                v.wait_ge(sem["T"], cnt["T"])
                pb = ps.bitcast(BF16)
                ins = None
                for c in range(CH):
                    ins = v.tensor_copy(out=xs_tm[:, c * DI + ci * 128:c * DI + ci * 128 + 128],
                                        in_=pb[:, c * 128:c * 128 + 128])
                v_free[j] = inc(ins, "V")

    dbg_dump("dt", [(None, dt_fm[0:64, :])], (64, T))
    dbg_dump("acum", [(None, acum_fm[0:64, :])], (64, T))
    dbg_dump("bfm", [(None, sb["B_fm"][:, 0:T])], (128, T), BF16)
    dbg_dump("cfm", [(None, sb["C_fm"][:, 0:T])], (128, T), BF16)
    dbg_dump("xstm", [((c,), xs_tm[:, c * DI:(c + 1) * DI]) for c in range(CH)],
             (CH, 128, DI), BF16)
    if upto == "p1":
        return finish()

    @block.sync
    def _(sy):
        sy.wait_ge(sem["T"], cnt["T"])
        sy.wait_ge(sem["V"], cnt["V"])
        for fc in range(KD):
            inc(sy.dma_start(out=scr["hin_scr"][fc], in_=hin[:, fc * TH:(fc + 1) * TH]),
                "dW", 16)
    hin_spill = cnt["dW"]

    barrier()
    ARA.reset(0)
    Y_tm = ARA.alloc(CH * DI, BF16)
    ARD.reset(0)
    P = ARD.alloc(NH * HP, F32)
    ARC.reset(0)
    bc = [ARC.alloc(T, F32), ARC.alloc(T, F32)]
    Db = [ARC.alloc(128, F32), ARC.alloc(128, F32)]
    Eb = [ARC.alloc(128, F32), ARC.alloc(128, F32)]
    T1 = [ARC.alloc(128, BF16), ARC.alloc(128, BF16)]
    dgD = [ARC.alloc(128, BF16), ARC.alloc(128, BF16)]

    @block.vector
    def _(v):
        v.wait_ge(sem["dW"], hin_spill)
        inc(v.memset(P[:], 0.0), "V")
    p_init = cnt["V"]

    for c in range(CH):
        @block.tensor
        def _(te, c=c):
            te.wait_ge(sem["V"], p_init if c == 0 else cnt["V"])
            inc(te.matmul(psA[:, 0:128], sb["B_fm"][:, c * 128:c * 128 + 128],
                          sb["C_fm"][:, c * 128:c * 128 + 128], start=True, stop=True), "T")

        @block.vector
        def _(v, c=c):
            v.wait_ge(sem["T"], cnt["T"])
            inc(v.tensor_tensor(out=sb["Gt"][:, c * 128:c * 128 + 128],
                                in0=psA[:, 0:128], in1=sb["tri01"][:], op=ALU.mult), "V")
    gt_V = cnt["V"]

    hv = {}
    ht = {}
    t_mm = {}

    for h in range(NH):
        par = h % 2
        ps = psA if par == 0 else psB
        pbf = sb["Pbf"][:, par * 128:par * 128 + 64]

        @block.gpsimd
        def _(g, h=h, par=par):
            if h >= 2:
                g.wait_ge(sem["V"], hv[h - 2])
            if h == 0:
                g.wait_ge(sem["V"], gt_V)
            inc(g.dma_start(out=bc[par][:, 0:T],
                            in_=scr["acum_scr"][h:h + 1, :].broadcast_to([128, T])), "dG", 16)
            g.wait_ge(sem["dG"], cnt["dG"])
        bc_ld = cnt["dG"]

        @block.vector
        def _(v, h=h, par=par):
            if h >= 2:
                v.wait_ge(sem["V"], hv[h - 2])
                v.wait_ge(sem["T"], ht[h - 2])
            inc(v.tensor_scalar_mul(out=dgD[par][:, 0:128], in0=sb["ident_b"][:],
                                    scalar1=sb["Dbc"][:, h:h + 1]), "V")

        @block.vector
        def _(v, h=h, par=par):
            if h >= 2:
                v.wait_ge(sem["T"], ht[h - 2])
            for c in range(CH):
                inc(v.tensor_scalar_mul(
                    out=sb["xdtd"][:, par * 512 + c * 64:par * 512 + c * 64 + 64],
                    in0=xs_tm[:, c * DI + h * 64:c * DI + h * 64 + 64],
                    scalar1=sb["sd_tm"][:, c * 64 + h:c * 64 + h + 1]), "V")
        xdtd_done = cnt["V"]

        pbf_prev = 0
        for c in range(CH):
            i = h * CH + c
            p2 = c % 2

            @block.vector
            def _(v, h=h, c=c, par=par, p2=p2, bc_ld=bc_ld, i=i):
                v.wait_ge(sem["dG"], bc_ld)
                if i >= 2 and (i - 2) in t_mm:
                    v.wait_ge(sem["T"], t_mm[i - 2])
                ins = v.tensor_scalar(out=Db[p2][:, 0:128],
                                      in0=bc[par][:, c * 128:c * 128 + 128],
                                      scalar1=sb["acum_tm"][:, c * 64 + h:c * 64 + h + 1],
                                      scalar2=0.0, op0=ALU.subtract, op1=ALU.min)
                inc(ins, "V")

            @block.scalar
            def _(s, h=h, c=c, p2=p2):
                s.wait_ge(sem["V"], cnt["V"])
                inc(s.activation(out=Eb[p2][:, 0:128], in_=Db[p2][:, 0:128], func=AF.Exp,
                                 bias=sb["lndt_tm"][:, c * 64 + h:c * 64 + h + 1], scale=1.0),
                    "S")

            @block.gpsimd
            def _(g, h=h, c=c, p2=p2):
                g.wait_ge(sem["S"], cnt["S"])
                inc(g.tensor_tensor(out=T1[p2][:, 0:128], in0=Eb[p2][:, 0:128],
                                    in1=sb["Gt"][:, c * 128:c * 128 + 128], op=ALU.mult), "G")
            t1_done = cnt["G"]

            @block.tensor
            def _(te, h=h, c=c, par=par, p2=p2, ps=ps, pbf=pbf, t1_done=t1_done,
                  xdtd_done=xdtd_done, i=i, pbf_prev=pbf_prev):
                te.wait_ge(sem["G"], t1_done)
                if c == 0 and h >= 2:
                    te.wait_ge(sem["V"], hv[h - 2])
                xsl = xs_tm[:, c * DI + h * 64:c * DI + h * 64 + 64]
                te.matmul(ps[:, c * 64:c * 64 + 64], T1[p2][:, 0:128], xsl,
                          start=True, stop=False)
                inc(te.matmul(ps[:, c * 64:c * 64 + 64], dgD[par][:, 0:128], xsl,
                              start=False, stop=True), "T")
                if c > 0:
                    te.wait_ge(sem["V"], pbf_prev)
                    inc(te.matmul(ps[:, 512 + c * 64:512 + c * 64 + 64],
                                  sb["C_fm"][:, c * 128:c * 128 + 128], pbf,
                                  start=True, stop=True), "T")
                te.wait_ge(sem["V"], xdtd_done)
                ins = te.matmul(ps[:, 1024 + c * 64:1024 + c * 64 + 64],
                                sb["B_tm"][:, c * 128:c * 128 + 128],
                                sb["xdtd"][:, par * 512 + c * 64:par * 512 + c * 64 + 64],
                                start=True, stop=True)
                t_mm[i] = inc(ins, "T")

            @block.vector
            def _(v, h=h, c=c, ps=ps, i=i, pbf=pbf):
                v.wait_ge(sem["T"], t_mm[i])
                vf(v, v.tensor_scalar_mul(out=sb["ptmp"][:], in0=P[:, h * 64:h * 64 + 64],
                                          scalar1=sb["ealrep"][:, c * 64 + h:c * 64 + h + 1]))
                vf(v, v.tensor_add(out=P[:, h * 64:h * 64 + 64],
                                   in0=ps[:, 1024 + c * 64:1024 + c * 64 + 64],
                                   in1=sb["ptmp"][:]))
                if c < CH - 1:
                    inc(v.tensor_copy(out=pbf, in_=P[:, h * 64:h * 64 + 64]), "V")
            pbf_prev = cnt["V"]

        ht[h] = cnt["T"]

        @block.vector
        def _(v, h=h, ps=ps):
            v.wait_ge(sem["T"], ht[h])
            yv = Y_tm.rearrange("p (c f) -> p c f", c=CH)[:, :, h * 64:h * 64 + 64]
            v.tensor_copy(out=yv, in_=ps[:, 0:512].rearrange("p (c f) -> p c f", c=CH))
            ins = None
            for c in range(1, CH):
                ins = v.scalar_tensor_tensor(out=Y_tm[:, c * DI + h * 64:c * DI + h * 64 + 64],
                                             in0=ps[:, 512 + c * 64:512 + c * 64 + 64],
                                             scalar=sb["ea_tm"][:, c * 64 + h:c * 64 + h + 1],
                                             in1=Y_tm[:, c * DI + h * 64:c * DI + h * 64 + 64],
                                             op0=ALU.mult, op1=ALU.add)
            hv[h] = inc(ins, "V")

    ssd_V = cnt["V"]

    @block.gpsimd
    def _(g):
        g.wait_ge(sem["V"], ssd_V)
        inc(g.dma_start(out=scr["cc4_in"][0, 0:128 * 4096].rearrange("(p f) -> p f", p=128),
                        in_=P[:]), "dG", 16)
        inc(g.dma_start(out=scr["cc4_in"][0, 128 * 4096:128 * 4096 + 64],
                        in_=sb["prefx"][:, 15:16]), "dG", 16)
        g.wait_ge(sem["dG"], cnt["dG"])
        inc(g.collective_compute("AllGather", ALU.bypass,
                                 replica_groups=[list(range(NCORES))],
                                 ins=[scr["cc4_in"]], outs=[scr["cc4_out"]]), "CC")
    cc4_done = cnt["CC"]

    dbg_dump("ytm_loc", [((c,), Y_tm[:, c * DI:(c + 1) * DI]) for c in range(CH)],
             (CH, 128, DI), BF16)
    dbg_dump("pstate", [(None, P[:])], (128, NH * HP))
    if upto == "ssd":
        return finish()

    barrier()
    ARC.reset(0)
    bbuf = ARC.alloc(4096, F32)
    cvb = ARC.alloc(64, F32)
    ARB.reset(0)
    coefb = ARB.alloc(4096, BF16)

    @block.gpsimd
    def _(g):
        g.wait_ge(sem["CC"], cc4_done)
        inc(g.dma_start(out=sb["svals"][:],
                        in_=scr["cc4_out"][:, 128 * 4096:128 * 4096 + 64]), "dG", 16)
        g.wait_ge(sem["dG"], cnt["dG"])

    @block.tensor
    def _(te):
        te.wait_ge(sem["dG"], cnt["dG"])
        inc(te.matmul(psA[0:8, 0:64], sb["selw"][:], sb["svals"][:], start=True, stop=True), "T")

    @block.scalar
    def _(s):
        s.wait_ge(sem["T"], cnt["T"])
        inc(s.activation(out=sb["cvals"][:], in_=psA[0:8, 0:64], func=AF.Exp), "S")

    @block.vector
    def _(v):
        v.wait_ge(sem["S"], cnt["S"])
        inc(v.tensor_scalar_mul(out=sb["svals"][:], in0=sb["cvals"][:],
                                scalar1=sb["loww"][:]), "V")

    @block.sync
    def _(sy):
        sy.wait_ge(sem["V"], cnt["V"])
        inc(sy.dma_start(out=scr["cvals_scr"], in_=sb["svals"][:]), "dW", 16)
    cv_w = cnt["dW"]

    @block.vector
    def _(v):
        v.wait_ge(sem["dW"], cv_w)
        inc(v.memset(P[:], 0.0), "V")

    for jj in range(NCORES):
        @block.sync
        def _(sy, jj=jj):
            sy.wait_ge(sem["V"], cnt["V"])
            inc(sy.dma_start(out=bbuf[:],
                             in_=scr["cc4_out"][jj, 0:128 * 4096]
                             .rearrange("(p f) -> p f", p=128)), "dW", 16)

        @block.gpsimd
        def _(g, jj=jj):
            g.wait_ge(sem["V"], cnt["V"])
            ins = g.dma_start(out=cvb[:, 0:64],
                              in_=scr["cvals_scr"][jj:jj + 1, :].broadcast_to([128, 64]))
            inc(ins, "dG", 16)
            g.wait_ge(sem["dG"], cnt["dG"])

        @block.vector
        def _(v, jj=jj):
            v.wait_ge(sem["dW"], cnt["dW"])
            v.wait_ge(sem["dG"], cnt["dG"])
            ins = None
            for h in range(NH):
                ins = v.scalar_tensor_tensor(out=P[:, h * 64:h * 64 + 64],
                                             in0=bbuf[:, h * 64:h * 64 + 64],
                                             scalar=cvb[:, h:h + 1],
                                             in1=P[:, h * 64:h * 64 + 64],
                                             op0=ALU.mult, op1=ALU.add)
            inc(ins, "V")

    @block.vector
    def _(v):
        v.wait_ge(sem["V"], cnt["V"])
        inc(v.tensor_copy(out=coefb[:], in_=P[:]), "V")
    pin_bf = cnt["V"]
    dbg_dump("pin", [(None, P[:])], (128, NH * HP))
    dbg_dump("cv", [(None, sb["svals"][:])], (8, 64))

    for c in range(CH):
        @block.tensor
        def _(te, c=c):
            te.wait_ge(sem["V"], cnt["V"])
            for nb in range(8):
                ps2 = psA if nb < 4 else psB
                inc(te.matmul(ps2[:, (nb % 4) * 512:(nb % 4) * 512 + 512],
                              sb["C_fm"][:, c * 128:c * 128 + 128],
                              coefb[:, nb * 512:nb * 512 + 512],
                              start=True, stop=True), "T")

        @block.vector
        def _(v, c=c):
            v.wait_ge(sem["T"], cnt["T"])
            ins = None
            for h in range(NH):
                ps2 = psA if h < 32 else psB
                ho = h % 32
                ins = v.scalar_tensor_tensor(out=Y_tm[:, c * DI + h * 64:c * DI + h * 64 + 64],
                                             in0=ps2[:, ho * 64:ho * 64 + 64],
                                             scalar=sb["ea2_tm"][:, c * 64 + h:c * 64 + h + 1],
                                             in1=Y_tm[:, c * DI + h * 64:c * DI + h * 64 + 64],
                                             op0=ALU.mult, op1=ALU.add)
            inc(ins, "V")

    dbg_dump("ytm", [((c,), Y_tm[:, c * DI:(c + 1) * DI]) for c in range(CH)],
             (CH, 128, DI), BF16)
    if upto == "dy":
        return finish()

    barrier()
    ARB.reset(0)
    Y_fm = ARB.alloc(KDI * T, BF16)

    for c in range(CH):
        for grp in range(2):
            ps = psAb if grp == 0 else psBb

            @block.tensor
            def _(te, c=c, grp=grp, ps=ps):
                te.wait_ge(sem["V"], cnt["V"])
                for q2 in range(16):
                    q = grp * 16 + q2
                    inc(te.transpose(ps[:, q2 * 128:q2 * 128 + 128],
                                     Y_tm[:, c * DI + q * 128:c * DI + q * 128 + 128],
                                     sb["ident_b"][:]), "T")

            @block.vector
            def _(v, c=c, grp=grp, ps=ps):
                v.wait_ge(sem["T"], cnt["T"])
                ins = None
                for q2 in range(16):
                    q = grp * 16 + q2
                    ins = v.tensor_copy(out=Y_fm[:, q * T + c * 128:q * T + c * 128 + 128],
                                        in_=ps[:, q2 * 128:q2 * 128 + 128])
                inc(ins, "V")
    ytrans_V = cnt["V"]

    barrier()
    ARA.reset(0)
    hin2 = ARA.alloc(KD * T, BF16)

    @block.sync
    def _(sy):
        sy.wait_ge(sem["V"], ytrans_V)
        for fc in range(KD):
            inc(sy.dma_start(out=hin2[:, fc * T:(fc + 1) * T],
                             in_=scr["hin_scr"][fc][:, 3:TH]), "dW", 16)
    hin_rel = cnt["dW"]

    ARC.reset(0)
    zwb = [[None, None, ARC.alloc(KD * 128, BF16), ARC.alloc(KD * 128, BF16)]]
    siluz = ARC.alloc(T, F32)
    sqb = ARC.alloc(T, F32)

    def z_evac(ncu, reg):
        ps, off = reg

        @block.scalar
        def _(s, ncu=ncu, ps=ps, off=off):
            s.wait_ge(sem["T"], cnt["T"])
            inc(s.activation(out=siluz[:, 0:T], in_=ps[:, off:off + 1024], func=AF.Silu), "S")

        @block.vector
        def _(v, ncu=ncu):
            v.wait_ge(sem["S"], cnt["S"])
            ins = v.tensor_tensor(out=Y_fm[:, ncu * T:(ncu + 1) * T],
                                  in0=siluz[:, 0:T],
                                  in1=Y_fm[:, ncu * T:(ncu + 1) * T], op=ALU.mult)
            m = inc(ins, "V")

        @block.scalar
        def _(s, ncu=ncu):
            s.wait_ge(sem["V"], cnt["V"])
            inc(s.activation(out=sqb[:, 0:T], in_=Y_fm[:, ncu * T:(ncu + 1) * T],
                             func=AF.Square), "S")

        @block.tensor
        def _(te, ncu=ncu):
            te.wait_ge(sem["S"], cnt["S"])
            st = (ncu == 0)
            inc(te.matmul(psB[0:1, 1536:2048], sb["onesc"][:, 0:1], sqb[:, 0:512],
                          start=st, stop=(ncu == KDI - 1)), "T")
            inc(te.matmul(psB[0:1, 1024:1536], sb["onesc"][:, 0:1], sqb[:, 512:1024],
                          start=st, stop=(ncu == KDI - 1)), "T")
        return cnt["V"]

    bgemm(NC=KDI, KTOT=KD, KG=KD, wsrcs=[lambda n, kg: dram["w_z"][n]
                                         .rearrange("k p n -> p k n")],
          wdt=BF16, cast=True,
          rhs_fn=lambda pi, k, cb: hin2[:, k * T + cb * 512:k * T + cb * 512 + 512],
          ncols=2, regions=[(psA, 0), (psA, 1024)], evac_fn=z_evac, bufs=[zwb[0]],
          pre_wait_T=lambda te: te.wait_ge(sem["dW"], hin_rel))
    zstat_T = cnt["T"]

    @block.vector
    def _(v):
        v.wait_ge(sem["T"], zstat_T)
        inc(v.tensor_copy(out=sb["rowbuf"][0:1, 0:512], in_=psB[0:1, 1536:2048]), "V")
        inc(v.tensor_copy(out=sb["rowbuf"][0:1, 512:1024], in_=psB[0:1, 1024:1536]), "V")

    @block.scalar
    def _(s):
        s.wait_ge(sem["V"], cnt["V"])
        inc(s.activation(out=sb["rowbuf"][0:1, 0:T], in_=sb["rowbuf"][0:1, 0:T],
                         func=AF.Sqrt, bias=sb["kc"][0:1, 1:2], scale=sb["kc"][0:1, 4:5]), "S")

    @block.vector
    def _(v):
        v.wait_ge(sem["S"], cnt["S"])
        inc(v.reciprocal(out=sb["rowbuf"][0:1, 0:T], in_=sb["rowbuf"][0:1, 0:T]), "V")

    @block.sync
    def _(sy):
        sy.wait_ge(sem["V"], cnt["V"])
        inc(sy.dma_start(out=scr["row_scr"][0], in_=sb["rowbuf"][0:1, 0:T]), "dW", 16)
    r3_w = cnt["dW"]

    ARC.reset(0)
    rho3bc = ARC.alloc(T, F32)
    mixcol = ARC.alloc(32, F32)

    @block.sync
    def _(sy):
        sy.wait_ge(sem["dW"], r3_w)
        inc(sy.dma_start(out=rho3bc[:], in_=scr["row_scr"][0:1, :].broadcast_to([128, T])),
            "dW", 16)
        inc(sy.dma_start(out=mixcol[:, 0:32],
                         in_=dram["mixw"].rearrange("a (c p) -> (a p) c", p=128),
                         allow_slow_non_contiguous=True), "dW", 16)
        sy.wait_ge(sem["dW"], cnt["dW"])
    r3_b = cnt["dW"]

    @block.vector
    def _(v):
        v.wait_ge(sem["dW"], r3_b)
        ins = None
        for zc in range(KDI):
            ins = v.scalar_tensor_tensor(out=Y_fm[:, zc * T:(zc + 1) * T],
                                         in0=rho3bc[:], scalar=mixcol[:, zc:zc + 1],
                                         in1=Y_fm[:, zc * T:(zc + 1) * T],
                                         op0=ALU.mult, op1=ALU.mult)
        inc(ins, "V")
    ymix_V = cnt["V"]

    dbg_dump("ymix", [((q,), Y_fm[:, q * T:(q + 1) * T]) for q in range(KDI)],
             (KDI, 128, T), BF16)
    if upto == "ymix":
        return finish()

    barrier()
    ARC.reset(0)
    owb = [[None, None, ARC.alloc(KD * 128, BF16), ARC.alloc(KD * 128, BF16)]]
    xnb2 = ARC.alloc(T, BF16)
    h1b = ARC.alloc(T, BF16)
    sqb2 = ARC.alloc(T, F32)

    def op_evac(ncu, reg):
        ps, off = reg

        @block.sync
        def _(sy, ncu=ncu):
            sy.wait_ge(sem["V"], cnt["V"])
            inc(sy.dma_start(out=xnb2[:], in_=scr["xn_scr"][ncu]), "dW", 16)

        @block.vector
        def _(v, ncu=ncu, ps=ps, off=off):
            v.wait_ge(sem["T"], cnt["T"])
            v.wait_ge(sem["dW"], cnt["dW"])
            inc(v.tensor_tensor(out=h1b[:, 0:T], in0=ps[:, off:off + 1024], in1=xnb2[:],
                                op=ALU.add), "V")
        vm = cnt["V"]

        @block.sync
        def _(sy, ncu=ncu, vm=vm):
            sy.wait_ge(sem["V"], vm)
            inc(sy.dma_start(out=scr["h1_scr"][ncu], in_=h1b[:, 0:T]), "dW", 16)

        @block.scalar
        def _(s, ncu=ncu, vm=vm):
            s.wait_ge(sem["V"], vm)
            inc(s.activation(out=sqb2[:, 0:T], in_=h1b[:, 0:T], func=AF.Square), "S")

        @block.tensor
        def _(te, ncu=ncu):
            te.wait_ge(sem["S"], cnt["S"])
            st = (ncu == 0)
            inc(te.matmul(psB[0:1, 1536:2048], sb["onesc"][:, 0:1], sqb2[:, 0:512],
                          start=st, stop=(ncu == KD - 1)), "T")
            inc(te.matmul(psB[0:1, 1024:1536], sb["onesc"][:, 0:1], sqb2[:, 512:1024],
                          start=st, stop=(ncu == KD - 1)), "T")
        return vm

    bgemm(NC=KD, KTOT=KDI, KG=KD, wsrcs=[lambda n, kg: dram["w_out"][n]
                                         .rearrange("k p n -> p k n")
                                         [:, kg * 16:kg * 16 + 16, :]],
          wdt=BF16, cast=True,
          rhs_fn=lambda pi, k, cb: Y_fm[:, k * T + cb * 512:k * T + cb * 512 + 512],
          ncols=2, regions=[(psA, 0), (psA, 1024)], evac_fn=op_evac, bufs=[owb[0]],
          pre_wait_T=lambda te: te.wait_ge(sem["V"], ymix_V))
    h1stat_T = cnt["T"]

    @block.vector
    def _(v):
        v.wait_ge(sem["T"], h1stat_T)
        inc(v.tensor_copy(out=sb["rowbuf"][0:1, 0:512], in_=psB[0:1, 1536:2048]), "V")
        inc(v.tensor_copy(out=sb["rowbuf"][0:1, 512:1024], in_=psB[0:1, 1024:1536]), "V")

    @block.scalar
    def _(s):
        s.wait_ge(sem["V"], cnt["V"])
        inc(s.activation(out=sb["rowbuf"][0:1, 0:T], in_=sb["rowbuf"][0:1, 0:T],
                         func=AF.Sqrt, bias=sb["kc"][0:1, 1:2], scale=sb["kc"][0:1, 5:6]), "S")

    @block.vector
    def _(v):
        v.wait_ge(sem["S"], cnt["S"])
        inc(v.reciprocal(out=sb["rowbuf"][0:1, 0:T], in_=sb["rowbuf"][0:1, 0:T]), "V")

    @block.sync
    def _(sy):
        sy.wait_ge(sem["V"], cnt["V"])
        inc(sy.dma_start(out=scr["row_scr"][1], in_=sb["rowbuf"][0:1, 0:T]), "dW", 16)
    r4_w = cnt["dW"]

    barrier()
    ARA.reset(0)
    h2_fm = ARA.alloc(KD * T, BF16)
    rho4bc = ARC.alloc(T, F32)
    n2col = ARC.alloc(16, F32)

    @block.sync
    def _(sy):
        sy.wait_ge(sem["dW"], r4_w)
        inc(sy.dma_start(out=rho4bc[:], in_=scr["row_scr"][1:2, :].broadcast_to([128, T])),
            "dW", 16)
        inc(sy.dma_start(out=n2col[:, 0:16],
                         in_=dram["norm2"].rearrange("a (c p) -> (a p) c", p=128),
                         allow_slow_non_contiguous=True), "dW", 16)
        sy.wait_ge(sem["dW"], cnt["dW"])
    r4_b = cnt["dW"]

    for fc in range(KD):
        @block.sync
        def _(sy, fc=fc):
            sy.wait_ge(sem["V"], cnt["V"])
            inc(sy.dma_start(out=h1b[:, 0:T], in_=scr["h1_scr"][fc]), "dW", 16)

        @block.vector
        def _(v, fc=fc):
            v.wait_ge(sem["dW"], max(cnt["dW"], r4_b))
            inc(v.scalar_tensor_tensor(out=h2_fm[:, fc * T:(fc + 1) * T],
                                       in0=rho4bc[:], scalar=n2col[:, fc:fc + 1],
                                       in1=h1b[:, 0:T], op0=ALU.mult, op1=ALU.mult), "V")
    h2_V = cnt["V"]

    dbg_dump("h2", [((fc,), h2_fm[:, fc * T:(fc + 1) * T]) for fc in range(KD)],
             (KD, 128, T), BF16)
    if upto == "h2":
        return finish()

    barrier()
    ARC.reset(0)
    gwb = [[None, None, ARC.alloc(8 * 128, BF16), ARC.alloc(8 * 128, BF16)],
           [None, None, ARC.alloc(8 * 128, BF16), ARC.alloc(8 * 128, BF16)]]
    silug = ARC.alloc(512, F32)
    ARB.reset(0)
    mid0 = ARB.alloc(KI * 512, BF16)
    ARD.reset(0)
    M1A = 32
    mid1a = ARA.alloc(M1A * 512, BF16)
    mid1b = ARD.alloc((KI - M1A) * 512, BF16)

    def mid_blk(half, k):
        if half == 0:
            return mid0[:, k * 512:k * 512 + 512]
        if k < M1A:
            return mid1a[:, k * 512:k * 512 + 512]
        return mid1b[:, (k - M1A) * 512:(k - M1A) * 512 + 512]

    def gu_evac(ncu, reg):
        ps, off = reg
        for half in range(2):
            @block.scalar
            def _(s, ncu=ncu, ps=ps, off=off, half=half):
                s.wait_ge(sem["T"], cnt["T"])
                s.wait_ge(sem["V"], cnt["V"])
                inc(s.activation(out=silug[:, 0:512],
                                 in_=ps[:, off + half * 512:off + half * 512 + 512],
                                 func=AF.Silu), "S")

            @block.vector
            def _(v, ncu=ncu, ps=ps, off=off, half=half):
                v.wait_ge(sem["S"], cnt["S"])
                inc(v.tensor_tensor(out=mid_blk(half, ncu), in0=silug[:, 0:512],
                                    in1=ps[:, off + 1024 + half * 512:
                                           off + 1024 + half * 512 + 512],
                                    op=ALU.mult), "V")
        return cnt["V"]

    bgemm(NC=KI, KTOT=KD, KG=8,
          wsrcs=[lambda n, kg: dram["w_gate"][n].rearrange("k p n -> p k n")
                 [:, kg * 8:kg * 8 + 8, :],
                 lambda n, kg: dram["w_up"][n].rearrange("k p n -> p k n")
                 [:, kg * 8:kg * 8 + 8, :]],
          wdt=BF16, cast=True,
          rhs_fn=lambda pi, k, cb: h2_fm[:, k * T + cb * 512:k * T + cb * 512 + 512],
          ncols=2, regions=[(psA, 0), (psB, 0)],
          evac_fn=gu_evac, bufs=gwb,
          pre_wait_T=lambda te: te.wait_ge(sem["V"], h2_V))
    gu_V = cnt["V"]

    dwb = [[None, None, ARC.alloc(11 * 128, BF16), ARC.alloc(11 * 128, BF16)]]
    h1b2 = ARC.alloc(T, BF16)
    sout = ARC.alloc(T, BF16)

    def dn_evac(ncu, reg):
        ps, off = reg

        @block.sync
        def _(sy, ncu=ncu):
            sy.wait_ge(sem["V"], cnt["V"])
            inc(sy.dma_start(out=h1b2[:, 0:T], in_=scr["h1_scr"][ncu]), "dW", 16)

        @block.vector
        def _(v, ncu=ncu, ps=ps, off=off):
            v.wait_ge(sem["T"], cnt["T"])
            v.wait_ge(sem["dW"], cnt["dW"])
            inc(v.tensor_tensor(out=sout[:, 0:T], in0=ps[:, off:off + T],
                                in1=h1b2[:, 0:T], op=ALU.add), "V")
        vm = cnt["V"]

        @block.sync
        def _(sy, ncu=ncu, vm=vm):
            sy.wait_ge(sem["V"], vm)
            inc(sy.dma_start(out=scr["s_scr"][ncu], in_=sout[:, 0:T]), "dW", 16)
        return vm

    bgemm(NC=KD, KTOT=KI, KG=11,
          wsrcs=[lambda n, kg: dram["w_down"][n].rearrange("k p n -> p k n")
                 [:, kg * 11:kg * 11 + 11, :]],
          wdt=BF16, cast=True,
          rhs_fn=lambda pi, k, cb: mid_blk(cb, k),
          ncols=2, regions=[(psA, 0), (psA, 1024), (psB, 0), (psB, 1024)],
          evac_fn=dn_evac, bufs=dwb,
          pre_wait_T=lambda te: te.wait_ge(sem["V"], gu_V))

    dbg_dump("s_scr", [((fc,), scr["s_scr"][fc]) for fc in range(KD)],
             (KD, 128, T), BF16)
    if upto == "s":
        return finish()

    @block.sync
    def _(sy):
        sy.wait_ge(sem["V"], cnt["V"])
        inc(sy.dma_start(out=scr["row_scr"][2].rearrange("(j p) -> p j", p=128),
                         in_=sb["iqs_cols"][:],
                         allow_slow_non_contiguous=True), "dW", 16)
    iqs_w = cnt["dW"]

    dbgcap = None
    if "m2cap" in dbgset:
        dbgcap = nc.dram_tensor("dbg_m2cap", [KI, 128, 512], BF16,
                                kind="ExternalOutput").ap()

    for half in range(2):
        barrier()
        ARC.reset(0)
        iqsbc = ARC.alloc(512, F32)
        q2bc = ARC.alloc(512, F32)
        w3cbc = ARC.alloc(512, F32)
        combo1 = ARC.alloc(512, F32)
        combo2 = ARC.alloc(512, F32)
        sldb = ARC.alloc(512, BF16)
        ARC_M = ARC.cur
        fwb = [[ARC.alloc(8 * 128, BF16), ARC.alloc(8 * 128, BF16)],
               [ARC.alloc(8 * 128, BF16), ARC.alloc(8 * 128, BF16)]]

        ARA.reset(0)
        mid2 = ARA.alloc(KI * 512, BF16)
        q2c = mid2
        macc = ARA.alloc(512, F32)
        mtmp = ARA.alloc(512, F32)
        gsc = ARA.alloc(512, F32)
        usc = ARA.alloc(512, F32)
        sqb3 = ARA.alloc(512, F32)

        ARD.reset(0)
        xqh = ARD.alloc(KD * 512, BF16)

        @block.sync
        def _(sy, half=half):
            sy.wait_ge(sem["dW"], iqs_w)
            sy.wait_ge(sem["V"], cnt["V"])
            inc(sy.dma_start(out=iqsbc[:],
                             in_=scr["row_scr"][2:3, half * 512:half * 512 + 512]
                             .broadcast_to([128, 512])), "dW", 16)
            for fc in range(KD):
                inc(sy.dma_start(out=xqh[:, fc * 512:fc * 512 + 512],
                                 in_=scr["xq_scr"][fc][:, half * 512:half * 512 + 512]),
                    "dW", 16)
            sy.wait_ge(sem["dW"], cnt["dW"])
        xq_ld = cnt["dW"]

        @block.vector
        def _(v):
            v.wait_ge(sem["dW"], xq_ld)
            v.memset(macc[:, 0:512], 0.0)
            v.tensor_scalar_mul(out=combo1[:, 0:512], in0=iqsbc[:, 0:512],
                                scalar1=sb["wsci_col"][:, 0:1])
            inc(v.tensor_scalar_mul(out=combo2[:, 0:512], in0=iqsbc[:, 0:512],
                                    scalar1=sb["wsci_col"][:, 1:2]), "V")
        macc_V = cnt["V"]

        def f12_evac(ncu, reg, half=half, mid2=mid2, macc=macc, iqsbc=iqsbc,
                     gsc=gsc, usc=usc, sqb3=sqb3):
            ps, off = reg

            @block.vector
            def _(v, ncu=ncu, ps=ps, off=off):
                v.wait_ge(sem["T"], cnt["T"])
                inc(v.tensor_tensor(out=gsc[:, 0:512], in0=ps[:, off:off + 512],
                                    in1=combo1[:], op=ALU.mult), "V")
                inc(v.tensor_tensor(out=usc[:, 0:512], in0=ps[:, off + 512:off + 1024],
                                    in1=combo2[:], op=ALU.mult), "V")

            @block.scalar
            def _(s, ncu=ncu):
                s.wait_ge(sem["V"], cnt["V"])
                inc(s.activation(out=sqb3[:, 0:512], in_=gsc[:, 0:512], func=AF.Silu), "S")

            @block.vector
            def _(v, ncu=ncu):
                v.wait_ge(sem["S"], cnt["S"])
                vf(v, v.tensor_tensor(out=mtmp[:, 0:512], in0=sqb3[:, 0:512],
                                      in1=usc[:, 0:512], op=ALU.mult))
                inc(v.tensor_copy(out=mid2[:, ncu * 512:ncu * 512 + 512],
                                  in_=mtmp[:, 0:512]), "V")

            @block.scalar
            def _(s, ncu=ncu):
                s.wait_ge(sem["V"], cnt["V"])
                inc(s.activation(out=sqb3[:, 0:512], in_=mtmp[:, 0:512],
                                 func=AF.Square), "S")

            @block.vector
            def _(v, ncu=ncu):
                v.wait_ge(sem["S"], cnt["S"])
                inc(v.tensor_tensor(out=macc[:, 0:512], in0=sqb3[:, 0:512],
                                    in1=macc[:, 0:512], op=ALU.max), "V")

            @block.tensor
            def _(te, ncu=ncu):
                te.wait_ge(sem["S"], cnt["S"])
                inc(te.matmul(psB[0:1, 1536:2048], sb["onesc"][:, 0:1], sqb3[:, 0:512],
                              start=(ncu == 0), stop=(ncu == KI - 1)), "T")

            if half == 0 and "m2cap" in dbgset:
                @block.gpsimd
                def _(g, ncu=ncu):
                    g.wait_ge(sem["V"], cnt["V"])
                    inc(g.dma_start(out=dbgcap[ncu],
                                    in_=mid2[:, ncu * 512:ncu * 512 + 512]), "dG", 16)
            return cnt["V"]

        bgemm(NC=KI, KTOT=KD, KG=8,
              wsrcs=[lambda n, kg: wq_group(1, n, kg, 8),
                     lambda n, kg: wq_group(2, n, kg, 8)],
              wdt=BF16, cast=False,
              rhs_fn=lambda pi, k, cb, xqh=xqh: xqh[:, k * 512:k * 512 + 512],
              ncols=1, regions=[(psA, 0), (psA, 1024)],
              evac_fn=f12_evac, bufs=fwb,
              pre_wait_T=(lambda te: (te.wait_ge(sem["CC"], cc3_done),
                                      te.wait_ge(sem["V"], macc_V))[0]))
        f12_T = cnt["T"]
        if upto == "f12":
            dbg_dump("mid2", [((q,), mid2[:, q * 512:(q + 1) * 512]) for q in range(KI)],
                     (KI, 128, 512), BF16)
            return finish()

        @block.vector
        def _(v):
            v.wait_ge(sem["T"], f12_T)
            v.wait_ge(sem["V"], cnt["V"])
            inc(v.tensor_copy(out=sb["rowbuf"][0:1, 0:512], in_=psB[0:1, 1536:2048]), "V")

        @block.sync
        def _(sy):
            sy.wait_ge(sem["V"], cnt["V"])
            inc(sy.dma_start(out=scr["row_scr"][5, 0:512], in_=sb["rowbuf"][0:1, 0:512]),
                "dW", 16)
        ss_w = cnt["dW"]

        @block.sync
        def _(sy):
            sy.wait_ge(sem["dW"], ss_w)
            inc(sy.dma_start(out=sb["dump"][:, 8:12],
                             in_=scr["row_scr"][5, 0:512].rearrange("(q p) -> p q", p=128),
                             allow_slow_non_contiguous=True), "dW", 16)
        ss_r = cnt["dW"]

        @block.tensor
        def _(te):
            te.wait_ge(sem["V"], cnt["V"])
            for q in range(4):
                inc(te.transpose(psA[:, q * 128:q * 128 + 128],
                                 macc[:, q * 128:q * 128 + 128], sb["ident_f"][:]), "T")

        @block.vector
        def _(v):
            v.wait_ge(sem["T"], cnt["T"])
            v.wait_ge(sem["dW"], ss_r)
            D0 = sb["dump"]
            for q in range(4):
                vf(v, v.reduce_max(out=D0[:, 4 + q:5 + q], in_=psA[:, q * 128:q * 128 + 128],
                                   axis=AX))
            inc(v.tensor_scalar(out=D0[:, 0:4], in0=D0[:, 8:12], scalar1=1.0 / INTER,
                                scalar2=None, op0=ALU.mult), "V")

        @block.scalar
        def _(s):
            s.wait_ge(sem["V"], cnt["V"])
            sf(s, s.activation(out=sb["dump"][:, 12:16], in_=sb["dump"][:, 4:8],
                               func=AF.Sqrt))
            inc(s.activation(out=sb["dump"][:, 4:8], in_=sb["dump"][:, 0:4], func=AF.Sqrt,
                             bias=sb["kc"][:, 1:2]), "S")

        @block.vector
        def _(v):
            v.wait_ge(sem["S"], cnt["S"])
            D0 = sb["dump"]
            for q in range(4):
                vf(v, v.reciprocal(out=D0[:, 0 + q:1 + q], in_=D0[:, 4 + q:5 + q]))
            vf(v, v.tensor_mul(out=D0[:, 4:8], in0=D0[:, 0:4], in1=D0[:, 12:16]))
            vf(v, v.tensor_scalar_max(out=D0[:, 8:12], in0=D0[:, 4:8], scalar1=EPS_FFN))
            for q in range(4):
                vf(v, v.reciprocal(out=D0[:, 4 + q:5 + q], in_=D0[:, 8 + q:9 + q]))
            vf(v, v.tensor_scalar_mul(out=D0[:, 12:16], in0=D0[:, 4:8], scalar1=127.0))
            vf(v, v.tensor_mul(out=D0[:, 4:8], in0=D0[:, 0:4], in1=D0[:, 12:16]))
            vf(v, v.tensor_scalar_mul(out=D0[:, 0:4], in0=D0[:, 8:12], scalar1=1.0 / 127.0))
            inc(v.tensor_scalar_mul(out=D0[:, 8:12], in0=D0[:, 0:4],
                                    scalar1=sb["wsci_col"][:, 2:3]), "V")

        @block.sync
        def _(sy):
            sy.wait_ge(sem["V"], cnt["V"])
            inc(sy.dma_start(out=scr["row_scr"][3, 0:512].rearrange("(q p) -> p q", p=128),
                             in_=sb["dump"][:, 4:8], allow_slow_non_contiguous=True),
                "dW", 16)
            inc(sy.dma_start(out=scr["row_scr"][4, 0:512].rearrange("(q p) -> p q", p=128),
                             in_=sb["dump"][:, 8:12], allow_slow_non_contiguous=True),
                "dW", 16)
        q2r_w = cnt["dW"]

        @block.sync
        def _(sy):
            sy.wait_ge(sem["dW"], q2r_w)
            inc(sy.dma_start(out=q2bc[:], in_=scr["row_scr"][3:4, 0:512]
                             .broadcast_to([128, 512])), "dW", 16)
            inc(sy.dma_start(out=w3cbc[:], in_=scr["row_scr"][4:5, 0:512]
                             .broadcast_to([128, 512])), "dW", 16)
        q2b_b = cnt["dW"]

        for ncu in range(KI):
            @block.vector
            def _(v, ncu=ncu):
                v.wait_ge(sem["dW"], q2b_b)
                sl = mid2[:, ncu * 512:ncu * 512 + 512]
                vf(v, v.tensor_tensor(out=gsc[:, 0:512], in0=sl, in1=q2bc[:], op=ALU.mult))
                vf(v, v.tensor_copy(out=usc[:, 0:512].bitcast(I32), in_=gsc[:, 0:512]))
                vf(v, v.tensor_copy(out=gsc[:, 0:512], in_=usc[:, 0:512].bitcast(I32)))
                inc(v.tensor_scalar(out=q2c[:, ncu * 512:ncu * 512 + 512], in0=gsc[:, 0:512],
                                    scalar1=-128.0, scalar2=127.0, op0=ALU.max, op1=ALU.min),
                    "V")
        q2_V = cnt["V"]
        if upto == "q2":
            dbg_dump("d0", [(None, sb["dump"][:, 0:16])], (128, 16))
            dbg_dump("rs5", [(None, scr["row_scr"][5:6, 0:512])], (1, 512))
            dbg_dump("rs3", [(None, scr["row_scr"][3:4, 0:512])], (1, 512))
            dbg_dump("q2bcd", [(None, q2bc[:, 0:512])], (128, 512))
            dbg_dump("gscd", [(None, gsc[:, 0:512])], (128, 512))
            dbg_dump("uscd", [(None, usc[:, 0:512])], (128, 512))
            dbg_dump("mid2last", [(None, mid2[:, 43 * 512:44 * 512])], (128, 512), BF16)
            dbg_dump("q2c", [((q,), q2c[:, q * 512:(q + 1) * 512]) for q in range(KI)],
                     (KI, 128, 512), BF16)
            return finish()

        barrier()
        ARB.reset(0)
        gpark = ARB.alloc(KD * 512, F32)
        sgb = ARB.alloc(KD * 512, F32)
        ARC.reset(ARC_M)
        w3b = [[ARC.alloc(11 * 128, BF16), ARC.alloc(11 * 128, BF16)]]
        w3tmp = ARC.alloc(512, F32)

        def w3_evac(ncu, reg, half=half, gpark=gpark):
            ps, off = reg

            @block.vector
            def _(v, ncu=ncu, ps=ps, off=off):
                v.wait_ge(sem["T"], cnt["T"])
                v.wait_ge(sem["S"], cnt["S"])
                inc(v.tensor_tensor(out=w3tmp[:, 0:512],
                                    in0=ps[:, off:off + 512], in1=w3cbc[:], op=ALU.mult), "V")

            @block.scalar
            def _(s, ncu=ncu):
                s.wait_ge(sem["V"], cnt["V"])
                inc(s.activation(out=gpark[:, ncu * 512:ncu * 512 + 512],
                                 in_=w3tmp[:, 0:512], func=AF.Silu), "S")
            return cnt["V"]

        bgemm(NC=KD, KTOT=KI, KG=11,
              wsrcs=[lambda n, kg: wq_group(3, n, kg, 11)],
              wdt=BF16, cast=False,
              rhs_fn=lambda pi, k, cb, q2c=q2c: q2c[:, k * 512:k * 512 + 512],
              ncols=1, regions=[(psA, 0), (psA, 512), (psA, 1024), (psA, 1536)],
              evac_fn=w3_evac, bufs=w3b,
              pre_wait_T=lambda te, q2_V=q2_V: te.wait_ge(sem["V"], q2_V))
        w3_S = cnt["S"]

        if upto == "w3":
            dbg_dump("gpark", [((q,), gpark[:, q * 512:(q + 1) * 512]) for q in range(KD)],
                     (KD, 128, 512))
            return finish()

        for fc in range(KD):
            @block.sync
            def _(sy, fc=fc, half=half):
                sy.wait_ge(sem["V"], cnt["V"])
                inc(sy.dma_start(out=sldb[:, 0:512],
                                 in_=scr["s_scr"][fc][:, half * 512:half * 512 + 512]),
                    "dW", 16)

            @block.vector
            def _(v, fc=fc, w3_S=w3_S):
                v.wait_ge(sem["dW"], cnt["dW"])
                v.wait_ge(sem["S"], w3_S)
                inc(v.tensor_tensor(out=sgb[:, fc * 512:fc * 512 + 512],
                                    in0=gpark[:, fc * 512:fc * 512 + 512],
                                    in1=sldb[:, 0:512], op=ALU.mult), "V")
        sg_V = cnt["V"]

        ARC.reset(ARC_M)
        xld = ARC.alloc(2048, F32)
        for tb in range(4):
            trow = half * 512 + tb * 128

            @block.tensor
            def _(te, tb=tb, sg_V=sg_V):
                te.wait_ge(sem["V"], cnt["V"])
                te.wait_ge(sem["V"], sg_V)
                for fc in range(KD):
                    inc(te.transpose(psA[:, fc * 128:fc * 128 + 128],
                                     sgb[:, fc * 512 + tb * 128:fc * 512 + tb * 128 + 128],
                                     sb["ident_f"][:]), "T")
            t_tp = cnt["T"]

            @block.sync
            def _(sy, trow=trow):
                sy.wait_ge(sem["V"], cnt["V"])
                sy.wait_ge(sem["dW"], cnt["dW"])
                inc(sy.dma_start(out=xld[:, 0:2048],
                                 in_=dram["x_slice"][3 + trow:3 + trow + 128, :]), "dW", 16)

            @block.vector
            def _(v, t_tp=t_tp):
                v.wait_ge(sem["T"], t_tp)
                v.wait_ge(sem["dW"], cnt["dW"])
                inc(v.tensor_tensor(out=xld[:, 0:2048], in0=psA[:, 0:2048],
                                    in1=xld[:, 0:2048], op=ALU.add), "V")
            v_y = cnt["V"]

            @block.sync
            def _(sy, trow=trow, v_y=v_y):
                sy.wait_ge(sem["V"], v_y)
                inc(sy.dma_start(out=y_out[trow:trow + 128, :], in_=xld[:, 0:2048]), "dW", 16)

        if half == 0 and "cap" in dbgset:
            dbg_dump("cap_d0", [(None, sb["dump"][:, 0:16])], (128, 16))
            dbg_dump("cap_q2bc", [(None, q2bc[:, 0:512])], (128, 512))
            dbg_dump("cap_w3cbc", [(None, w3cbc[:, 0:512])], (128, 512))
            dbg_dump("cap_iqsbc", [(None, iqsbc[:, 0:512])], (128, 512))
            dbg_dump("cap_gpark", [(None, gpark[:, 0:512])], (128, 512))
            dbg_dump("cap_sgb", [(None, sgb[:, 0:512])], (128, 512))
            dbg_dump("cap_macc", [(None, macc[:, 0:512])], (128, 512))
            dbg_dump("cap_rowbuf", [(None, sb["rowbuf"][0:1, 0:512])], (1, 512))
            dbg_dump("cap_q2", [(None, mid2[:, 0:512])], (128, 512), BF16)
            dbg_dump("cap_xqh", [(None, xqh[:, 0:512])], (128, 512), BF16)

    return finish()


def kernel(**inputs):
    in_maps = _prep_inputs(inputs)
    nc = build_program()
    res = run_bass_kernel_spmd(nc, in_maps, core_ids=list(range(NCORES)))
    out = np.concatenate([np.asarray(r["y_out"]) for r in res.results], axis=0)
    return out.reshape(1, SEQ, D).astype(np.float32)
